# version 13
# baseline (speedup 1.0000x reference)
import numpy as np
import ml_dtypes
import concourse.bass as bass
import concourse.mybir as mybir
from concourse.bass_utils import run_bass_kernel_spmd
from contextlib import ExitStack

F32 = mybir.dt.float32
BF16 = mybir.dt.bfloat16
F16 = mybir.dt.float16
I32 = mybir.dt.int32
AF = mybir.ActivationFunctionType
ALU = mybir.AluOpType
AX = mybir.AxisListType

NCORES = 8
TOK = 1024
D = 4096
NT = TOK // 128
ND = D // 128
SEQ = 2048
NK = SEQ // 128

L0_NH = 16
L0_HD = 256
L0_SCALE = 1.0 / 16.0
L1_H = 32
L1_KV = 8
L1_HD = 128

NF4 = [-1.0, -0.6961928009986877, -0.5250730514526367, -0.39491748809814453,
       -0.28444138169288635, -0.18477343022823334, -0.09105003625154495, 0.0,
       0.07958029955625534, 0.16093020141124725, 0.24611230194568634,
       0.33791524171829224, 0.44070982933044434, 0.5626170039176941,
       0.7229568362236023, 1.0]
FFN_SCALE = 2.0
LN_EPS = 1e-5
BF = ml_dtypes.bfloat16


def _geo_consts():
    g = [1.0, 1.0, 1.0]
    dim, n = 3, 8
    masks = sorted(range(n), key=lambda m: (bin(m).count("1"), m))
    inv = {m: i for i, m in enumerate(masks)}
    C = np.zeros((n, n, n), dtype=np.float64)
    for i, a in enumerate(masks):
        for j, b in enumerate(masks):
            sw, at = 0, a >> 1
            while at:
                sw += bin(at & b).count("1")
                at >>= 1
            s = -1.0 if sw & 1 else 1.0
            for t in range(dim):
                if a >> t & 1 and b >> t & 1:
                    s *= g[t]
            C[i, j, inv[a ^ b]] = s
    grades = np.array([bin(m).count("1") for m in masks])
    rev = np.where(grades * (grades - 1) // 2 % 2 == 1, -1.0, 1.0)
    met = np.array([np.prod([g[t] for t in range(dim) if m >> t & 1] + [1.0])
                    for m in masks])
    bivk = C[:, :, [4, 5, 6]]
    KD = np.kron(np.eye(16), L0_SCALE * np.diag(rev * met)).astype(np.float32)
    KG = [np.kron(np.eye(16), np.diag(rev) @ bivk[:, :, c]).astype(np.float32)
          for c in range(3)]
    return np.concatenate([KD] + KG, axis=0)


def _rope_tables():
    hd = L1_HD
    inv_freq = 1.0 / (10000.0 ** (np.arange(0, hd, 2, dtype=np.float64) / hd))
    ang = np.arange(SEQ, dtype=np.float64)[:, None] * inv_freq[None, :]
    return (np.ascontiguousarray(np.cos(ang).T).astype(np.float32),
            np.ascontiguousarray(np.sin(ang).T).astype(np.float32))




class Ch:

    def __init__(self, eng, sem):
        self.e, self.sem, self.n, self.last = eng, sem, 0, None

    def wait(self, sem, val):
        self.e.wait_ge(sem, val)

    def op(self, f, *a, **kw):
        if self.last is not None:
            self.e.wait_ge(*self.last)
        ins = f(*a, **kw)
        self.n += 1
        ins.then_inc(self.sem, 1)
        self.last = (self.sem, self.n)
        return ins

    def op_m(self, mile_sem, mile_val, f, *a, **kw):
        if self.last is not None:
            self.e.wait_ge(*self.last)
        ins = f(*a, **kw)
        ins.then_inc(mile_sem, 1)
        self.last = (mile_sem, mile_val)
        return ins


def _semf(nc, sst, prefix):
    def sf(name):
        return sst.enter_context(nc.semaphore(prefix + name))
    return sf


def build_phaseA():
    nc = bass.Bass("TRN2", target_bir_lowering=False)
    T = {}
    T["x"] = nc.dram_tensor("x", [TOK, D], F32, kind="ExternalInput")
    T["wqT"] = nc.dram_tensor("wqT", [D, D], F32, kind="ExternalInput")
    T["wkT"] = nc.dram_tensor("wkT", [D, D], F32, kind="ExternalInput")
    T["wvT"] = nc.dram_tensor("wvT", [D, D], F32, kind="ExternalInput")
    T["bq"] = nc.dram_tensor("bq", [128, ND], F32, kind="ExternalInput")
    T["bk"] = nc.dram_tensor("bk", [128, ND], F32, kind="ExternalInput")
    T["bv"] = nc.dram_tensor("bv", [1, D], F32, kind="ExternalInput")
    T["bwf"] = nc.dram_tensor("bwf", [1, 48], F32, kind="ExternalInput")
    T["kdkg"] = nc.dram_tensor("kdkg", [512, 128], F32, kind="ExternalInput")
    T["idn"] = nc.dram_tensor("idn", [128, 128], BF16, kind="ExternalInput")
    T["qT"] = nc.dram_tensor("qT", [D, TOK], BF16, kind="ExternalOutput")
    T["kT"] = nc.dram_tensor("kT", [D, TOK], BF16, kind="ExternalOutput")
    T["v"] = nc.dram_tensor("v", [TOK, D], BF16, kind="ExternalOutput")
    with ExitStack() as sst:
        emit_phaseA(nc, _semf(nc, sst, "A"), T, fused=False)
    return nc


def emit_phaseA(nc, sf, T, fused, pre="", deq=None):
    x_in, wqT, wkT, wvT = T["x"], T["wqT"], T["wkT"], T["wvT"]
    bq, bk, bv, bwf, kdkg, idn = T["bq"], T["bk"], T["bv"], T["bwf"], T["kdkg"], T["idn"]
    qT_o = T["qT"]
    if fused:
        k0, k1, v0, v1 = T["k0"], T["k1"], T["v0"], T["v1"]
        side_sb, flags_sb, fl0, fl1 = T["side_sb"], T["flags_sb"], T["flA0"], T["flA1"]
        FLC = T["flcA"]
    else:
        kT_o, v_o = T["kT"], T["v"]

    with ExitStack() as st:
        sb = lambda nm, sh, dt: st.enter_context(nc.sbuf_tensor(pre + nm, sh, dt))
        ps = lambda nm, sh, dt: st.enter_context(nc.psum_tensor(pre + nm, sh, dt))
        sem = sf

        xnT = sb("xnT", [128, ND * TOK], BF16)
        xnT_r = xnT.ap().rearrange("p (f t) -> p f t", t=TOK)
        x_t = [sb(f"x_t{i}", [128, D], F32) for i in range(2)]
        if deq is not None:
            xa0 = nc.lookup_mloc(pre + "x_t0").addr
            xa1 = nc.lookup_mloc(pre + "x_t1").addr
            dq_cbuf = [nc.alloc_sbuf_tensor_at(pre + f"dq_cbuf{i}", [128, 2048], I32,
                                               offset=xa0 + i * 8192) for i in range(2)]
            dq_mbuf = [nc.alloc_sbuf_tensor_at(pre + f"dq_mbuf{i}", [128, 2048], F32,
                                               offset=xa1 + i * 8192) for i in range(2)]
            dq_acc = [sb(f"dq_acc{i}", [128, 2048], BF16) for i in range(2)]
            dq_codes, dq_amx, dq_out = deq["codes"], deq["amx"], deq["wT_sh"]
        stat = sb("stat", [128, 8 * NT], F32)
        bnbuf = sb("bnbuf", [128, 48], F32)
        xn_tok = sb("xn_tok", [128, D], BF16)
        if deq is not None:
            xn_addr = nc.lookup_mloc(pre + "xn_tok").addr
            dq_cf = nc.alloc_sbuf_tensor_at(pre + "dq_cf", [128, 2048], BF16,
                                            offset=xn_addr)
            dq_selt = nc.alloc_sbuf_tensor_at(pre + "dq_selt", [128, 2048], BF16,
                                              offset=xn_addr + 4096)
        idn_sb = sb("idn_sb", [128, 128], BF16)
        bias_sb = sb("bias_sb", [128, 2 * ND], F32)
        bvb = [sb(f"bvb{i}", [128, 512], BF16) for i in range(2)]
        bw_sb = sb("bw_sb", [128, 48], F32)
        if deq is None:
            kdkg_sb = sb("kdkg_sb", [128, 512], F32)
            btmp = sb("btmp", [128, 256], F32)
        bmat = sb("bmat", [128, L0_NH * 128], BF16)
        wvbuf = [sb(f"wvbuf{i}", [128, ND * 512], BF16) for i in range(2)]
        wbuf = [wvbuf[i][:, 0:D] if False else wvbuf[i] for i in range(2)]
        stq = [sb(f"stq{i}", [128, TOK], BF16) for i in range(2)]
        stq2 = [sb(f"stq2{i}", [128, TOK], BF16) for i in range(2)]
        stk = [sb(f"stk{i}", [128, TOK], BF16) for i in range(2)]
        vstage = [sb(f"vstage{i}", [128, 512], BF16) for i in range(8)]
        if deq is not None:
            btmp = nc.alloc_sbuf_tensor_at(pre + "btmp", [128, 256], F32,
                                           offset=nc.lookup_mloc(pre + "vstage0").addr)
            kdkg_sb = nc.alloc_sbuf_tensor_at(pre + "kdkg_sb", [128, 512], F32,
                                              offset=nc.lookup_mloc(pre + "vstage2").addr)

        ps_ln = [ps(f"ps_ln{i}", [128, 1024], BF16) for i in range(2)]
        ps_a = ps("ps_a", [128, 512], F32)
        ps_b = ps("ps_b", [128, 512], F32)
        ps_v = [ps(f"ps_v{i}", [128, 512], F32) for i in range(4)]

        blk = st.enter_context(nc.Block())
        S = {n: sem(n) for n in
             ["cload", "vch", "ach", "xload0", "xload1", "var", "sqv", "xn",
              "tp", "tpc", "wload0", "wload1", "mm", "ev", "foldmm", "fev",
              "stq_s0", "stq_s1", "stk_s0", "stk_s1", "vwload0", "vwload1",
              "vmm", "vev", "flst"]
             + [f"vst{i}" for i in range(8)]}
        if deq is not None:
            for n in ["dwd", "qdcl0", "qdcl1", "qdst0", "qdst1", "dcl0", "dcl1"]:
                S[n] = sem(n)

        def dq_srcs(j):
            r, hf = j // 2, j % 2
            return (dq_codes[r * 128:(r + 1) * 128, hf * 2048:(hf + 1) * 2048],
                    bass.AP(dq_amx, (r * 2) * D + hf * 2048,
                            [[D, 2], [0, 64], [1, 2048]]))

        @blk.sync
        def _(s):
            s.dma_start(out=idn_sb[:, :], in_=idn[:, :]).then_inc(S["cload"], 16)
            s.dma_start(out=bw_sb[:, :], in_=bass.AP(bwf, 0, [[0, 128], [1, 48]])).then_inc(S["cload"], 16)
            s.dma_start(out=kdkg_sb[:, :], in_=bass.AP(kdkg, 0, [[128, 128], [128 * 128, 4], [1, 128]])).then_inc(S["cload"], 16)
            s.dma_start(out=bias_sb[:, 0:ND], in_=bq[:, :]).then_inc(S["cload"], 16)
            s.dma_start(out=bias_sb[:, ND:2 * ND], in_=bk[:, :]).then_inc(S["cload"], 16)
            for i in range(NT):
                if i >= 2:
                    s.wait_ge(S["xn"], i - 1)
                s.dma_start(out=x_t[i % 2][:, :], in_=x_in[i * 128:(i + 1) * 128, :]).then_inc(S[f"xload{i % 2}"], 16)
            if deq is not None:
                s.wait_ge(S["xn"], NT)
                for j in range(2):
                    csrc, msrc = dq_srcs(j)
                    s.dma_start(out=dq_cbuf[j][:, :], in_=csrc).then_inc(S[f"qdcl{j}"], 16)
                    s.dma_start(out=dq_mbuf[j][:, :], in_=msrc).then_inc(S[f"qdcl{j}"], 16)
            for ot in range(ND):
                s.wait_ge(S["fev"], 2 * ot + 2)
                s.dma_start(out=qT_o[ot * 128:(ot + 1) * 128, :], in_=stq2[ot % 2][:, :]).then_inc(S[f"stq_s{ot % 2}"], 16)
            sd_ctx = s.register(pre + "sd") if fused else None
            sd = sd_ctx.__enter__() if fused else None
            if fused:
                s.reg_load(sd, side_sb[0:1, 0:1])
            def kdst(ot):
                return slice(ot * 128, (ot + 1) * 128)
            for ot in range(ND):
                s.wait_ge(S["ev"], 2 * ND + 2 * ot + 2)
                if fused:
                    with s.If_eq(sd, 0):
                        s.dma_start(out=k0[kdst(ot), :], in_=stk[ot % 2][:, :]).then_inc(S[f"stk_s{ot % 2}"], 16)
                    with s.Else():
                        s.dma_start(out=k1[kdst(ot), :], in_=stk[ot % 2][:, :]).then_inc(S[f"stk_s{ot % 2}"], 16)
                else:
                    s.dma_start(out=kT_o[kdst(ot), :], in_=stk[ot % 2][:, :]).then_inc(S[f"stk_s{ot % 2}"], 16)
            def dq_store(j):
                s.wait_ge(S["dwd"], j + 1)
                r, hf = j // 2, j % 2
                cs = slice(hf * 2048, (hf + 1) * 2048)
                with s.If_eq(sd, 0):
                    s.dma_start(out=dq_out[r * 128:(r + 1) * 128, cs],
                                in_=dq_acc[j % 2][:, :]).then_inc(S[f"qdst{j % 2}"], 16)
                with s.Else():
                    s.dma_start(out=dq_out[2048 + r * 128:2048 + (r + 1) * 128, cs],
                                in_=dq_acc[j % 2][:, :]).then_inc(S[f"qdst{j % 2}"], 16)
            for ch in range(8):
                if deq is not None and ch >= 1:
                    dq_store(2 * (ch - 1))
                    dq_store(2 * (ch - 1) + 1)
                for tt in range(8):
                    s.wait_ge(S["vev"], ch * 8 + tt + 1)
                    if fused:
                        with s.If_eq(sd, 0):
                            s.dma_start(out=v0[tt * 128:(tt + 1) * 128, ch * 512:(ch + 1) * 512],
                                        in_=vstage[tt][:, :]).then_inc(S[f"vst{tt}"], 16)
                        with s.Else():
                            s.dma_start(out=v1[tt * 128:(tt + 1) * 128, ch * 512:(ch + 1) * 512],
                                        in_=vstage[tt][:, :]).then_inc(S[f"vst{tt}"], 16)
                    else:
                        s.dma_start(out=v_o[tt * 128:(tt + 1) * 128, ch * 512:(ch + 1) * 512],
                                    in_=vstage[tt][:, :]).then_inc(S[f"vst{tt}"], 16)
            if deq is not None:
                dq_store(14)
                dq_store(15)
                for p in range(2):
                    s.wait_ge(S[f"qdst{p}"], 16 * 8)
            for tt in range(8):
                s.wait_ge(S[f"vst{tt}"], 16 * 8)
            for p in range(2):
                s.wait_ge(S[f"stq_s{p}"], 16 * 16)
                s.wait_ge(S[f"stk_s{p}"], 16 * 16)
            if fused:
                with s.If_eq(sd, 0):
                    s.dma_start(out=fl0[:, :], in_=flags_sb[0:1, FLC * 16:(FLC + 1) * 16]).then_inc(S["flst"], 16)
                with s.Else():
                    s.dma_start(out=fl1[:, :], in_=flags_sb[0:1, FLC * 16:(FLC + 1) * 16]).then_inc(S["flst"], 16)
                s.wait_ge(S["flst"], 16)
                sd_ctx.__exit__(None, None, None)

        @blk.gpsimd
        def _(g):
            for widx in range(2 * ND):
                wT = wqT if widx < ND else wkT
                ot = widx % ND
                if widx >= 2:
                    g.wait_ge(S["ev"], 2 * widx - 2)
                for part in range(4):
                    src = bass.AP(wT, (part * 8 * 128) * D + ot * 128,
                                  [[D, 128], [128 * D, 8], [1, 128]])
                    g.dma_start(out=wbuf[widx % 2][:, part * 1024:(part + 1) * 1024],
                                in_=src).then_inc(S[f"wload{widx % 2}"], 16)
            for ch in range(8):
                if ch < 2:
                    g.wait_ge(S["mm"], 4 * ND)
                if ch >= 2:
                    g.wait_ge(S["vev"], (ch - 1) * 8)
                g.dma_start(out=bvb[ch % 2][:, :],
                            in_=bass.AP(bv, ch * 512, [[0, 128], [1, 512]])).then_inc(S[f"vwload{ch % 2}"], 16)
                for part in range(4):
                    src = bass.AP(wvT, (part * 8 * 128) * D + ch * 512,
                                  [[D, 128], [128 * D, 8], [1, 512]])
                    g.dma_start(out=wvbuf[ch % 2][:, part * 8 * 512:(part + 1) * 8 * 512],
                                in_=src).then_inc(S[f"vwload{ch % 2}"], 16)
                if deq is not None and ch >= 1:
                    for j in (2 * ch, 2 * ch + 1):
                        g.wait_ge(S["dwd"], j - 1)
                        csrc, msrc = dq_srcs(j)
                        g.dma_start(out=dq_cbuf[j % 2][:, :],
                                    in_=csrc).then_inc(S[f"dcl{j % 2}"], 16)
                        g.dma_start(out=dq_mbuf[j % 2][:, :],
                                    in_=msrc).then_inc(S[f"dcl{j % 2}"], 16)

        @blk.vector
        def _(v):
            ch = Ch(v, S["vch"])
            ch.wait(S["cload"], 16 * 5)
            for h in range(L0_NH):
                c0 = h * 3
                ch.op(v.scalar_tensor_tensor, btmp[:, 0:128], kdkg_sb[:, 128:256],
                      bw_sb[:, c0:c0 + 1], kdkg_sb[:, 0:128], ALU.mult, ALU.add)
                ch.op(v.scalar_tensor_tensor, btmp[:, 128:256], kdkg_sb[:, 256:384],
                      bw_sb[:, c0 + 1:c0 + 2], btmp[:, 0:128], ALU.mult, ALU.add)
                ch.op(v.scalar_tensor_tensor, bmat[:, h * 128:(h + 1) * 128], kdkg_sb[:, 384:512],
                      bw_sb[:, c0 + 2:c0 + 3], btmp[:, 128:256], ALU.mult, ALU.add)
            xn_n = 0
            tpc_n = 0
            for i in range(NT):
                c = 8 * i
                ch.wait(S[f"xload{i % 2}"], 16 * (i // 2 + 1))
                for ck in range(8):
                    ch.op(v.bn_stats, bnbuf[:, ck * 6:(ck + 1) * 6],
                          x_t[i % 2][:, ck * 512:(ck + 1) * 512])
                ch.op(v.bn_aggr, stat[:, c:c + 2], bnbuf[:, :])
                ch.op_m(S["var"], i + 1, v.tensor_scalar, stat[:, c + 2:c + 3],
                        stat[:, c + 1:c + 2], LN_EPS, None, ALU.add)
                ch.wait(S["sqv"], i + 1)
                ch.op(v.reciprocal, stat[:, c + 7:c + 8], stat[:, c + 6:c + 7])
                if i >= 1:
                    ch.wait(S["tp"], 4 * i)
                xn_n += 1
                ch.op_m(S["xn"], xn_n, v.tensor_scalar, xn_tok[:, :], x_t[i % 2][:, :],
                        stat[:, c:c + 1], stat[:, c + 7:c + 8], ALU.subtract, ALU.mult)
                for gidx in range(4):
                    gi = i * 4 + gidx
                    ch.wait(S["tp"], gi + 1)
                    tpc_n += 1
                    ch.op_m(S["tpc"], tpc_n, v.tensor_copy,
                            xnT_r[:, gidx * 8:(gidx + 1) * 8, i * 128:(i + 1) * 128],
                            ps_ln[gi % 2][:, :])
            def deq_tile(j):
                if j < 2:
                    ch.wait(S[f"qdcl{j}"], 32)
                else:
                    ch.wait(S[f"dcl{j % 2}"], 32 * ((j - 2) // 2 + 1))
                if j >= 2:
                    ch.wait(S[f"qdst{j % 2}"], 16 * (j // 2))
                ch.op(v.tensor_copy, dq_cf[:, :], dq_cbuf[j % 2][:, :])
                ch.op(v.tensor_scalar, dq_acc[j % 2][:, :], dq_cf[:, :], 0.0,
                      NF4[0], ALU.is_equal, ALU.mult)
                for tv in range(1, 16):
                    ch.op(v.tensor_scalar, dq_selt[:, :], dq_cf[:, :], float(tv),
                          NF4[tv], ALU.is_equal, ALU.mult)
                    ch.op(v.tensor_tensor, dq_acc[j % 2][:, :], dq_acc[j % 2][:, :],
                          dq_selt[:, :], ALU.add)
                ch.op_m(S["dwd"], j + 1, v.tensor_tensor, dq_acc[j % 2][:, :],
                        dq_acc[j % 2][:, :], dq_mbuf[j % 2][:, :], ALU.mult)

            if deq is not None:
                deq_tile(0)
                deq_tile(1)
            vev_n = 0
            for ch_i in range(8):
                for w in range(2):
                    wv_idx = ch_i * 2 + w
                    ch.wait(S["vmm"], wv_idx + 1)
                    for t4 in range(4):
                        tt = w * 4 + t4
                        if ch_i >= 1:
                            ch.wait(S[f"vst{tt}"], 16 * ch_i)
                        vev_n += 1
                        ch.op_m(S["vev"], vev_n, v.tensor_tensor, vstage[tt][:, :],
                                ps_v[t4][:, 0:512], bvb[ch_i % 2][:, :], ALU.add)
                if deq is not None and ch_i <= 6:
                    deq_tile(2 * ch_i + 2)
                    deq_tile(2 * ch_i + 3)

        @blk.scalar
        def _(a):
            ch = Ch(a, S["ach"])
            for i in range(NT):
                c = 8 * i
                ch.wait(S["var"], i + 1)
                ch.op_m(S["sqv"], i + 1, a.activation, stat[:, c + 6:c + 7],
                        stat[:, c + 2:c + 3], AF.Sqrt)
            ev_n = 0
            fev_n = 0
            for ot in range(ND):
                for tc in range(2):
                    gidx = ot * 2 + tc
                    ch.wait(S["mm"], gidx + 1)
                    if ot >= 2 and tc == 0:
                        ch.wait(S["foldmm"], ot - 1)
                    ev_n += 1
                    ch.op_m(S["ev"], ev_n, a.activation,
                            stq[ot % 2][:, tc * 512:(tc + 1) * 512],
                            (ps_a if gidx % 2 == 0 else ps_b)[:, :],
                            AF.Identity, bias=bias_sb[:, ot:ot + 1])
                for fc in range(2):
                    ch.wait(S["foldmm"], ot + 1)
                    if ot >= 2:
                        ch.wait(S[f"stq_s{ot % 2}"], 16 * (ot // 2))
                    fev_n += 1
                    ch.op_m(S["fev"], fev_n, a.activation,
                            stq2[ot % 2][:, fc * 512:(fc + 1) * 512],
                            ps_v[fc][:, :], AF.Copy)
            for ot in range(ND):
                for tc in range(2):
                    gidx = 2 * ND + ot * 2 + tc
                    ch.wait(S["mm"], gidx + 1)
                    if ot >= 2 and tc == 0:
                        ch.wait(S[f"stk_s{ot % 2}"], 16 * (ot // 2))
                    ev_n += 1
                    ch.op_m(S["ev"], ev_n, a.activation,
                            stk[ot % 2][:, tc * 512:(tc + 1) * 512],
                            (ps_a if gidx % 2 == 0 else ps_b)[:, :],
                            AF.Identity, bias=bias_sb[:, ND + ot:ND + ot + 1])

        @blk.tensor
        def _(t):
            t.wait_ge(S["cload"], 16 * 5)
            for i in range(NT):
                t.wait_ge(S["xn"], i + 1)
                for gidx in range(4):
                    gi = i * 4 + gidx
                    if gi >= 2:
                        t.wait_ge(S["tpc"], gi - 1)
                    for j in range(8):
                        ins = t.transpose(ps_ln[gi % 2][:, j * 128:(j + 1) * 128],
                                          xn_tok[:, (gidx * 8 + j) * 128:(gidx * 8 + j + 1) * 128],
                                          idn_sb[:, :])
                        if j == 7:
                            ins.then_inc(S["tp"], 1)
            t.wait_ge(S["tpc"], 32)
            for widx in range(2 * ND):
                ot = widx % ND
                is_q = widx < ND
                if is_q and ot >= 1:
                    _emit_fold(t, S, bmat, stq, ps_v, ot - 1)
                if widx == ND:
                    _emit_fold(t, S, bmat, stq, ps_v, ND - 1)
                t.wait_ge(S[f"wload{widx % 2}"], 64 * (widx // 2 + 1))
                for tc in range(2):
                    gidx = widx * 2 + tc
                    if gidx >= 2:
                        t.wait_ge(S["ev"], gidx - 1)
                    for in_ in range(ND):
                        ins = t.matmul((ps_a if gidx % 2 == 0 else ps_b)[:, :],
                                       wbuf[widx % 2][:, in_ * 128:(in_ + 1) * 128],
                                       xnT[:, in_ * TOK + tc * 512:in_ * TOK + (tc + 1) * 512],
                                       start=(in_ == 0), stop=(in_ == ND - 1))
                        if in_ == ND - 1:
                            ins.then_inc(S["mm"], 1)

            t.wait_ge(S["fev"], 2 * ND)
            for ch in range(8):
                t.wait_ge(S[f"vwload{ch % 2}"], 80 * (ch // 2 + 1))
                for w in range(2):
                    wv_idx = ch * 2 + w
                    if wv_idx >= 1:
                        t.wait_ge(S["vev"], wv_idx * 4)
                    for in_ in range(ND):
                        for t4 in range(4):
                            tt = w * 4 + t4
                            ins = t.matmul(ps_v[t4][:, 0:512],
                                           xnT[:, in_ * TOK + tt * 128:in_ * TOK + (tt + 1) * 128],
                                           wvbuf[ch % 2][:, in_ * 512:(in_ + 1) * 512],
                                           start=(in_ == 0), stop=(in_ == ND - 1))
                            if in_ == ND - 1 and t4 == 3:
                                ins.then_inc(S["vmm"], 1)


def _emit_fold(t, S, bmat, stq, ps_v, ot):
    h = ot // 2
    t.wait_ge(S["vch"], 48)
    t.wait_ge(S["ev"], 2 * ot + 2)
    if ot >= 1:
        t.wait_ge(S["fev"], 2 * (ot - 1) + 2)
    for fc in range(2):
        ins = t.matmul(ps_v[fc][:, :], bmat[:, h * 128:(h + 1) * 128],
                       stq[ot % 2][:, fc * 512:(fc + 1) * 512],
                       start=True, stop=True)
        if fc == 1:
            ins.then_inc(S["foldmm"], 1)




def emit_poll(g, S, psem_name, side_sb, flags_sb, poll_sb, flc, fl0, fl1, pre=""):
    with (g.register(pre + "sdp") as sd, g.register(pre + "cont") as cont,
          g.register(pre + "fv") as fv, g.register(pre + "exp") as exp,
          g.register(pre + "pc") as pc):
        g.reg_load(sd, side_sb[0:1, 0:1])
        g.reg_load(exp, flags_sb[0:1, flc * 16:flc * 16 + 1])
        g.reg_mov(pc, 0)

        def loop(fl):
            g.reg_mov(cont, 1)
            with g.While(cont):
                g.reg_add(pc, pc, 16)
                g.dma_start(out=poll_sb[:, :], in_=fl[0:1, :]).then_inc(S[psem_name], 16)
                g.wait_ge(S[psem_name], pc)
                g.reg_load(fv, poll_sb[0:1, 0:1])
                g.reg_mov(cont, 1)
                with g.If_eq(fv, exp):
                    g.reg_mov(cont, 0)

        with g.If_eq(sd, 0):
            loop(fl1)
        with g.Else():
            loop(fl0)


def build_attn(layer):
    nc = bass.Bass("TRN2", target_bir_lowering=False)
    KVD = D if layer == 0 else L1_KV * L1_HD
    T = {}
    T["qT"] = nc.dram_tensor("qT", [D, TOK], BF16, kind="ExternalInput")
    T["kT"] = nc.dram_tensor("kT", [KVD, SEQ], BF16, kind="ExternalInput")
    T["v"] = nc.dram_tensor("v", [SEQ, KVD], BF16, kind="ExternalInput")
    T["x"] = nc.dram_tensor("x", [TOK, D], F32, kind="ExternalInput")
    T["idn"] = nc.dram_tensor("idn", [128, 128], BF16, kind="ExternalInput")
    T["idnf"] = nc.dram_tensor("idnf", [128, 128], F32, kind="ExternalInput")
    T["x1"] = nc.dram_tensor("x1", [TOK, D], F32, kind="ExternalOutput")
    with ExitStack() as sst:
        emit_attn(nc, _semf(nc, sst, "B"), T, layer, fused=False)
    return nc


def emit_attn(nc, sf, T, layer, fused, pre="", deq=None):
    if layer == 0:
        NH, HD, REP, SCALE = L0_NH, L0_HD, 1, None
        KVD = D
    else:
        NH, HD, REP, SCALE = L1_H, L1_HD, 4, 1.0 / np.sqrt(L1_HD)
        KVD = L1_KV * L1_HD
    qT_i, x_i, idn, x1_o = T["qT"], T["x"], T["idn"], T["x1"]
    idn_f = T["idnf"]
    if fused:
        k0, k1, v0, v1 = T["k0"], T["k1"], T["v0"], T["v1"]
        side_sb, flags_sb, poll_sb = T["side_sb"], T["flags_sb"], T["poll_sb"]
        fl0, fl1, FLC = T["fl0"], T["fl1"], T["flc"]
    else:
        kT_i, v_i = T["kT"], T["v"]
    NHT = HD // 128
    NU = NH * NT
    NG = NH // REP
    GU = REP * NT

    with ExitStack() as st:
        sb = lambda nm, sh, dt: st.enter_context(nc.sbuf_tensor(pre + nm, sh, dt))
        ps = lambda nm, sh, dt: st.enter_context(nc.psum_tensor(pre + nm, sh, dt))
        sem = sf

        qhb = [sb(f"qhb{i}", [128, NHT * TOK], BF16) for i in range(2)]
        kbuf = [sb(f"kbuf{i}", [128, NHT * SEQ], BF16) for i in range(2)]
        vbuf = [sb(f"vbuf{i}", [128, NK * HD], BF16) for i in range(2)]
        pb_sb = [sb(f"pb_sb{i}", [128, SEQ], BF16) for i in range(2)]
        ptsb = [sb(f"ptsb{i}", [128, NK * 512], BF16) for i in range(2)]
        osb = [sb(f"osb{i}", [128, 512], F32) for i in range(2)]
        xsl = [sb(f"xsl{i}", [128, HD], F32) for i in range(4)]
        xo = [sb(f"xo{i}", [128, HD], F32) for i in range(4)]
        stat2 = sb("stat2", [128, 64], F32)
        idn_sb = sb("idn_sb", [128, 128], BF16)
        idn32 = sb("idn32", [128, 128], F32)

        ps_s = ps("ps_s", [128, SEQ], F32)
        ps_t = [ps(f"ps_t{i}", [128, 1024], BF16) for i in range(2)]
        ps_o = [ps(f"ps_o{i}", [128, 512], F32) for i in range(2)]

        blk = st.enter_context(nc.Block())
        S = {n: sem(n) for n in
             ["cload", "qlo0", "qlo1", "klo0", "klo1", "smmc", "mx", "exq", "tp2",
              "tpc2", "tpc2a", "pvs", "oev", "untr", "sst", "xld0", "xld1", "xld2", "xld3",
              "xst0", "xst1", "xst2", "xst3", "vch", "ach", "psem"]}
        NKC = SEQ // 512
        NSU = NU // 4
        if deq is not None:
            for n in ["dcl0", "dcl1", "dwd", "dst0", "dst1", "dflst"]:
                S[n] = sem(n)
            dq_codes, dq_amx, dq_out = deq["codes"], deq["amx"], deq["wT_sh"]
            dq_fl0, dq_fl1, dq_flc = deq["flW0"], deq["flW1"], deq["flcW"]
            DQ0 = 2 * deq.get("t0", 0)
            dq_cbuf = [sb(f"dq_cbuf{i}", [128, 2048], I32) for i in range(2)]
            dq_mbuf = [sb(f"dq_mbuf{i}", [128, 2048], F32) for i in range(2)]
            dq_cf = sb("dq_cf", [128, 2048], BF16)
            dq_selt = sb("dq_selt", [128, 2048], BF16)
            dq_acc = [sb(f"dq_acc{i}", [128, 2048], BF16) for i in range(2)]
            NDQ = 32

            def dq_srcs(j):
                r, hf = j // 2, j % 2
                return (dq_codes[r * 128:(r + 1) * 128, hf * 2048:(hf + 1) * 2048],
                        bass.AP(dq_amx, (r * 2) * D + hf * 2048,
                                [[D, 2], [0, 64], [1, 2048]]))

        KLN = 64 if fused else 32

        def kv_load(s, g):
            if not fused:
                srck = bass.AP(kT_i, (g * HD) * SEQ,
                               [[SEQ, 128], [128 * SEQ, NHT], [1, SEQ]])
                s.dma_start(out=kbuf[g % 2][:, :], in_=srck).then_inc(S[f"klo{g % 2}"], 16)
                srcv = bass.AP(v_i, g * HD,
                               [[KVD, 128], [128 * KVD, NK], [1, HD]])
                s.dma_start(out=vbuf[g % 2][:, :], in_=srcv).then_inc(S[f"klo{g % 2}"], 16)
            else:
                kb_r = kbuf[g % 2].ap().rearrange("p (d s) -> p d s", s=SEQ)
                for half, ksrc in ((0, k0), (1, k1)):
                    srck = bass.AP(ksrc, (g * HD) * TOK,
                                   [[TOK, 128], [128 * TOK, NHT], [1, TOK]])
                    s.dma_start(out=kb_r[:, :, half * TOK:(half + 1) * TOK],
                                in_=srck).then_inc(S[f"klo{g % 2}"], 16)
                for half, vsrc in ((0, v0), (1, v1)):
                    srcv = bass.AP(vsrc, g * HD,
                                   [[KVD, 128], [128 * KVD, NK // 2], [1, HD]])
                    s.dma_start(out=vbuf[g % 2][:, half * (NK // 2) * HD:(half + 1) * (NK // 2) * HD],
                                in_=srcv).then_inc(S[f"klo{g % 2}"], 16)

        @blk.sync
        def _(s):
            s.dma_start(out=idn_sb[:, :], in_=idn[:, :]).then_inc(S["cload"], 16)
            s.dma_start(out=idn32[:, :], in_=idn_f[:, :]).then_inc(S["cload"], 16)
            for u in range(NU):
                h, qt = u // NT, u % NT
                if u >= 4:
                    s.wait_ge(S["sst"], u - 3)
                s.dma_start(out=xsl[u % 4][:, :],
                            in_=bass.AP(x_i, qt * 128 * D + h * HD,
                                        [[D, 128], [1, HD]])).then_inc(S[f"xld{u % 4}"], 16)
                if u >= 1:
                    up = u - 1
                    hp, qtp = up // NT, up % NT
                    s.wait_ge(S["sst"], up + 1)
                    s.dma_start(out=bass.AP(x1_o, qtp * 128 * D + hp * HD,
                                            [[D, 128], [1, HD]]),
                                in_=xo[up % 4][:, :]).then_inc(S[f"xst{up % 4}"], 16)
            up = NU - 1
            s.wait_ge(S["sst"], up + 1)
            s.dma_start(out=bass.AP(x1_o, (up % NT) * 128 * D + (up // NT) * HD,
                                    [[D, 128], [1, HD]]),
                        in_=xo[up % 4][:, :]).then_inc(S[f"xst{up % 4}"], 16)
            for p_ in range(4):
                s.wait_ge(S[f"xst{p_}"], 16 * (NU // 4))

        @blk.gpsimd
        def _(g):
            dq_sd_ctx = g.register(pre + "gsd") if deq is not None else None
            dq_sd = dq_sd_ctx.__enter__() if deq is not None else None

            def dq_load(t_):
                if t_ - DQ0 >= 2:
                    g.wait_ge(S["dwd"], t_ - DQ0 - 1)
                csrc, msrc = dq_srcs(t_)
                g.dma_start(out=dq_cbuf[t_ % 2][:, :],
                            in_=csrc).then_inc(S[f"dcl{t_ % 2}"], 16)
                g.dma_start(out=dq_mbuf[t_ % 2][:, :],
                            in_=msrc).then_inc(S[f"dcl{t_ % 2}"], 16)

            def dq_store(t_):
                g.wait_ge(S["dwd"], t_ - DQ0 + 1)
                r, hf = t_ // 2, t_ % 2
                cs = slice(hf * 2048, (hf + 1) * 2048)
                with g.If_eq(dq_sd, 0):
                    g.dma_start(out=dq_out[r * 128:(r + 1) * 128, cs],
                                in_=dq_acc[t_ % 2][:, :]).then_inc(S[f"dst{t_ % 2}"], 16)
                with g.Else():
                    g.dma_start(out=dq_out[2048 + r * 128:2048 + (r + 1) * 128, cs],
                                in_=dq_acc[t_ % 2][:, :]).then_inc(S[f"dst{t_ % 2}"], 16)

            if deq is not None:
                g.reg_load(dq_sd, side_sb[0:1, 0:1])
                dq_load(DQ0)
                dq_load(DQ0 + 1)
                for t_ in range(DQ0 + 2, NDQ):
                    dq_load(t_)
                    dq_store(t_ - 2)
                for t_ in range(NDQ - 2, NDQ):
                    dq_store(t_)
                g.wait_ge(S["dst0"], 16 * ((NDQ - DQ0) // 2))
                g.wait_ge(S["dst1"], 16 * ((NDQ - DQ0) // 2))
                with g.If_eq(dq_sd, 0):
                    g.dma_start(out=dq_fl0[:, :],
                                in_=flags_sb[0:1, dq_flc * 16:(dq_flc + 1) * 16]).then_inc(S["dflst"], 16)
                with g.Else():
                    g.dma_start(out=dq_fl1[:, :],
                                in_=flags_sb[0:1, dq_flc * 16:(dq_flc + 1) * 16]).then_inc(S["dflst"], 16)
                g.wait_ge(S["dflst"], 16)
                dq_sd_ctx.__exit__(None, None, None)
            if fused:
                emit_poll(g, S, "psem", side_sb, flags_sb, poll_sb, FLC, fl0, fl1, pre=pre)
            for h in range(NH):
                if h % REP == 0:
                    gg = h // REP
                    if gg >= 2:
                        g.wait_ge(S["smmc"], 4 * GU * (gg - 1))
                        g.wait_ge(S["pvs"], (GU // 4) * NHT * (gg - 1))
                    kv_load(g, gg)
                if h >= 2:
                    g.wait_ge(S["smmc"], 4 * NT * (h - 1))
                srcq = bass.AP(qT_i, (h * NHT * 128) * TOK,
                               [[TOK, 128], [128 * TOK, NHT], [1, TOK]])
                g.dma_start(out=qhb[h % 2][:, :], in_=srcq).then_inc(S[f"qlo{h % 2}"], 16)


        @blk.tensor
        def _(t):
            t.wait_ge(S["cload"], 32)
            for u in range(NU):
                h, qt = u // NT, u % NT
                g_ = h // REP
                t.wait_ge(S[f"klo{g_ % 2}"], KLN * (g_ // 2 + 1))
                t.wait_ge(S[f"qlo{h % 2}"], 16 * (h // 2 + 1))
                for d in range(NHT):
                    for kc in range(NKC):
                        if d == 0 and u >= 1:
                            t.wait_ge(S["exq"], (u - 1) * NKC + kc + 1)
                        ins = t.matmul(ps_s[:, kc * 512:(kc + 1) * 512],
                                       qhb[h % 2][:, d * TOK + qt * 128:d * TOK + (qt + 1) * 128],
                                       kbuf[g_ % 2][:, d * SEQ + kc * 512:d * SEQ + (kc + 1) * 512],
                                       start=(d == 0), stop=(d == NHT - 1))
                        if d == NHT - 1:
                            ins.then_inc(S["smmc"], 1)
                if u >= 5 and (u - 5) % 4 == 0:
                    _b_untr(t, S, (u - 5) // 4, osb, ps_o, idn32, NHT)
                if u >= 1:
                    _b_trans(t, S, u - 1, pb_sb, ps_t, idn_sb, NK)
                    if (u - 1) % 4 == 3:
                        _b_pv(t, S, (u - 1) // 4, ptsb, vbuf, osb, ps_o, idn32,
                              NHT, NK, HD, REP)
            _b_trans(t, S, NU - 1, pb_sb, ps_t, idn_sb, NK)
            _b_pv(t, S, NSU - 1, ptsb, vbuf, osb, ps_o, idn32, NHT, NK, HD, REP)
            _b_untr(t, S, NSU - 1, osb, ps_o, idn32, NHT)

        @blk.scalar
        def _(a):
            ch = Ch(a, S["ach"])
            for u in range(NU):
                ch.wait(S["mx"], u + 1)
                if u >= 2:
                    ch.wait(S["tp2"], u - 1)
                c = (u % 4) * 16
                kw = {} if SCALE is None else {"scale": float(SCALE)}
                for kc in range(4):
                    ch.op_m(S["exq"], u * 4 + kc + 1, a.activation,
                            pb_sb[u % 2][:, kc * 512:(kc + 1) * 512],
                            ps_s[:, kc * 512:(kc + 1) * 512],
                            AF.Exp, bias=stat2[:, c + 4:c + 5],
                            accum_out=stat2[:, c + 5 + kc:c + 6 + kc], **kw)
                if u >= 1:
                    _attn_acopy(a, ch, S, u - 1, ptsb, ps_t)
            _attn_acopy(a, ch, S, NU - 1, ptsb, ps_t)

        @blk.vector
        def _(v):
            ch = Ch(v, S["vch"])

            def deq_gen():
                for t_ in range(DQ0, NDQ):
                    lt = t_ - DQ0
                    ch.wait(S[f"dcl{t_ % 2}"], 32 * (lt // 2 + 1))
                    if lt >= 2:
                        ch.wait(S[f"dst{t_ % 2}"], 16 * (lt // 2))
                    ch.op(v.tensor_copy, dq_cf[:, :], dq_cbuf[t_ % 2][:, :])
                    ch.op(v.tensor_scalar, dq_acc[t_ % 2][:, :], dq_cf[:, :], 0.0,
                          NF4[0], ALU.is_equal, ALU.mult)
                    for tv in range(1, 16):
                        ch.op(v.tensor_scalar, dq_selt[:, :], dq_cf[:, :], float(tv),
                              NF4[tv], ALU.is_equal, ALU.mult)
                        ch.op(v.tensor_tensor, dq_acc[t_ % 2][:, :], dq_acc[t_ % 2][:, :],
                              dq_selt[:, :], ALU.add)
                    ch.op_m(S["dwd"], lt + 1, v.tensor_tensor, dq_acc[t_ % 2][:, :],
                            dq_acc[t_ % 2][:, :], dq_mbuf[t_ % 2][:, :], ALU.mult)

            if deq is not None:
                deq_gen()
            mxn = 0
            sstn = 0
            for u in range(NU):
                c = (u % 4) * 16
                for kc in range(4):
                    ch.wait(S["smmc"], u * 4 + kc + 1)
                    ch.op(v.reduce_max, stat2[:, c + kc:c + kc + 1],
                          ps_s[:, kc * 512:(kc + 1) * 512], axis=AX.X, negate=True)
                mxn += 1
                if SCALE is None:
                    ch.op_m(S["mx"], mxn, v.tensor_reduce, stat2[:, c + 4:c + 5],
                            stat2[:, c:c + 4], AX.X, ALU.min)
                else:
                    ch.op(v.tensor_reduce, stat2[:, c + 9:c + 10],
                          stat2[:, c:c + 4], AX.X, ALU.min)
                    ch.op_m(S["mx"], mxn, v.tensor_scalar, stat2[:, c + 4:c + 5],
                            stat2[:, c + 9:c + 10], float(SCALE), None, ALU.mult)
                if u >= 5 and (u - 5) % 4 == 0:
                    sstn = _attn_vstt(v, ch, S, (u - 5) // 4, ps_o, stat2,
                                      xo, xsl, HD, NHT, sstn)
                if u >= 1:
                    _attn_vcopy(v, ch, S, u - 1, ptsb, ps_t, stat2)
                    if (u - 1) % 4 == 3:
                        _attn_vev(v, ch, S, (u - 1) // 4, osb, ps_o, NHT)
            _attn_vcopy(v, ch, S, NU - 1, ptsb, ps_t, stat2)
            _attn_vev(v, ch, S, NSU - 1, osb, ps_o, NHT)
            sstn = _attn_vstt(v, ch, S, NSU - 1, ps_o, stat2, xo, xsl, HD, NHT, sstn)


def _attn_vcopy(v, ch, S, up, ptsb, ps_t, stat2):
    su, sq = up // 4, up % 4
    cp = (up % 4) * 16
    pt_r = ptsb[su % 2].ap().rearrange("p (k q) -> p k q", q=512)
    ch.wait(S["tp2"], up + 1)
    ch.op_m(S["tpc2"], up + 1, v.tensor_copy,
            pt_r[:, 0:8, sq * 128:(sq + 1) * 128], ps_t[0][:, :])
    ch.wait(S["exq"], 4 * (up + 1))
    ch.op(v.tensor_reduce, stat2[:, cp + 10:cp + 11], stat2[:, cp + 5:cp + 9],
          AX.X, ALU.add)
    ch.op(v.reciprocal, stat2[:, cp + 11:cp + 12], stat2[:, cp + 10:cp + 11])


def _attn_acopy(a, ch, S, up, ptsb, ps_t):
    su, sq = up // 4, up % 4
    pt_r = ptsb[su % 2].ap().rearrange("p (k q) -> p k q", q=512)
    ch.wait(S["tp2"], up + 1)
    ch.op_m(S["tpc2a"], up + 1, a.activation,
            pt_r[:, 8:16, sq * 128:(sq + 1) * 128], ps_t[1][:, :], AF.Copy)


def _attn_vev(v, ch, S, su, osb, ps_o, NHT):
    for dt in range(NHT):
        odt = su * NHT + dt
        ch.wait(S["pvs"], odt + 1)
        ch.op_m(S["oev"], odt + 1, v.tensor_copy, osb[odt % 2][:, :],
                ps_o[odt % 2][:, :])


def _attn_vstt(v, ch, S, su, ps_o, stat2, xo, xsl, HD, NHT, sstn):
    for sq in range(4):
        u = su * 4 + sq
        cp = (u % 4) * 16
        ch.wait(S[f"xld{u % 4}"], 16 * (u // 4 + 1))
        if u >= 4:
            ch.wait(S[f"xst{u % 4}"], 16 * (u // 4))
        for dt in range(NHT):
            odt = su * NHT + dt
            ch.wait(S["untr"], odt + 1)
            sstn += (1 if dt == NHT - 1 else 0)
            if dt == NHT - 1:
                ch.op_m(S["sst"], sstn, v.scalar_tensor_tensor,
                        xo[u % 4][:, dt * 128:(dt + 1) * 128],
                        ps_o[odt % 2][:, sq * 128:(sq + 1) * 128],
                        stat2[:, cp + 11:cp + 12],
                        xsl[u % 4][:, dt * 128:(dt + 1) * 128], ALU.mult, ALU.add)
            else:
                ch.op(v.scalar_tensor_tensor,
                      xo[u % 4][:, dt * 128:(dt + 1) * 128],
                      ps_o[odt % 2][:, sq * 128:(sq + 1) * 128],
                      stat2[:, cp + 11:cp + 12],
                      xsl[u % 4][:, dt * 128:(dt + 1) * 128], ALU.mult, ALU.add)
    return sstn


def _b_trans(t, S, up, pb_sb, ps_t, idn_sb, NKt):
    t.wait_ge(S["exq"], 4 * (up + 1))
    if up >= 1:
        t.wait_ge(S["tpc2"], up)
        t.wait_ge(S["tpc2a"], up)
    for j in range(NKt):
        ins = t.transpose(ps_t[j // 8][:, (j % 8) * 128:(j % 8 + 1) * 128],
                          pb_sb[up % 2][:, j * 128:(j + 1) * 128], idn_sb[:, :])
        if j == NKt - 1:
            ins.then_inc(S["tp2"], 1)


def _b_pv(t, S, su, ptsb, vbuf, osb, ps_o, idn32, NHT, NKt, HD, REP):
    hp = ((su * 4) // NT) // REP
    t.wait_ge(S["tpc2"], 4 * su + 4)
    t.wait_ge(S["tpc2a"], 4 * su + 4)
    for dt in range(NHT):
        odt = su * NHT + dt
        if odt >= 2:
            su2 = (odt - 2) // NHT
            t.wait_ge(S["sst"], 4 * (su2 + 1))
        for kt in range(NKt):
            ins = t.matmul(ps_o[odt % 2][:, :],
                           vbuf[hp % 2][:, kt * HD + dt * 128:kt * HD + (dt + 1) * 128],
                           ptsb[su % 2][:, kt * 512:(kt + 1) * 512],
                           start=(kt == 0), stop=(kt == NKt - 1))
            if kt == NKt - 1:
                ins.then_inc(S["pvs"], 1)

def _b_untr(t, S, su, osb, ps_o, idn32, NHT):
    for dt in range(NHT):
        odt = su * NHT + dt
        t.wait_ge(S["oev"], odt + 1)
        for sq in range(4):
            ins = t.transpose(ps_o[odt % 2][:, sq * 128:(sq + 1) * 128],
                              osb[odt % 2][:, sq * 128:(sq + 1) * 128], idn32[:, :])
            if sq == 3:
                ins.then_inc(S["untr"], 1)




def build_dequant():
    nc = bass.Bass("TRN2", target_bir_lowering=False)
    T = {}
    T["codes"] = nc.dram_tensor("codes", [512, D], I32, kind="ExternalInput")
    T["amx"] = nc.dram_tensor("amx", [8, D], F32, kind="ExternalInput")
    T["wsh"] = nc.dram_tensor("wsh", [512, D], BF16, kind="ExternalOutput")
    with ExitStack() as sst:
        emit_dequant(nc, _semf(nc, sst, "C"), T, fused=False)
    return nc


def emit_dequant(nc, sf, T, fused, pre=""):
    codes, amx = T["codes"], T["amx"]
    NTL = 16 if fused else 4
    if fused:
        wT_sh = T["wT_sh"]
        side_sb, flags_sb = T["side_sb"], T["flags_sb"]
        fl0, fl1, FLC = T["flW0"], T["flW1"], T["flcW"]
    else:
        wsh = T["wsh"]
    with ExitStack() as st:
        sb = lambda nm, sh, dt: st.enter_context(nc.sbuf_tensor(pre + nm, sh, dt))
        sem = sf
        cbuf = [sb(f"cbuf{i}", [128, D], I32) for i in range(2)]
        mbuf = [sb(f"mbuf{i}", [128, D], F32) for i in range(2)]
        cf = sb("cf", [128, D], BF16)
        selt = sb("selt", [128, D], BF16)
        acc = [sb(f"acc{i}", [128, D], BF16) for i in range(2)]
        blk = st.enter_context(nc.Block())
        S = {n: sem(n) for n in ["cl0", "cl1", "wd", "st0", "st1", "vch", "flst"]}

        @blk.sync
        def _(s):
            sd_ctx = s.register(pre + "sd") if fused else None
            sd = sd_ctx.__enter__() if fused else None
            if fused:
                s.reg_load(sd, side_sb[0:1, 0:1])

            def do_store(t):
                s.wait_ge(S["wd"], t + 1)
                if fused:
                    with s.If_eq(sd, 0):
                        s.dma_start(out=wT_sh[t * 128:(t + 1) * 128, :],
                                    in_=acc[t % 2][:, :]).then_inc(S[f"st{t % 2}"], 16)
                    with s.Else():
                        s.dma_start(out=wT_sh[2048 + t * 128:2048 + (t + 1) * 128, :],
                                    in_=acc[t % 2][:, :]).then_inc(S[f"st{t % 2}"], 16)
                else:
                    s.dma_start(out=wsh[t * 128:(t + 1) * 128, :],
                                in_=acc[t % 2][:, :]).then_inc(S[f"st{t % 2}"], 16)

            for t in range(NTL):
                if t >= 2:
                    s.wait_ge(S["wd"], t - 1)
                s.dma_start(out=cbuf[t % 2][:, :],
                            in_=codes[t * 128:(t + 1) * 128, :]).then_inc(S[f"cl{t % 2}"], 16)
                s.dma_start(out=mbuf[t % 2][:, :],
                            in_=bass.AP(amx, (t * 2) * D, [[D, 2], [0, 64], [1, D]])).then_inc(S[f"cl{t % 2}"], 16)
                if t >= 2:
                    do_store(t - 2)
            do_store(NTL - 2)
            do_store(NTL - 1)
            s.wait_ge(S["st0"], 16 * (NTL // 2))
            s.wait_ge(S["st1"], 16 * (NTL // 2))
            if fused:
                with s.If_eq(sd, 0):
                    s.dma_start(out=fl0[:, :], in_=flags_sb[0:1, FLC * 16:(FLC + 1) * 16]).then_inc(S["flst"], 16)
                with s.Else():
                    s.dma_start(out=fl1[:, :], in_=flags_sb[0:1, FLC * 16:(FLC + 1) * 16]).then_inc(S["flst"], 16)
                s.wait_ge(S["flst"], 16)
                sd_ctx.__exit__(None, None, None)

        @blk.vector
        def _(v):
            ch = Ch(v, S["vch"])
            for t in range(NTL):
                ch.wait(S[f"cl{t % 2}"], 32 * (t // 2 + 1))
                if t >= 2:
                    ch.wait(S[f"st{t % 2}"], 16 * (t // 2))
                ch.op(v.tensor_copy, cf[:, :], cbuf[t % 2][:, :])
                ch.op(v.tensor_scalar, acc[t % 2][:, :], cf[:, :], 0.0, NF4[0],
                      ALU.is_equal, ALU.mult)
                for tv in range(1, 16):
                    ch.op(v.tensor_scalar, selt[:, :], cf[:, :], float(tv), NF4[tv],
                          ALU.is_equal, ALU.mult)
                    ch.op(v.tensor_tensor, acc[t % 2][:, :], acc[t % 2][:, :],
                          selt[:, :], ALU.add)
                ch.op_m(S["wd"], t + 1, v.tensor_tensor, acc[t % 2][:, :],
                        acc[t % 2][:, :], mbuf[t % 2][:, :], ALU.mult)


def build_ffn(layer):
    nc = bass.Bass("TRN2", target_bir_lowering=False)
    T = {}
    T["x"] = nc.dram_tensor("x", [TOK, D], F32, kind="ExternalInput")
    if layer == 0:
        T["wT"] = nc.dram_tensor("wT", [D, D], BF16, kind="ExternalInput")
        T["waT"] = nc.dram_tensor("waT", [128, 512], F32, kind="ExternalInput")
        T["wbT"] = nc.dram_tensor("wbT", [16, D], F32, kind="ExternalInput")
    else:
        T["wT"] = nc.dram_tensor("wT", [D, D], F16, kind="ExternalInput")
        T["brow"] = nc.dram_tensor("brow", [1, D], F16, kind="ExternalInput")
    T["idn"] = nc.dram_tensor("idn", [128, 128], BF16, kind="ExternalInput")
    T["xo"] = nc.dram_tensor("xo", [TOK, D], F32, kind="ExternalOutput")
    with ExitStack() as sst:
        emit_ffn(nc, _semf(nc, sst, "D"), T, layer, fused=False)
    return nc


def emit_ffn(nc, sf, T, layer, fused, pre=""):
    x_i, wT, idn, xo_o = T["x"], T["wT"], T["idn"], T["xo"]
    if layer == 0:
        waT, wbT = T["waT"], T["wbT"]
    else:
        brow = T["brow"]
    if fused and layer == 0:
        side_sb, flags_sb, poll_sb = T["side_sb"], T["flags_sb"], T["poll_sb"]
        fl0, fl1, FLC = T["flW0"], T["flW1"], T["flcW"]
    NCH = 8
    CW = 512
    WDT = BF16 if layer == 0 else F16
    import os
    dbg_nolora = bool(os.environ.get("FUSE_DBG_NOLORA"))

    with ExitStack() as st:
        sb = lambda nm, sh, dt: st.enter_context(nc.sbuf_tensor(pre + nm, sh, dt))
        ps = lambda nm, sh, dt: st.enter_context(nc.psum_tensor(pre + nm, sh, dt))
        sem = sf
        xnT = sb("xnT", [128, ND * TOK], BF16)
        xnT_r = xnT.ap().rearrange("p (f t) -> p f t", t=TOK)
        x_t = [sb(f"x_t{i}", [128, D], F32) for i in range(2)]
        stat = sb("stat", [128, 8 * NT], F32)
        bnbuf = sb("bnbuf", [128, 48], F32)
        xn_tok = sb("xn_tok", [128, D], BF16)
        idn_sb = sb("idn_sb", [128, 128], BF16)
        wvbuf = [sb(f"wvbuf{i}", [128, ND * CW], WDT) for i in range(2)]
        xsl = [sb(f"xsl{i}", [128, CW], F32) for i in range(2)]
        xos = [sb(f"xos{i}", [128, CW], F32) for i in range(2)]
        if layer == 0:
            wa_sb = sb("wa_sb", [128, 512], BF16)
            wb_sb = sb("wb_sb", [16, D], BF16)
            zsb = sb("zsb", [16, TOK], BF16)
        else:
            biasb = [sb(f"biasb{i}", [128, CW], F32) for i in range(2)]
            tmpb = sb("tmpb", [128, CW], F32)
        ps_ln = [ps(f"ps_ln{i}", [128, 1024], BF16) for i in range(2)]
        ps_y = [ps(f"ps_y{i}", [128, 512], F32) for i in range(4)]
        if layer == 0:
            ps_z = ps("ps_z", [16, TOK], F32)
        blk = st.enter_context(nc.Block())
        S = {n: sem(n) for n in
             ["cload", "cload2", "vch", "ach", "xload0", "xload1", "var", "sqv", "xn",
              "tp", "tpc", "vwload0", "vwload1", "vmm", "vev",
              "zmm", "zev", "xld0", "xld1", "xst0", "xst1", "psem"]}

        @blk.sync
        def _(s):
            s.dma_start(out=idn_sb[:, :], in_=idn[:, :]).then_inc(S["cload"], 16)
            for i in range(NT):
                if i >= 2:
                    s.wait_ge(S["xn"], i - 1)
                s.dma_start(out=x_t[i % 2][:, :], in_=x_i[i * 128:(i + 1) * 128, :]).then_inc(S[f"xload{i % 2}"], 16)
            NE = NCH * 8
            for e in range(NE):
                chn, tt = e // 8, e % 8
                if e >= 2:
                    s.wait_ge(S["vev"], e - 1)
                s.dma_start(out=xsl[e % 2][:, :],
                            in_=bass.AP(x_i, tt * 128 * D + chn * CW,
                                        [[D, 128], [1, CW]])).then_inc(S[f"xld{e % 2}"], 16)
                if e >= 1:
                    ep = e - 1
                    s.wait_ge(S["vev"], ep + 1)
                    s.dma_start(out=bass.AP(xo_o, (ep % 8) * 128 * D + (ep // 8) * CW,
                                            [[D, 128], [1, CW]]),
                                in_=xos[ep % 2][:, :]).then_inc(S[f"xst{ep % 2}"], 16)
            ep = NE - 1
            s.wait_ge(S["vev"], ep + 1)
            s.dma_start(out=bass.AP(xo_o, (ep % 8) * 128 * D + (ep // 8) * CW,
                                    [[D, 128], [1, CW]]),
                        in_=xos[ep % 2][:, :]).then_inc(S[f"xst{ep % 2}"], 16)
            s.wait_ge(S["xst0"], 16 * (NE // 2))
            s.wait_ge(S["xst1"], 16 * (NE // 2))

        @blk.gpsimd
        def _(g):
            if layer == 0:
                g.dma_start(out=wa_sb[:, :], in_=waT[:, :]).then_inc(S["cload2"], 16)
                g.dma_start(out=wb_sb[:, :], in_=wbT[:, :]).then_inc(S["cload2"], 16)
            import os
            if fused and layer == 0 and not os.environ.get("FUSE_DBG_NOPOLL"):
                emit_poll(g, S, "psem", side_sb, flags_sb, poll_sb, FLC, fl0, fl1, pre=pre)
            for chn in range(NCH):
                if chn >= 2:
                    g.wait_ge(S["vev"], (chn - 1) * 8)
                if layer == 1:
                    g.dma_start(out=biasb[chn % 2][:, :],
                                in_=bass.AP(brow, chn * CW, [[0, 128], [1, CW]])).then_inc(S[f"vwload{chn % 2}"], 16)
                for part in range(4):
                    srcw = bass.AP(wT, (part * 8 * 128) * D + chn * CW,
                                   [[D, 128], [128 * D, 8], [1, CW]])
                    g.dma_start(out=wvbuf[chn % 2][:, part * 8 * CW:(part + 1) * 8 * CW],
                                in_=srcw).then_inc(S[f"vwload{chn % 2}"], 16)

        @blk.vector
        def _(v):
            ch = Ch(v, S["vch"])
            ch.wait(S["cload"], 16)
            xn_n = 0
            tpc_n = 0
            for i in range(NT):
                c = 8 * i
                ch.wait(S[f"xload{i % 2}"], 16 * (i // 2 + 1))
                for ck in range(8):
                    ch.op(v.bn_stats, bnbuf[:, ck * 6:(ck + 1) * 6],
                          x_t[i % 2][:, ck * 512:(ck + 1) * 512])
                ch.op(v.bn_aggr, stat[:, c:c + 2], bnbuf[:, :])
                ch.op_m(S["var"], i + 1, v.tensor_scalar, stat[:, c + 2:c + 3],
                        stat[:, c + 1:c + 2], LN_EPS, None, ALU.add)
                ch.wait(S["sqv"], i + 1)
                ch.op(v.reciprocal, stat[:, c + 7:c + 8], stat[:, c + 6:c + 7])
                if i >= 1:
                    ch.wait(S["tp"], 4 * i)
                xn_n += 1
                ch.op_m(S["xn"], xn_n, v.tensor_scalar, xn_tok[:, :], x_t[i % 2][:, :],
                        stat[:, c:c + 1], stat[:, c + 7:c + 8], ALU.subtract, ALU.mult)
                for gidx in range(4):
                    gi = i * 4 + gidx
                    ch.wait(S["tp"], gi + 1)
                    tpc_n += 1
                    ch.op_m(S["tpc"], tpc_n, v.tensor_copy,
                            xnT_r[:, gidx * 8:(gidx + 1) * 8, i * 128:(i + 1) * 128],
                            ps_ln[gi % 2][:, :])
            vev_n = 0
            for chn in range(NCH):
                for w in range(2):
                    wv_idx = chn * 2 + w
                    ch.wait(S["vmm"], wv_idx + 1)
                    for t4 in range(4):
                        tt = w * 4 + t4
                        e = chn * 8 + tt
                        ch.wait(S[f"xld{e % 2}"], 16 * (e // 2 + 1))
                        if e >= 2:
                            ch.wait(S[f"xst{e % 2}"], 16 * (e // 2))
                        vev_n += 1
                        if layer == 0:
                            ch.op_m(S["vev"], vev_n, v.tensor_tensor, xos[e % 2][:, :],
                                    ps_y[t4][:, 0:CW], xsl[e % 2][:, :], ALU.add)
                        else:
                            ch.op(v.tensor_tensor, tmpb[:, :], ps_y[t4][:, 0:CW],
                                  biasb[chn % 2][:, :], ALU.add)
                            ch.op_m(S["vev"], vev_n, v.tensor_tensor, xos[e % 2][:, :],
                                    tmpb[:, :], xsl[e % 2][:, :], ALU.add)

        @blk.scalar
        def _(a):
            ch = Ch(a, S["ach"])
            for i in range(NT):
                c = 8 * i
                ch.wait(S["var"], i + 1)
                ch.op_m(S["sqv"], i + 1, a.activation, stat[:, c + 6:c + 7],
                        stat[:, c + 2:c + 3], AF.Sqrt)
            if layer == 0 and not dbg_nolora:
                for tc in range(2):
                    ch.wait(S["zmm"], tc + 1)
                    ch.op_m(S["zev"], tc + 1, a.activation,
                            zsb[:, tc * 512:(tc + 1) * 512],
                            ps_z[:, tc * 512:(tc + 1) * 512], AF.Copy,
                            scale=float(FFN_SCALE))

        @blk.tensor
        def _(t):
            t.wait_ge(S["cload"], 16)
            if layer == 0:
                t.wait_ge(S["cload2"], 32)
            for i in range(NT):
                t.wait_ge(S["xn"], i + 1)
                for gidx in range(4):
                    gi = i * 4 + gidx
                    if gi >= 2:
                        t.wait_ge(S["tpc"], gi - 1)
                    for j in range(8):
                        ins = t.transpose(ps_ln[gi % 2][:, j * 128:(j + 1) * 128],
                                          xn_tok[:, (gidx * 8 + j) * 128:(gidx * 8 + j + 1) * 128],
                                          idn_sb[:, :])
                        if j == 7:
                            ins.then_inc(S["tp"], 1)
            t.wait_ge(S["tpc"], 32)
            if layer == 0 and not dbg_nolora:
                for tc in range(2):
                    for in_ in range(ND):
                        ins = t.matmul(ps_z[:, tc * 512:(tc + 1) * 512],
                                       wa_sb[:, in_ * 16:(in_ + 1) * 16],
                                       xnT[:, in_ * TOK + tc * 512:in_ * TOK + (tc + 1) * 512],
                                       start=(in_ == 0), stop=(in_ == ND - 1))
                        if in_ == ND - 1:
                            ins.then_inc(S["zmm"], 1)
            for chn in range(NCH):
                t.wait_ge(S[f"vwload{chn % 2}"], (80 if layer == 1 else 64) * (chn // 2 + 1))
                for w in range(2):
                    wv_idx = chn * 2 + w
                    if wv_idx >= 1:
                        t.wait_ge(S["vev"], wv_idx * 4)
                    last_in = (layer == 1) or dbg_nolora
                    for in_ in range(ND):
                        for t4 in range(4):
                            tt = w * 4 + t4
                            ins = t.matmul(ps_y[t4][:, 0:CW],
                                           xnT[:, in_ * TOK + tt * 128:in_ * TOK + (tt + 1) * 128],
                                           wvbuf[chn % 2][:, in_ * CW:(in_ + 1) * CW],
                                           start=(in_ == 0),
                                           stop=(last_in and in_ == ND - 1))
                            if layer == 1 and in_ == ND - 1 and t4 == 3:
                                ins.then_inc(S["vmm"], 1)
                    if layer == 0 and not dbg_nolora:
                        if chn == 0 and w == 0:
                            t.wait_ge(S["zev"], 2)
                        for t4 in range(4):
                            tt = w * 4 + t4
                            ins = t.matmul(ps_y[t4][:, 0:CW],
                                           zsb[:, tt * 128:(tt + 1) * 128],
                                           wb_sb[:, chn * CW:(chn + 1) * CW],
                                           start=False, stop=True)
                            if t4 == 3:
                                ins.then_inc(S["vmm"], 1)




def build_l1qkv():
    nc = bass.Bass("TRN2", target_bir_lowering=False)
    KVD = L1_KV * L1_HD
    NKT = KVD // 128
    T = {}
    T["x"] = nc.dram_tensor("x", [TOK, D], F32, kind="ExternalInput")
    T["wq1"] = nc.dram_tensor("wq1", [D, D], F32, kind="ExternalInput")
    T["wk1"] = nc.dram_tensor("wk1", [D, KVD], F32, kind="ExternalInput")
    T["wv1"] = nc.dram_tensor("wv1", [D, KVD], F32, kind="ExternalInput")
    T["bq1"] = nc.dram_tensor("bq1", [128, ND], F32, kind="ExternalInput")
    T["bk1"] = nc.dram_tensor("bk1", [128, NKT], F32, kind="ExternalInput")
    T["bv1"] = nc.dram_tensor("bv1", [1, KVD], F32, kind="ExternalInput")
    T["cosx"] = nc.dram_tensor("cosx", [64, TOK], F32, kind="ExternalInput")
    T["sinx"] = nc.dram_tensor("sinx", [64, TOK], F32, kind="ExternalInput")
    T["idn"] = nc.dram_tensor("idn", [128, 128], BF16, kind="ExternalInput")
    T["qT1"] = nc.dram_tensor("qT1", [D, TOK], BF16, kind="ExternalOutput")
    T["kT1"] = nc.dram_tensor("kT1", [KVD, TOK], BF16, kind="ExternalOutput")
    T["v1"] = nc.dram_tensor("v1", [TOK, KVD], BF16, kind="ExternalOutput")
    with ExitStack() as sst:
        emit_l1qkv(nc, _semf(nc, sst, "E"), T, fused=False)
    return nc


def emit_l1qkv(nc, sf, T, fused, pre=""):
    KVD = L1_KV * L1_HD
    NKT = KVD // 128
    x_i = T["x"]
    wq1, wk1, wv1 = T["wq1"], T["wk1"], T["wv1"]
    bq1, bk1, bv1 = T["bq1"], T["bk1"], T["bv1"]
    cosx, sinx, idn = T["cosx"], T["sinx"], T["idn"]
    qT_o = T["qT1"]
    if fused:
        k0, k1, v0, v1 = T["k10"], T["k11"], T["v10"], T["v11"]
        side_sb, flags_sb = T["side_sb"], T["flags_sb"]
        fl0, fl1, FLC = T["flE0"], T["flE1"], T["flcE"]
    else:
        kT_o, v_o = T["kT1"], T["v1"]
    NW = ND + NKT
    NCH = KVD // 512

    with ExitStack() as st:
        sb = lambda nm, sh, dt: st.enter_context(nc.sbuf_tensor(pre + nm, sh, dt))
        ps = lambda nm, sh, dt: st.enter_context(nc.psum_tensor(pre + nm, sh, dt))
        sem = sf
        xnT = sb("xnT", [128, ND * TOK], BF16)
        xnT_r = xnT.ap().rearrange("p (f t) -> p f t", t=TOK)
        x_t = [sb(f"x_t{i}", [128, D], F32) for i in range(2)]
        stat = sb("stat", [128, 8 * NT], F32)
        bnbuf = sb("bnbuf", [128, 48], F32)
        xn_tok = sb("xn_tok", [128, D], BF16)
        idn_sb = sb("idn_sb", [128, 128], BF16)
        bias_sb = sb("bias_sb", [128, NW], F32)
        cs_sb = sb("cs_sb", [128, 2 * TOK], F32)
        rtmp = sb("rtmp", [128, TOK], F32)
        swp = [sb(f"swp{i}", [128, TOK], BF16) for i in range(2)]
        wvbuf = [sb(f"wvbuf{i}", [128, ND * 512], BF16) for i in range(2)]
        wbuf = wvbuf
        stq = [sb(f"stq{i}", [128, TOK], BF16) for i in range(2)]
        stq2 = [sb(f"stq2{i}", [128, TOK], BF16) for i in range(2)]
        bvb = [sb(f"bvb{i}", [128, 512], BF16) for i in range(2)]
        vstage = [sb(f"vstage{i}", [128, 512], BF16) for i in range(8)]
        ps_ln = [ps(f"ps_ln{i}", [128, 1024], BF16) for i in range(2)]
        ps_a = ps("ps_a", [128, 512], F32)
        ps_b = ps("ps_b", [128, 512], F32)
        ps_v = [ps(f"ps_v{i}", [128, 512], F32) for i in range(4)]
        blk = st.enter_context(nc.Block())
        S = {n: sem(n) for n in
             ["cload", "vch", "ach", "xload0", "xload1", "var", "sqv", "xn",
              "tp", "tpc", "wload0", "wload1", "mm", "ev", "rope", "sw0", "sw1",
              "stq_s0", "stq_s1", "vwload0", "vwload1", "vmm", "vev", "flst"]
             + [f"vst{i}" for i in range(8)]}

        @blk.sync
        def _(s):
            s.dma_start(out=idn_sb[:, :], in_=idn[:, :]).then_inc(S["cload"], 16)
            s.dma_start(out=bias_sb[:, 0:ND], in_=bq1[:, :]).then_inc(S["cload"], 16)
            s.dma_start(out=bias_sb[:, ND:NW], in_=bk1[:, :]).then_inc(S["cload"], 16)
            s.dma_start(out=cs_sb[0:64, 0:TOK], in_=cosx[:, :]).then_inc(S["cload"], 16)
            s.dma_start(out=cs_sb[64:128, 0:TOK], in_=cosx[:, :]).then_inc(S["cload"], 16)
            s.dma_start(out=cs_sb[0:64, TOK:2 * TOK], in_=sinx[:, :]).then_inc(S["cload"], 16)
            s.dma_start(out=cs_sb[64:128, TOK:2 * TOK], in_=sinx[:, :]).then_inc(S["cload"], 16)
            for i in range(NT):
                if i >= 2:
                    s.wait_ge(S["xn"], i - 1)
                s.dma_start(out=x_t[i % 2][:, :], in_=x_i[i * 128:(i + 1) * 128, :]).then_inc(S[f"xload{i % 2}"], 16)
            sd_ctx = s.register(pre + "sd") if fused else None
            sd = sd_ctx.__enter__() if fused else None
            if fused:
                s.reg_load(sd, side_sb[0:1, 0:1])

            def qk_store(ot):
                s.wait_ge(S["rope"], ot + 1)
                if ot < ND:
                    s.dma_start(out=qT_o[ot * 128:(ot + 1) * 128, :],
                                in_=stq2[ot % 2][:, :]).then_inc(S[f"stq_s{ot % 2}"], 16)
                elif not fused:
                    s.dma_start(out=kT_o[(ot - ND) * 128:(ot - ND + 1) * 128, :],
                                in_=stq2[ot % 2][:, :]).then_inc(S[f"stq_s{ot % 2}"], 16)
                else:
                    kt = ot - ND
                    with s.If_eq(sd, 0):
                        s.dma_start(out=k0[kt * 128:(kt + 1) * 128, :],
                                    in_=stq2[ot % 2][:, :]).then_inc(S[f"stq_s{ot % 2}"], 16)
                    with s.Else():
                        s.dma_start(out=k1[kt * 128:(kt + 1) * 128, :],
                                    in_=stq2[ot % 2][:, :]).then_inc(S[f"stq_s{ot % 2}"], 16)
            for ot in range(NW):
                s.wait_ge(S["ev"], 2 * ot + 2)
                if ot >= 2:
                    s.wait_ge(S["rope"], ot - 1)
                s.dma_start(out=swp[ot % 2][0:64, :], in_=stq[ot % 2][64:128, :]).then_inc(S[f"sw{ot % 2}"], 16)
                s.dma_start(out=swp[ot % 2][64:128, :], in_=stq[ot % 2][0:64, :]).then_inc(S[f"sw{ot % 2}"], 16)
                if ot >= 1:
                    qk_store(ot - 1)
            qk_store(NW - 1)
            for chn in range(NCH):
                for tt in range(8):
                    s.wait_ge(S["vev"], chn * 8 + tt + 1)
                    if fused:
                        with s.If_eq(sd, 0):
                            s.dma_start(out=v0[tt * 128:(tt + 1) * 128, chn * 512:(chn + 1) * 512],
                                        in_=vstage[tt][:, :]).then_inc(S[f"vst{tt}"], 16)
                        with s.Else():
                            s.dma_start(out=v1[tt * 128:(tt + 1) * 128, chn * 512:(chn + 1) * 512],
                                        in_=vstage[tt][:, :]).then_inc(S[f"vst{tt}"], 16)
                    else:
                        s.dma_start(out=v_o[tt * 128:(tt + 1) * 128, chn * 512:(chn + 1) * 512],
                                    in_=vstage[tt][:, :]).then_inc(S[f"vst{tt}"], 16)
            for tt in range(8):
                s.wait_ge(S[f"vst{tt}"], 16 * NCH)
            for p in range(2):
                s.wait_ge(S[f"stq_s{p}"], 16 * (NW // 2))
            if fused:
                with s.If_eq(sd, 0):
                    s.dma_start(out=fl0[:, :], in_=flags_sb[0:1, FLC * 16:(FLC + 1) * 16]).then_inc(S["flst"], 16)
                with s.Else():
                    s.dma_start(out=fl1[:, :], in_=flags_sb[0:1, FLC * 16:(FLC + 1) * 16]).then_inc(S["flst"], 16)
                s.wait_ge(S["flst"], 16)
                sd_ctx.__exit__(None, None, None)

        @blk.gpsimd
        def _(g):
            for widx in range(NW):
                if widx >= 2:
                    g.wait_ge(S["ev"], 2 * widx - 2)
                wT = wq1 if widx < ND else wk1
                wout = D if widx < ND else KVD
                ot = widx if widx < ND else widx - ND
                for part in range(4):
                    srcw = bass.AP(wT, (part * 8 * 128) * wout + ot * 128,
                                   [[wout, 128], [128 * wout, 8], [1, 128]])
                    g.dma_start(out=wbuf[widx % 2][:, part * 1024:(part + 1) * 1024],
                                in_=srcw).then_inc(S[f"wload{widx % 2}"], 16)
            for chn in range(NCH):
                if chn < 2:
                    g.wait_ge(S["mm"], 2 * NW)
                if chn >= 2:
                    g.wait_ge(S["vev"], (chn - 1) * 8)
                g.dma_start(out=bvb[chn % 2][:, :],
                            in_=bass.AP(bv1, chn * 512, [[0, 128], [1, 512]])).then_inc(S[f"vwload{chn % 2}"], 16)
                for part in range(4):
                    srcw = bass.AP(wv1, (part * 8 * 128) * KVD + chn * 512,
                                   [[KVD, 128], [128 * KVD, 8], [1, 512]])
                    g.dma_start(out=wvbuf[chn % 2][:, part * 8 * 512:(part + 1) * 8 * 512],
                                in_=srcw).then_inc(S[f"vwload{chn % 2}"], 16)

        @blk.vector
        def _(v):
            ch = Ch(v, S["vch"])
            ch.wait(S["cload"], 16 * 7)
            xn_n = 0
            tpc_n = 0
            for i in range(NT):
                c = 8 * i
                ch.wait(S[f"xload{i % 2}"], 16 * (i // 2 + 1))
                for ck in range(8):
                    ch.op(v.bn_stats, bnbuf[:, ck * 6:(ck + 1) * 6],
                          x_t[i % 2][:, ck * 512:(ck + 1) * 512])
                ch.op(v.bn_aggr, stat[:, c:c + 2], bnbuf[:, :])
                ch.op_m(S["var"], i + 1, v.tensor_scalar, stat[:, c + 2:c + 3],
                        stat[:, c + 1:c + 2], LN_EPS, None, ALU.add)
                ch.wait(S["sqv"], i + 1)
                ch.op(v.reciprocal, stat[:, c + 7:c + 8], stat[:, c + 6:c + 7])
                if i >= 1:
                    ch.wait(S["tp"], 4 * i)
                xn_n += 1
                ch.op_m(S["xn"], xn_n, v.tensor_scalar, xn_tok[:, :], x_t[i % 2][:, :],
                        stat[:, c:c + 1], stat[:, c + 7:c + 8], ALU.subtract, ALU.mult)
                for gidx in range(4):
                    gi = i * 4 + gidx
                    ch.wait(S["tp"], gi + 1)
                    tpc_n += 1
                    ch.op_m(S["tpc"], tpc_n, v.tensor_copy,
                            xnT_r[:, gidx * 8:(gidx + 1) * 8, i * 128:(i + 1) * 128],
                            ps_ln[gi % 2][:, :])
            for ot in range(NW):
                ch.wait(S["ev"], 2 * ot + 2)
                ch.wait(S[f"sw{ot % 2}"], 32 * (ot // 2 + 1))
                if ot >= 2:
                    ch.wait(S[f"stq_s{ot % 2}"], 16 * (ot // 2))
                sg, sg2, sw = stq[ot % 2], stq2[ot % 2], swp[ot % 2]
                cosb, sinb = cs_sb[:, 0:TOK], cs_sb[:, TOK:2 * TOK]
                ch.op(v.tensor_tensor, rtmp[0:64, :], sg[0:64, :], cosb[0:64, :], ALU.mult)
                ch.op(v.tensor_tensor, sg2[0:64, :], sw[0:64, :], sinb[0:64, :], ALU.mult)
                ch.op(v.tensor_tensor, sg2[0:64, :], rtmp[0:64, :], sg2[0:64, :], ALU.subtract)
                ch.op(v.tensor_tensor, rtmp[64:128, :], sw[64:128, :], sinb[64:128, :], ALU.mult)
                ch.op(v.tensor_tensor, sg2[64:128, :], sg[64:128, :], cosb[64:128, :], ALU.mult)
                rp = ot + 1
                ch.op_m(S["rope"], rp, v.tensor_tensor, sg2[64:128, :],
                        rtmp[64:128, :], sg2[64:128, :], ALU.add)
            vev_n = 0
            for chn in range(NCH):
                for w in range(2):
                    wv_idx = chn * 2 + w
                    ch.wait(S["vmm"], wv_idx + 1)
                    for t4 in range(4):
                        tt = w * 4 + t4
                        if chn >= 1:
                            ch.wait(S[f"vst{tt}"], 16 * chn)
                        vev_n += 1
                        ch.op_m(S["vev"], vev_n, v.tensor_tensor, vstage[tt][:, :],
                                ps_v[t4][:, 0:512], bvb[chn % 2][:, :], ALU.add)

        @blk.scalar
        def _(a):
            ch = Ch(a, S["ach"])
            for i in range(NT):
                c = 8 * i
                ch.wait(S["var"], i + 1)
                ch.op_m(S["sqv"], i + 1, a.activation, stat[:, c + 6:c + 7],
                        stat[:, c + 2:c + 3], AF.Sqrt)
            ev_n = 0
            for ot in range(NW):
                for tc in range(2):
                    gidx = ot * 2 + tc
                    ch.wait(S["mm"], gidx + 1)
                    if ot >= 2 and tc == 0:
                        ch.wait(S["rope"], ot - 1)
                    ev_n += 1
                    ch.op_m(S["ev"], ev_n, a.activation,
                            stq[ot % 2][:, tc * 512:(tc + 1) * 512],
                            (ps_a if gidx % 2 == 0 else ps_b)[:, :],
                            AF.Identity, bias=bias_sb[:, ot:ot + 1])

        @blk.tensor
        def _(t):
            t.wait_ge(S["cload"], 16 * 7)
            for i in range(NT):
                t.wait_ge(S["xn"], i + 1)
                for gidx in range(4):
                    gi = i * 4 + gidx
                    if gi >= 2:
                        t.wait_ge(S["tpc"], gi - 1)
                    for j in range(8):
                        ins = t.transpose(ps_ln[gi % 2][:, j * 128:(j + 1) * 128],
                                          xn_tok[:, (gidx * 8 + j) * 128:(gidx * 8 + j + 1) * 128],
                                          idn_sb[:, :])
                        if j == 7:
                            ins.then_inc(S["tp"], 1)
            t.wait_ge(S["tpc"], 32)
            for widx in range(NW):
                t.wait_ge(S[f"wload{widx % 2}"], 64 * (widx // 2 + 1))
                for tc in range(2):
                    gidx = widx * 2 + tc
                    if gidx >= 2:
                        t.wait_ge(S["ev"], gidx - 1)
                    for in_ in range(ND):
                        ins = t.matmul((ps_a if gidx % 2 == 0 else ps_b)[:, :],
                                       wbuf[widx % 2][:, in_ * 128:(in_ + 1) * 128],
                                       xnT[:, in_ * TOK + tc * 512:in_ * TOK + (tc + 1) * 512],
                                       start=(in_ == 0), stop=(in_ == ND - 1))
                        if in_ == ND - 1:
                            ins.then_inc(S["mm"], 1)
            t.wait_ge(S["ev"], 2 * NW)
            for chn in range(NCH):
                t.wait_ge(S[f"vwload{chn % 2}"], 80 * (chn // 2 + 1))
                for w in range(2):
                    wv_idx = chn * 2 + w
                    if wv_idx >= 1:
                        t.wait_ge(S["vev"], wv_idx * 4)
                    for in_ in range(ND):
                        for t4 in range(4):
                            tt = w * 4 + t4
                            ins = t.matmul(ps_v[t4][:, 0:512],
                                           xnT[:, in_ * TOK + tt * 128:in_ * TOK + (tt + 1) * 128],
                                           wvbuf[chn % 2][:, in_ * 512:(in_ + 1) * 512],
                                           start=(in_ == 0), stop=(in_ == ND - 1))
                            if in_ == ND - 1 and t4 == 3:
                                ins.then_inc(S["vmm"], 1)


def _prep_host(inputs):
    d = {}
    x = np.asarray(inputs["L0_attn_x"], np.float32).reshape(NCORES * TOK, D)
    d["x_sh"] = [np.ascontiguousarray(x[c * TOK:(c + 1) * TOK]) for c in range(NCORES)]
    d["wqT"] = np.ascontiguousarray(np.asarray(inputs["L0_attn_Wq"], np.float32).T)
    d["wkT"] = np.ascontiguousarray(np.asarray(inputs["L0_attn_Wk"], np.float32).T)
    d["wvT"] = np.ascontiguousarray(np.asarray(inputs["L0_attn_Wv"], np.float32).T)
    d["bq"] = np.ascontiguousarray(np.asarray(inputs["L0_attn_bq"], np.float32).reshape(ND, 128).T)
    d["bk"] = np.ascontiguousarray(np.asarray(inputs["L0_attn_bk"], np.float32).reshape(ND, 128).T)
    d["bv"] = np.ascontiguousarray(np.asarray(inputs["L0_attn_bv"], np.float32).reshape(1, D))
    d["bwf"] = np.ascontiguousarray(np.asarray(inputs["L0_attn_bias_weight"], np.float32).reshape(1, 48))
    d["kdkg"] = _geo_consts()
    d["idn"] = np.eye(128, dtype=BF)
    d["idnf"] = np.eye(128, dtype=np.float32)
    codesT = np.ascontiguousarray(np.asarray(inputs["L0_ffn_w_codes"], np.int32).T)
    d["codes_sh"] = [np.ascontiguousarray(codesT[c * 512:(c + 1) * 512]) for c in range(NCORES)]
    amxT = np.ascontiguousarray(np.asarray(inputs["L0_ffn_absmax"], np.float32).T)
    d["amx_sh"] = [np.ascontiguousarray(amxT[c * 8:(c + 1) * 8]) for c in range(NCORES)]
    wa = np.asarray(inputs["L0_ffn_W_a"], np.float32)
    d["waT"] = np.ascontiguousarray(wa.reshape(ND, 128, 16).transpose(1, 0, 2).reshape(128, 512))
    d["wbT"] = np.ascontiguousarray(np.asarray(inputs["L0_ffn_W_b"], np.float32))
    d["wq1"] = np.ascontiguousarray(np.asarray(inputs["L1_attn_wq"], np.float32))
    d["wk1"] = np.ascontiguousarray(np.asarray(inputs["L1_attn_wk"], np.float32))
    d["wv1"] = np.ascontiguousarray(np.asarray(inputs["L1_attn_wv"], np.float32))
    d["bq1"] = np.ascontiguousarray(np.asarray(inputs["L1_attn_bq"], np.float32).reshape(ND, 128).T)
    d["bk1"] = np.ascontiguousarray(np.asarray(inputs["L1_attn_bk"], np.float32).reshape(8, 128).T)
    d["bv1"] = np.ascontiguousarray(np.asarray(inputs["L1_attn_bv"], np.float32).reshape(1, 1024))
    cosT, sinT = _rope_tables()
    d["cos_sh"] = [np.ascontiguousarray(cosT[:, (c % 2) * TOK:(c % 2 + 1) * TOK]) for c in range(NCORES)]
    d["sin_sh"] = [np.ascontiguousarray(sinT[:, (c % 2) * TOK:(c % 2 + 1) * TOK]) for c in range(NCORES)]
    d["w1T"] = np.ascontiguousarray(np.asarray(inputs["L1_ffn_weight"], np.float16).T)
    d["b1row"] = np.ascontiguousarray(np.asarray(inputs["L1_ffn_bias"], np.float16).reshape(1, D))
    return d


def _pair_concat(arrs, ncores, axis):
    out = []
    for c in range(ncores):
        b = c // 2
        out.append(np.concatenate([arrs[2 * b], arrs[2 * b + 1]], axis=axis))
    return out


def _run_phases(d, runner, ncores=NCORES):
    idn = d["idn"]
    resA = runner(build_phaseA(), [
        {"x": d["x_sh"][c], "wqT": d["wqT"], "wkT": d["wkT"], "wvT": d["wvT"],
         "bq": d["bq"], "bk": d["bk"], "bv": d["bv"], "bwf": d["bwf"],
         "kdkg": d["kdkg"], "idn": idn} for c in range(ncores)])
    kT_f = _pair_concat([resA[c]["kT"] for c in range(ncores)], ncores, 1)
    v_f = _pair_concat([resA[c]["v"] for c in range(ncores)], ncores, 0)
    resB = runner(build_attn(0), [
        {"qT": resA[c]["qT"], "kT": kT_f[c], "v": v_f[c],
         "x": d["x_sh"][c], "idn": idn} for c in range(ncores)])
    resC = runner(build_dequant(), [
        {"codes": d["codes_sh"][c], "amx": d["amx_sh"][c]} for c in range(ncores)])
    wTfull = np.concatenate([resC[c]["wsh"] for c in range(ncores)], axis=0)
    if ncores < NCORES:
        pass
    resD = runner(build_ffn(0), [
        {"x": resB[c]["x1"], "wT": wTfull, "waT": d["waT"], "wbT": d["wbT"],
         "idn": idn} for c in range(ncores)])
    resE = runner(build_l1qkv(), [
        {"x": resD[c]["xo"], "wq1": d["wq1"], "wk1": d["wk1"], "wv1": d["wv1"],
         "bq1": d["bq1"], "bk1": d["bk1"], "bv1": d["bv1"],
         "cosx": d["cos_sh"][c], "sinx": d["sin_sh"][c], "idn": idn}
        for c in range(ncores)])
    kT1_f = _pair_concat([resE[c]["kT1"] for c in range(ncores)], ncores, 1)
    v1_f = _pair_concat([resE[c]["v1"] for c in range(ncores)], ncores, 0)
    resF = runner(build_attn(1), [
        {"qT": resE[c]["qT1"], "kT": kT1_f[c], "v": v1_f[c],
         "x": resD[c]["xo"], "idn": idn} for c in range(ncores)])
    resG = runner(build_ffn(1), [
        {"x": resF[c]["x1"], "wT": d["w1T"], "brow": d["b1row"], "idn": idn}
        for c in range(ncores)])
    return [resG[c]["xo"] for c in range(ncores)]


def _device_runner(nc, in_maps):
    res = run_bass_kernel_spmd(nc, in_maps, core_ids=list(range(len(in_maps))))
    return res.results




def build_fused(upto=7, detect_races=True):
    import os
    _mask = os.environ.get("FUSE_PHASES")
    _sel = set(int(x) for x in _mask.split(",")) if _mask else None
    def _on(i):
        return (i in _sel) if _sel is not None else True
    nc = bass.Bass("TRN2", target_bir_lowering=False,
                   detect_race_conditions=detect_races)
    KVD = L1_KV * L1_HD
    t = {}
    EI = lambda n, sh, dt: nc.dram_tensor(n, sh, dt, kind="ExternalInput")
    t["x"] = EI("x", [TOK, D], F32)
    t["wqT"] = EI("wqT", [D, D], F32)
    t["wkT"] = EI("wkT", [D, D], F32)
    t["wvT"] = EI("wvT", [D, D], F32)
    t["bq"] = EI("bq", [128, ND], F32)
    t["bk"] = EI("bk", [128, ND], F32)
    t["bv"] = EI("bv", [1, D], F32)
    t["bwf"] = EI("bwf", [1, 48], F32)
    t["kdkg"] = EI("kdkg", [512, 128], F32)
    t["idn"] = EI("idn", [128, 128], BF16)
    t["idnf"] = EI("idnf", [128, 128], F32)
    t["codes"] = EI("codes", [2048, D], I32)
    t["amx"] = EI("amx", [32, D], F32)
    t["waT"] = EI("waT", [128, 512], F32)
    t["wbT"] = EI("wbT", [16, D], F32)
    t["wq1"] = EI("wq1", [D, D], F32)
    t["wk1"] = EI("wk1", [D, KVD], F32)
    t["wv1"] = EI("wv1", [D, KVD], F32)
    t["bq1"] = EI("bq1", [128, ND], F32)
    t["bk1"] = EI("bk1", [128, 8], F32)
    t["bv1"] = EI("bv1", [1, KVD], F32)
    t["cosx"] = EI("cosx", [64, TOK], F32)
    t["sinx"] = EI("sinx", [64, TOK], F32)
    t["w1T"] = EI("w1T", [D, D], F16)
    t["b1row"] = EI("b1row", [1, D], F16)
    t["side"] = EI("side", [1, 1], I32)
    t["flags"] = EI("flags", [1, 64], I32)
    t["out"] = nc.dram_tensor("out", [TOK, D], F32, kind="ExternalOutput")
    t["qT"] = nc.dram_tensor("qT", [D, TOK], BF16)
    t["x1"] = nc.dram_tensor("x1", [TOK, D], F32)
    t["x2"] = nc.dram_tensor("x2", [TOK, D], F32)
    t["x3"] = nc.dram_tensor("x3", [TOK, D], F32)
    t["qT1"] = nc.dram_tensor("qT1", [D, TOK], BF16)
    SH = lambda n, sh, dt: nc.dram_tensor(n, sh, dt, addr_space="Shared")
    t["k0"] = SH("k0", [D, TOK], BF16)
    t["k1"] = SH("k1", [D, TOK], BF16)
    t["v0"] = SH("v0", [TOK, D], BF16)
    t["v1"] = SH("v1", [TOK, D], BF16)
    t["wT_sh"] = SH("wT_sh", [D, D], BF16)
    t["k10"] = SH("k10", [KVD, TOK], BF16)
    t["k11"] = SH("k11", [KVD, TOK], BF16)
    t["v10"] = SH("v10", [TOK, KVD], BF16)
    t["v11"] = SH("v11", [TOK, KVD], BF16)
    for n in ["flA0", "flA1", "flW0", "flW1", "flE0", "flE1"]:
        t[n] = SH(n, [1, 16], I32)

    with ExitStack() as sst:
        side_sb = sst.enter_context(nc.sbuf_tensor("side_sb", [1, 1], I32))
        flags_sb = sst.enter_context(nc.sbuf_tensor("flags_sb", [1, 64], I32))
        poll_sb = sst.enter_context(nc.sbuf_tensor("poll_sb", [1, 16], I32))
        isem = sst.enter_context(nc.semaphore("init_sem"))
        pool = {}
        uniq = bool(os.environ.get("FUSE_DBG_UNIQSEM"))
        phase_tag = [""]
        SWDGE_NAMES = {"wload0", "wload1", "vwload0", "vwload1", "klo0", "klo1",
                       "psem", "cload2", "qlo0", "qlo1", "dcl0", "dcl1",
                       "dst0", "dst1", "dflst"}

        def sfp(name):
            key = (phase_tag[0] + name) if (uniq or name in SWDGE_NAMES) else name
            if key not in pool:
                pool[key] = sst.enter_context(nc.semaphore("P_" + key))
            return pool[key]

        def clear_sems():
            if uniq:
                return
            with nc.Block() as bc:
                @bc.sync
                def _(s):
                    for k, h in pool.items():
                        if not any(k.endswith(n) and len(k) == len(n) + 1 or k == n
                                   for n in SWDGE_NAMES):
                            s.sem_clear(h)
        with nc.Block() as b0:
            @b0.sync
            def _(s):
                s.dma_start(out=side_sb[:, :], in_=t["side"][:, :]).then_inc(isem, 16)
                s.dma_start(out=flags_sb[:, :], in_=t["flags"][:, :]).then_inc(isem, 16)
                s.wait_ge(isem, 32)
        com = {"side_sb": side_sb, "flags_sb": flags_sb, "poll_sb": poll_sb}
        TA = {**com, "x": t["x"], "wqT": t["wqT"], "wkT": t["wkT"], "wvT": t["wvT"],
              "bq": t["bq"], "bk": t["bk"], "bv": t["bv"], "bwf": t["bwf"],
              "kdkg": t["kdkg"], "idn": t["idn"], "qT": t["qT"],
              "k0": t["k0"], "k1": t["k1"], "v0": t["v0"], "v1": t["v1"],
              "flA0": t["flA0"], "flA1": t["flA1"], "flcA": 0}
        TCA = {"codes": t["codes"], "amx": t["amx"], "wT_sh": t["wT_sh"]}
        if upto >= 1 and _on(1):
            phase_tag[0] = "A"
            emit_phaseA(nc, sfp, TA, fused=True, pre="A_", deq=TCA)
        TB = {**com, "qT": t["qT"], "x": t["x"], "idn": t["idn"], "idnf": t["idnf"], "x1": t["x1"],
              "k0": t["k0"], "k1": t["k1"], "v0": t["v0"], "v1": t["v1"],
              "fl0": t["flA0"], "fl1": t["flA1"], "flc": 0}
        TC = {"codes": t["codes"], "amx": t["amx"], "wT_sh": t["wT_sh"],
              "flW0": t["flW0"], "flW1": t["flW1"], "flcW": 1}
        if upto >= 2 and _on(2):
            phase_tag[0] = "B"
            clear_sems()
            emit_attn(nc, sfp, TB, 0, fused=True, pre="B_", deq={**TC, "t0": 8})
        import os
        if os.environ.get("FUSE_DBG_LOCALW"):
            t["wtest"] = nc.dram_tensor("wtest", [D, D], BF16)
            _wD = t["wtest"]
        else:
            _wD = t["wT_sh"]
        TD = {**com, "x": t["x1"], "wT": _wD, "waT": t["waT"], "wbT": t["wbT"],
              "idn": t["idn"], "xo": t["x2"],
              "flW0": t["flW0"], "flW1": t["flW1"], "flcW": 1}
        if upto >= 4 and _on(4):
            phase_tag[0] = "D"
            clear_sems()
        if upto >= 4 and _on(4):
            emit_ffn(nc, sfp, TD, 0, fused=True, pre="D_")
        TE = {**com, "x": t["x2"], "wq1": t["wq1"], "wk1": t["wk1"], "wv1": t["wv1"],
              "bq1": t["bq1"], "bk1": t["bk1"], "bv1": t["bv1"],
              "cosx": t["cosx"], "sinx": t["sinx"], "idn": t["idn"],
              "qT1": t["qT1"], "k10": t["k10"], "k11": t["k11"],
              "v10": t["v10"], "v11": t["v11"],
              "flE0": t["flE0"], "flE1": t["flE1"], "flcE": 2}
        if upto >= 5 and _on(5):
            phase_tag[0] = "E"
            clear_sems()
        if upto >= 5 and _on(5):
            emit_l1qkv(nc, sfp, TE, fused=True, pre="E_")
        TF = {**com, "qT": t["qT1"], "x": t["x2"], "idn": t["idn"], "idnf": t["idnf"], "x1": t["x3"],
              "k0": t["k10"], "k1": t["k11"], "v0": t["v10"], "v1": t["v11"],
              "fl0": t["flE0"], "fl1": t["flE1"], "flc": 2}
        if upto >= 6 and _on(6):
            phase_tag[0] = "F"
            clear_sems()
        if upto >= 6 and _on(6):
            emit_attn(nc, sfp, TF, 1, fused=True, pre="F_")
        TG = {"x": t["x3"], "wT": t["w1T"], "brow": t["b1row"],
              "idn": t["idn"], "xo": t["out"]}
        if upto >= 7 and _on(7):
            phase_tag[0] = "G"
            clear_sems()
        if upto >= 7 and _on(7):
            emit_ffn(nc, sfp, TG, 1, fused=False, pre="G_")
    return nc


def _fused_in_maps(d):
    rng = np.random.default_rng()
    nonce = int(rng.integers(1, 2 ** 30))
    flags = np.zeros((1, 64), np.int32)
    for k in range(4):
        flags[0, k * 16:(k + 1) * 16] = nonce + k + 1
    maps = []
    for c in range(NCORES):
        s = c % 2
        maps.append({
            "x": d["x_sh"][c], "wqT": d["wqT"], "wkT": d["wkT"], "wvT": d["wvT"],
            "bq": d["bq"], "bk": d["bk"], "bv": d["bv"], "bwf": d["bwf"],
            "kdkg": d["kdkg"], "idn": d["idn"],
            "codes": d["codes_pair"][s], "amx": d["amx_pair"][s],
            "waT": d["waT"], "wbT": d["wbT"],
            "wq1": d["wq1"], "wk1": d["wk1"], "wv1": d["wv1"],
            "bq1": d["bq1"], "bk1": d["bk1"], "bv1": d["bv1"],
            "cosx": d["cos_sh"][c], "sinx": d["sin_sh"][c],
            "w1T": d["w1T"], "b1row": d["b1row"], "idnf": d["idnf"],
            "side": np.array([[s]], np.int32), "flags": flags,
        })
    return maps


def kernel(**inputs):
    d = _prep_host(inputs)
    codesT = np.concatenate(d["codes_sh"], axis=0)
    amxT = np.concatenate(d["amx_sh"], axis=0)
    d["codes_pair"] = [np.ascontiguousarray(codesT[s * 2048:(s + 1) * 2048]) for s in range(2)]
    d["amx_pair"] = [np.ascontiguousarray(amxT[s * 32:(s + 1) * 32]) for s in range(2)]
    nc = build_fused()
    res = run_bass_kernel_spmd(nc, _fused_in_maps(d), core_ids=list(range(NCORES)))
    out = np.concatenate([np.asarray(res.results[c]["out"], np.float32) for c in range(NCORES)], axis=0)
    return out.reshape(4, SEQ, D)


def kernel_unfused(**inputs):
    d = _prep_host(inputs)
    shards = _run_phases(d, _device_runner)
    out = np.concatenate([np.asarray(s, np.float32) for s in shards], axis=0)
    return out.reshape(4, SEQ, D)



# version 14
# speedup vs baseline: 1.1403x; 1.1403x over previous
import numpy as np
import ml_dtypes
import concourse.bass as bass
import concourse.mybir as mybir
from concourse.bass_utils import run_bass_kernel_spmd
from contextlib import ExitStack

F32 = mybir.dt.float32
BF16 = mybir.dt.bfloat16
F16 = mybir.dt.float16
I32 = mybir.dt.int32
AF = mybir.ActivationFunctionType
ALU = mybir.AluOpType
AX = mybir.AxisListType

NCORES = 8
TOK = 1024
D = 4096
NT = TOK // 128
ND = D // 128
SEQ = 2048
NK = SEQ // 128

L0_NH = 16
L0_HD = 256
L0_SCALE = 1.0 / 16.0
L1_H = 32
L1_KV = 8
L1_HD = 128

NF4 = [-1.0, -0.6961928009986877, -0.5250730514526367, -0.39491748809814453,
       -0.28444138169288635, -0.18477343022823334, -0.09105003625154495, 0.0,
       0.07958029955625534, 0.16093020141124725, 0.24611230194568634,
       0.33791524171829224, 0.44070982933044434, 0.5626170039176941,
       0.7229568362236023, 1.0]
FFN_SCALE = 2.0
LN_EPS = 1e-5
BF = ml_dtypes.bfloat16


def _geo_consts():
    g = [1.0, 1.0, 1.0]
    dim, n = 3, 8
    masks = sorted(range(n), key=lambda m: (bin(m).count("1"), m))
    inv = {m: i for i, m in enumerate(masks)}
    C = np.zeros((n, n, n), dtype=np.float64)
    for i, a in enumerate(masks):
        for j, b in enumerate(masks):
            sw, at = 0, a >> 1
            while at:
                sw += bin(at & b).count("1")
                at >>= 1
            s = -1.0 if sw & 1 else 1.0
            for t in range(dim):
                if a >> t & 1 and b >> t & 1:
                    s *= g[t]
            C[i, j, inv[a ^ b]] = s
    grades = np.array([bin(m).count("1") for m in masks])
    rev = np.where(grades * (grades - 1) // 2 % 2 == 1, -1.0, 1.0)
    met = np.array([np.prod([g[t] for t in range(dim) if m >> t & 1] + [1.0])
                    for m in masks])
    bivk = C[:, :, [4, 5, 6]]
    KD = np.kron(np.eye(16), L0_SCALE * np.diag(rev * met)).astype(np.float32)
    KG = [np.kron(np.eye(16), np.diag(rev) @ bivk[:, :, c]).astype(np.float32)
          for c in range(3)]
    return np.concatenate([KD] + KG, axis=0)


def _rope_tables():
    hd = L1_HD
    inv_freq = 1.0 / (10000.0 ** (np.arange(0, hd, 2, dtype=np.float64) / hd))
    ang = np.arange(SEQ, dtype=np.float64)[:, None] * inv_freq[None, :]
    return (np.ascontiguousarray(np.cos(ang).T).astype(np.float32),
            np.ascontiguousarray(np.sin(ang).T).astype(np.float32))




class Ch:

    def __init__(self, eng, sem):
        self.e, self.sem, self.n, self.last = eng, sem, 0, None

    def wait(self, sem, val):
        self.e.wait_ge(sem, val)

    def op(self, f, *a, **kw):
        if self.last is not None:
            self.e.wait_ge(*self.last)
        ins = f(*a, **kw)
        self.n += 1
        ins.then_inc(self.sem, 1)
        self.last = (self.sem, self.n)
        return ins

    def op_m(self, mile_sem, mile_val, f, *a, **kw):
        if self.last is not None:
            self.e.wait_ge(*self.last)
        ins = f(*a, **kw)
        ins.then_inc(mile_sem, 1)
        self.last = (mile_sem, mile_val)
        return ins


def _semf(nc, sst, prefix):
    def sf(name):
        return sst.enter_context(nc.semaphore(prefix + name))
    return sf


def build_phaseA():
    nc = bass.Bass("TRN2", target_bir_lowering=False)
    T = {}
    T["x"] = nc.dram_tensor("x", [TOK, D], F32, kind="ExternalInput")
    T["wqT"] = nc.dram_tensor("wqT", [D, D], F32, kind="ExternalInput")
    T["wkT"] = nc.dram_tensor("wkT", [D, D], F32, kind="ExternalInput")
    T["wvT"] = nc.dram_tensor("wvT", [D, D], F32, kind="ExternalInput")
    T["bq"] = nc.dram_tensor("bq", [128, ND], F32, kind="ExternalInput")
    T["bk"] = nc.dram_tensor("bk", [128, ND], F32, kind="ExternalInput")
    T["bv"] = nc.dram_tensor("bv", [1, D], F32, kind="ExternalInput")
    T["bwf"] = nc.dram_tensor("bwf", [1, 48], F32, kind="ExternalInput")
    T["kdkg"] = nc.dram_tensor("kdkg", [512, 128], F32, kind="ExternalInput")
    T["idn"] = nc.dram_tensor("idn", [128, 128], BF16, kind="ExternalInput")
    T["qT"] = nc.dram_tensor("qT", [D, TOK], BF16, kind="ExternalOutput")
    T["kT"] = nc.dram_tensor("kT", [D, TOK], BF16, kind="ExternalOutput")
    T["v"] = nc.dram_tensor("v", [TOK, D], BF16, kind="ExternalOutput")
    with ExitStack() as sst:
        emit_phaseA(nc, _semf(nc, sst, "A"), T, fused=False)
    return nc


def emit_phaseA(nc, sf, T, fused, pre="", deq=None):
    x_in, wqT, wkT, wvT = T["x"], T["wqT"], T["wkT"], T["wvT"]
    bq, bk, bv, bwf, kdkg, idn = T["bq"], T["bk"], T["bv"], T["bwf"], T["kdkg"], T["idn"]
    qT_o = T["qT"]
    if fused:
        k0, k1, v0, v1 = T["k0"], T["k1"], T["v0"], T["v1"]
        side_sb, flags_sb, fl0, fl1 = T["side_sb"], T["flags_sb"], T["flA0"], T["flA1"]
        FLC = T["flcA"]
    else:
        kT_o, v_o = T["kT"], T["v"]

    with ExitStack() as st:
        sb = lambda nm, sh, dt: st.enter_context(nc.sbuf_tensor(pre + nm, sh, dt))
        ps = lambda nm, sh, dt: st.enter_context(nc.psum_tensor(pre + nm, sh, dt))
        sem = sf

        xnT = sb("xnT", [128, ND * TOK], BF16)
        xnT_r = xnT.ap().rearrange("p (f t) -> p f t", t=TOK)
        x_t = [sb(f"x_t{i}", [128, D], F32) for i in range(2)]
        if deq is not None:
            xa0 = nc.lookup_mloc(pre + "x_t0").addr
            xa1 = nc.lookup_mloc(pre + "x_t1").addr
            dq_cbuf = [nc.alloc_sbuf_tensor_at(pre + f"dq_cbuf{i}", [128, 2048], I32,
                                               offset=xa0 + i * 8192) for i in range(2)]
            dq_mbuf = [nc.alloc_sbuf_tensor_at(pre + f"dq_mbuf{i}", [128, 2048], F32,
                                               offset=xa1 + i * 8192) for i in range(2)]
            dq_acc = [sb(f"dq_acc{i}", [128, 2048], BF16) for i in range(2)]
            dq_codes, dq_amx, dq_out = deq["codes"], deq["amx"], deq["wT_sh"]
        stat = sb("stat", [128, 8 * NT], F32)
        bnbuf = sb("bnbuf", [128, 48], F32)
        xn_tok = sb("xn_tok", [128, D], BF16)
        if deq is not None:
            xn_addr = nc.lookup_mloc(pre + "xn_tok").addr
            dq_cf = nc.alloc_sbuf_tensor_at(pre + "dq_cf", [128, 2048], BF16,
                                            offset=xn_addr)
            dq_selt = nc.alloc_sbuf_tensor_at(pre + "dq_selt", [128, 2048], BF16,
                                              offset=xn_addr + 4096)
        idn_sb = sb("idn_sb", [128, 128], BF16)
        bias_sb = sb("bias_sb", [128, 2 * ND], F32)
        bvb = [sb(f"bvb{i}", [128, 512], BF16) for i in range(2)]
        bw_sb = sb("bw_sb", [128, 48], F32)
        if deq is None:
            kdkg_sb = sb("kdkg_sb", [128, 512], F32)
            btmp = sb("btmp", [128, 256], F32)
        bmat = sb("bmat", [128, L0_NH * 128], BF16)
        wvbuf = [sb(f"wvbuf{i}", [128, ND * 512], BF16) for i in range(2)]
        wbuf = [wvbuf[i][:, 0:D] if False else wvbuf[i] for i in range(2)]
        stq = [sb(f"stq{i}", [128, TOK], BF16) for i in range(2)]
        stq2 = [sb(f"stq2{i}", [128, TOK], BF16) for i in range(2)]
        stk = [sb(f"stk{i}", [128, TOK], BF16) for i in range(2)]
        vstage = [sb(f"vstage{i}", [128, 512], BF16) for i in range(8)]
        if deq is not None:
            btmp = nc.alloc_sbuf_tensor_at(pre + "btmp", [128, 256], F32,
                                           offset=nc.lookup_mloc(pre + "vstage0").addr)
            kdkg_sb = nc.alloc_sbuf_tensor_at(pre + "kdkg_sb", [128, 512], F32,
                                              offset=nc.lookup_mloc(pre + "vstage2").addr)

        ps_ln = [ps(f"ps_ln{i}", [128, 1024], BF16) for i in range(2)]
        ps_a = ps("ps_a", [128, 512], F32)
        ps_b = ps("ps_b", [128, 512], F32)
        ps_v = [ps(f"ps_v{i}", [128, 512], F32) for i in range(4)]

        blk = st.enter_context(nc.Block())
        S = {n: sem(n) for n in
             ["cload", "vch", "ach", "xload0", "xload1", "var", "sqv", "xn",
              "tp", "tpc", "wload0", "wload1", "mm", "ev", "foldmm", "fev",
              "stq_s0", "stq_s1", "stk_s0", "stk_s1", "vwload0", "vwload1",
              "vmm", "vev", "flst"]
             + [f"vst{i}" for i in range(8)]}
        if deq is not None:
            for n in ["dwd", "qdcl0", "qdcl1", "qdst0", "qdst1", "dcl0", "dcl1"]:
                S[n] = sem(n)

        def dq_srcs(j):
            r, hf = j // 2, j % 2
            return (dq_codes[r * 128:(r + 1) * 128, hf * 2048:(hf + 1) * 2048],
                    bass.AP(dq_amx, (r * 2) * D + hf * 2048,
                            [[D, 2], [0, 64], [1, 2048]]))

        @blk.sync
        def _(s):
            sd_ctx = s.register(pre + "sd") if fused else None
            sd = sd_ctx.__enter__() if fused else None
            if fused:
                s.reg_load(sd, side_sb[0:1, 0:1])

            def dq_store(j):
                r, hf = j // 2, j % 2
                cs = slice(hf * 2048, (hf + 1) * 2048)
                with s.If_eq(sd, 0):
                    s.dma_start(out=dq_out[r * 128:(r + 1) * 128, cs],
                                in_=dq_acc[j % 2][:, :]).then_inc(S[f"qdst{j % 2}"], 16)
                with s.Else():
                    s.dma_start(out=dq_out[2048 + r * 128:2048 + (r + 1) * 128, cs],
                                in_=dq_acc[j % 2][:, :]).then_inc(S[f"qdst{j % 2}"], 16)

            s.dma_start(out=idn_sb[:, :], in_=idn[:, :]).then_inc(S["cload"], 16)
            s.dma_start(out=bw_sb[:, :], in_=bass.AP(bwf, 0, [[0, 128], [1, 48]])).then_inc(S["cload"], 16)
            s.dma_start(out=kdkg_sb[:, :], in_=bass.AP(kdkg, 0, [[128, 128], [128 * 128, 4], [1, 128]])).then_inc(S["cload"], 16)
            s.dma_start(out=bias_sb[:, 0:ND], in_=bq[:, :]).then_inc(S["cload"], 16)
            s.dma_start(out=bias_sb[:, ND:2 * ND], in_=bk[:, :]).then_inc(S["cload"], 16)
            for i in range(NT):
                if i >= 2:
                    s.wait_ge(S["xn"], i - 1)
                s.dma_start(out=x_t[i % 2][:, :], in_=x_in[i * 128:(i + 1) * 128, :]).then_inc(S[f"xload{i % 2}"], 16)
            if deq is not None:
                s.wait_ge(S["xn"], NT)
                for j in range(2):
                    csrc, msrc = dq_srcs(j)
                    s.dma_start(out=dq_cbuf[j][:, :], in_=csrc).then_inc(S[f"qdcl{j}"], 16)
                    s.dma_start(out=dq_mbuf[j][:, :], in_=msrc).then_inc(S[f"qdcl{j}"], 16)
                for j in range(2, 16):
                    s.wait_ge(S["dwd"], j - 1)
                    csrc, msrc = dq_srcs(j)
                    s.dma_start(out=dq_cbuf[j % 2][:, :],
                                in_=csrc).then_inc(S[f"dcl{j % 2}"], 16)
                    s.dma_start(out=dq_mbuf[j % 2][:, :],
                                in_=msrc).then_inc(S[f"dcl{j % 2}"], 16)
                    dq_store(j - 2)
                s.wait_ge(S["dwd"], 15)
                dq_store(14)
                s.wait_ge(S["dwd"], 16)
                dq_store(15)
            for ot in range(ND):
                s.wait_ge(S["fev"], 2 * ot + 2)
                s.dma_start(out=qT_o[ot * 128:(ot + 1) * 128, :], in_=stq2[ot % 2][:, :]).then_inc(S[f"stq_s{ot % 2}"], 16)
            def kdst(ot):
                return slice(ot * 128, (ot + 1) * 128)
            for ot in range(ND):
                s.wait_ge(S["ev"], 2 * ND + 2 * ot + 2)
                if fused:
                    with s.If_eq(sd, 0):
                        s.dma_start(out=k0[kdst(ot), :], in_=stk[ot % 2][:, :]).then_inc(S[f"stk_s{ot % 2}"], 16)
                    with s.Else():
                        s.dma_start(out=k1[kdst(ot), :], in_=stk[ot % 2][:, :]).then_inc(S[f"stk_s{ot % 2}"], 16)
                else:
                    s.dma_start(out=kT_o[kdst(ot), :], in_=stk[ot % 2][:, :]).then_inc(S[f"stk_s{ot % 2}"], 16)
            for ch in range(8):
                for tt in range(8):
                    s.wait_ge(S["vev"], ch * 8 + tt + 1)
                    if fused:
                        with s.If_eq(sd, 0):
                            s.dma_start(out=v0[tt * 128:(tt + 1) * 128, ch * 512:(ch + 1) * 512],
                                        in_=vstage[tt][:, :]).then_inc(S[f"vst{tt}"], 16)
                        with s.Else():
                            s.dma_start(out=v1[tt * 128:(tt + 1) * 128, ch * 512:(ch + 1) * 512],
                                        in_=vstage[tt][:, :]).then_inc(S[f"vst{tt}"], 16)
                    else:
                        s.dma_start(out=v_o[tt * 128:(tt + 1) * 128, ch * 512:(ch + 1) * 512],
                                    in_=vstage[tt][:, :]).then_inc(S[f"vst{tt}"], 16)
            if deq is not None:
                for p in range(2):
                    s.wait_ge(S[f"qdst{p}"], 16 * 8)
            for tt in range(8):
                s.wait_ge(S[f"vst{tt}"], 16 * 8)
            for p in range(2):
                s.wait_ge(S[f"stq_s{p}"], 16 * 16)
                s.wait_ge(S[f"stk_s{p}"], 16 * 16)
            if fused:
                with s.If_eq(sd, 0):
                    s.dma_start(out=fl0[:, :], in_=flags_sb[0:1, FLC * 16:(FLC + 1) * 16]).then_inc(S["flst"], 16)
                with s.Else():
                    s.dma_start(out=fl1[:, :], in_=flags_sb[0:1, FLC * 16:(FLC + 1) * 16]).then_inc(S["flst"], 16)
                s.wait_ge(S["flst"], 16)
                sd_ctx.__exit__(None, None, None)

        @blk.gpsimd
        def _(g):
            for widx in range(2 * ND):
                wT = wqT if widx < ND else wkT
                ot = widx % ND
                if widx >= 2:
                    g.wait_ge(S["ev"], 2 * widx - 2)
                for part in range(4):
                    src = bass.AP(wT, (part * 8 * 128) * D + ot * 128,
                                  [[D, 128], [128 * D, 8], [1, 128]])
                    g.dma_start(out=wbuf[widx % 2][:, part * 1024:(part + 1) * 1024],
                                in_=src).then_inc(S[f"wload{widx % 2}"], 16)
            for ch in range(8):
                if ch < 2:
                    g.wait_ge(S["mm"], 4 * ND)
                if ch >= 2:
                    g.wait_ge(S["vev"], (ch - 1) * 8)
                g.dma_start(out=bvb[ch % 2][:, :],
                            in_=bass.AP(bv, ch * 512, [[0, 128], [1, 512]])).then_inc(S[f"vwload{ch % 2}"], 16)
                for part in range(4):
                    src = bass.AP(wvT, (part * 8 * 128) * D + ch * 512,
                                  [[D, 128], [128 * D, 8], [1, 512]])
                    g.dma_start(out=wvbuf[ch % 2][:, part * 8 * 512:(part + 1) * 8 * 512],
                                in_=src).then_inc(S[f"vwload{ch % 2}"], 16)

        @blk.vector
        def _(v):
            ch = Ch(v, S["vch"])
            ch.wait(S["cload"], 16 * 5)
            for h in range(L0_NH):
                c0 = h * 3
                ch.op(v.scalar_tensor_tensor, btmp[:, 0:128], kdkg_sb[:, 128:256],
                      bw_sb[:, c0:c0 + 1], kdkg_sb[:, 0:128], ALU.mult, ALU.add)
                ch.op(v.scalar_tensor_tensor, btmp[:, 128:256], kdkg_sb[:, 256:384],
                      bw_sb[:, c0 + 1:c0 + 2], btmp[:, 0:128], ALU.mult, ALU.add)
                ch.op(v.scalar_tensor_tensor, bmat[:, h * 128:(h + 1) * 128], kdkg_sb[:, 384:512],
                      bw_sb[:, c0 + 2:c0 + 3], btmp[:, 128:256], ALU.mult, ALU.add)
            xn_n = 0
            tpc_n = 0
            for i in range(NT):
                c = 8 * i
                ch.wait(S[f"xload{i % 2}"], 16 * (i // 2 + 1))
                for ck in range(8):
                    ch.op(v.bn_stats, bnbuf[:, ck * 6:(ck + 1) * 6],
                          x_t[i % 2][:, ck * 512:(ck + 1) * 512])
                ch.op(v.bn_aggr, stat[:, c:c + 2], bnbuf[:, :])
                ch.op_m(S["var"], i + 1, v.tensor_scalar, stat[:, c + 2:c + 3],
                        stat[:, c + 1:c + 2], LN_EPS, None, ALU.add)
                ch.wait(S["sqv"], i + 1)
                ch.op(v.reciprocal, stat[:, c + 7:c + 8], stat[:, c + 6:c + 7])
                if i >= 1:
                    ch.wait(S["tp"], 4 * i)
                xn_n += 1
                ch.op_m(S["xn"], xn_n, v.tensor_scalar, xn_tok[:, :], x_t[i % 2][:, :],
                        stat[:, c:c + 1], stat[:, c + 7:c + 8], ALU.subtract, ALU.mult)
                for gidx in range(4):
                    gi = i * 4 + gidx
                    ch.wait(S["tp"], gi + 1)
                    tpc_n += 1
                    ch.op_m(S["tpc"], tpc_n, v.tensor_copy,
                            xnT_r[:, gidx * 8:(gidx + 1) * 8, i * 128:(i + 1) * 128],
                            ps_ln[gi % 2][:, :])
            def deq_tile(j):
                if j < 2:
                    ch.wait(S[f"qdcl{j}"], 32)
                else:
                    ch.wait(S[f"dcl{j % 2}"], 32 * ((j - 2) // 2 + 1))
                if j >= 2:
                    ch.wait(S[f"qdst{j % 2}"], 16 * (j // 2))
                ch.op(v.tensor_copy, dq_cf[:, :], dq_cbuf[j % 2][:, :])
                ch.op(v.tensor_scalar, dq_acc[j % 2][:, :], dq_cf[:, :], 0.0,
                      NF4[0], ALU.is_equal, ALU.mult)
                for tv in range(1, 16):
                    ch.op(v.tensor_scalar, dq_selt[:, :], dq_cf[:, :], float(tv),
                          NF4[tv], ALU.is_equal, ALU.mult)
                    ch.op(v.tensor_tensor, dq_acc[j % 2][:, :], dq_acc[j % 2][:, :],
                          dq_selt[:, :], ALU.add)
                ch.op_m(S["dwd"], j + 1, v.tensor_tensor, dq_acc[j % 2][:, :],
                        dq_acc[j % 2][:, :], dq_mbuf[j % 2][:, :], ALU.mult)

            if deq is not None:
                for j in range(16):
                    deq_tile(j)
            vev_n = 0
            for ch_i in range(8):
                for w in range(2):
                    wv_idx = ch_i * 2 + w
                    ch.wait(S["vmm"], wv_idx + 1)
                    for t4 in range(4):
                        tt = w * 4 + t4
                        if ch_i >= 1:
                            ch.wait(S[f"vst{tt}"], 16 * ch_i)
                        vev_n += 1
                        ch.op_m(S["vev"], vev_n, v.tensor_tensor, vstage[tt][:, :],
                                ps_v[t4][:, 0:512], bvb[ch_i % 2][:, :], ALU.add)

        @blk.scalar
        def _(a):
            ch = Ch(a, S["ach"])
            for i in range(NT):
                c = 8 * i
                ch.wait(S["var"], i + 1)
                ch.op_m(S["sqv"], i + 1, a.activation, stat[:, c + 6:c + 7],
                        stat[:, c + 2:c + 3], AF.Sqrt)
            ev_n = 0
            fev_n = 0
            for ot in range(ND):
                for tc in range(2):
                    gidx = ot * 2 + tc
                    ch.wait(S["mm"], gidx + 1)
                    if ot >= 2 and tc == 0:
                        ch.wait(S["foldmm"], ot - 1)
                    ev_n += 1
                    ch.op_m(S["ev"], ev_n, a.activation,
                            stq[ot % 2][:, tc * 512:(tc + 1) * 512],
                            (ps_a if gidx % 2 == 0 else ps_b)[:, :],
                            AF.Identity, bias=bias_sb[:, ot:ot + 1])
                for fc in range(2):
                    ch.wait(S["foldmm"], ot + 1)
                    if ot >= 2:
                        ch.wait(S[f"stq_s{ot % 2}"], 16 * (ot // 2))
                    fev_n += 1
                    ch.op_m(S["fev"], fev_n, a.activation,
                            stq2[ot % 2][:, fc * 512:(fc + 1) * 512],
                            ps_v[fc][:, :], AF.Copy)
            for ot in range(ND):
                for tc in range(2):
                    gidx = 2 * ND + ot * 2 + tc
                    ch.wait(S["mm"], gidx + 1)
                    if ot >= 2 and tc == 0:
                        ch.wait(S[f"stk_s{ot % 2}"], 16 * (ot // 2))
                    ev_n += 1
                    ch.op_m(S["ev"], ev_n, a.activation,
                            stk[ot % 2][:, tc * 512:(tc + 1) * 512],
                            (ps_a if gidx % 2 == 0 else ps_b)[:, :],
                            AF.Identity, bias=bias_sb[:, ND + ot:ND + ot + 1])

        @blk.tensor
        def _(t):
            t.wait_ge(S["cload"], 16 * 5)
            for i in range(NT):
                t.wait_ge(S["xn"], i + 1)
                for gidx in range(4):
                    gi = i * 4 + gidx
                    if gi >= 2:
                        t.wait_ge(S["tpc"], gi - 1)
                    for j in range(8):
                        ins = t.transpose(ps_ln[gi % 2][:, j * 128:(j + 1) * 128],
                                          xn_tok[:, (gidx * 8 + j) * 128:(gidx * 8 + j + 1) * 128],
                                          idn_sb[:, :])
                        if j == 7:
                            ins.then_inc(S["tp"], 1)
            t.wait_ge(S["tpc"], 32)
            for widx in range(2 * ND):
                ot = widx % ND
                is_q = widx < ND
                if is_q and ot >= 1:
                    _emit_fold(t, S, bmat, stq, ps_v, ot - 1)
                if widx == ND:
                    _emit_fold(t, S, bmat, stq, ps_v, ND - 1)
                t.wait_ge(S[f"wload{widx % 2}"], 64 * (widx // 2 + 1))
                for tc in range(2):
                    gidx = widx * 2 + tc
                    if gidx >= 2:
                        t.wait_ge(S["ev"], gidx - 1)
                    for in_ in range(ND):
                        ins = t.matmul((ps_a if gidx % 2 == 0 else ps_b)[:, :],
                                       wbuf[widx % 2][:, in_ * 128:(in_ + 1) * 128],
                                       xnT[:, in_ * TOK + tc * 512:in_ * TOK + (tc + 1) * 512],
                                       start=(in_ == 0), stop=(in_ == ND - 1))
                        if in_ == ND - 1:
                            ins.then_inc(S["mm"], 1)

            t.wait_ge(S["fev"], 2 * ND)
            for ch in range(8):
                t.wait_ge(S[f"vwload{ch % 2}"], 80 * (ch // 2 + 1))
                for w in range(2):
                    wv_idx = ch * 2 + w
                    if wv_idx >= 1:
                        t.wait_ge(S["vev"], wv_idx * 4)
                    for in_ in range(ND):
                        for t4 in range(4):
                            tt = w * 4 + t4
                            ins = t.matmul(ps_v[t4][:, 0:512],
                                           xnT[:, in_ * TOK + tt * 128:in_ * TOK + (tt + 1) * 128],
                                           wvbuf[ch % 2][:, in_ * 512:(in_ + 1) * 512],
                                           start=(in_ == 0), stop=(in_ == ND - 1))
                            if in_ == ND - 1 and t4 == 3:
                                ins.then_inc(S["vmm"], 1)


def _emit_fold(t, S, bmat, stq, ps_v, ot):
    h = ot // 2
    t.wait_ge(S["vch"], 48)
    t.wait_ge(S["ev"], 2 * ot + 2)
    if ot >= 1:
        t.wait_ge(S["fev"], 2 * (ot - 1) + 2)
    for fc in range(2):
        ins = t.matmul(ps_v[fc][:, :], bmat[:, h * 128:(h + 1) * 128],
                       stq[ot % 2][:, fc * 512:(fc + 1) * 512],
                       start=True, stop=True)
        if fc == 1:
            ins.then_inc(S["foldmm"], 1)




def emit_poll(g, S, psem_name, side_sb, flags_sb, poll_sb, flc, fl0, fl1, pre=""):
    with (g.register(pre + "sdp") as sd, g.register(pre + "cont") as cont,
          g.register(pre + "fv") as fv, g.register(pre + "exp") as exp,
          g.register(pre + "pc") as pc):
        g.reg_load(sd, side_sb[0:1, 0:1])
        g.reg_load(exp, flags_sb[0:1, flc * 16:flc * 16 + 1])
        g.reg_mov(pc, 0)

        def loop(fl):
            g.reg_mov(cont, 1)
            with g.While(cont):
                g.reg_add(pc, pc, 16)
                g.dma_start(out=poll_sb[:, :], in_=fl[0:1, :]).then_inc(S[psem_name], 16)
                g.wait_ge(S[psem_name], pc)
                g.reg_load(fv, poll_sb[0:1, 0:1])
                g.reg_mov(cont, 1)
                with g.If_eq(fv, exp):
                    g.reg_mov(cont, 0)

        with g.If_eq(sd, 0):
            loop(fl1)
        with g.Else():
            loop(fl0)


def build_attn(layer):
    nc = bass.Bass("TRN2", target_bir_lowering=False)
    KVD = D if layer == 0 else L1_KV * L1_HD
    T = {}
    T["qT"] = nc.dram_tensor("qT", [D, TOK], BF16, kind="ExternalInput")
    T["kT"] = nc.dram_tensor("kT", [KVD, SEQ], BF16, kind="ExternalInput")
    T["v"] = nc.dram_tensor("v", [SEQ, KVD], BF16, kind="ExternalInput")
    T["x"] = nc.dram_tensor("x", [TOK, D], F32, kind="ExternalInput")
    T["idn"] = nc.dram_tensor("idn", [128, 128], BF16, kind="ExternalInput")
    T["idnf"] = nc.dram_tensor("idnf", [128, 128], F32, kind="ExternalInput")
    T["x1"] = nc.dram_tensor("x1", [TOK, D], F32, kind="ExternalOutput")
    with ExitStack() as sst:
        emit_attn(nc, _semf(nc, sst, "B"), T, layer, fused=False)
    return nc


def emit_attn(nc, sf, T, layer, fused, pre="", deq=None):
    if layer == 0:
        NH, HD, REP, SCALE = L0_NH, L0_HD, 1, None
        KVD = D
    else:
        NH, HD, REP, SCALE = L1_H, L1_HD, 4, 1.0 / np.sqrt(L1_HD)
        KVD = L1_KV * L1_HD
    qT_i, x_i, idn, x1_o = T["qT"], T["x"], T["idn"], T["x1"]
    idn_f = T["idnf"]
    if fused:
        k0, k1, v0, v1 = T["k0"], T["k1"], T["v0"], T["v1"]
        side_sb, flags_sb, poll_sb = T["side_sb"], T["flags_sb"], T["poll_sb"]
        fl0, fl1, FLC = T["fl0"], T["fl1"], T["flc"]
    else:
        kT_i, v_i = T["kT"], T["v"]
    NHT = HD // 128
    NU = NH * NT
    NG = NH // REP
    GU = REP * NT

    with ExitStack() as st:
        sb = lambda nm, sh, dt: st.enter_context(nc.sbuf_tensor(pre + nm, sh, dt))
        ps = lambda nm, sh, dt: st.enter_context(nc.psum_tensor(pre + nm, sh, dt))
        sem = sf

        qhb = [sb(f"qhb{i}", [128, NHT * TOK], BF16) for i in range(2)]
        kbuf = [sb(f"kbuf{i}", [128, NHT * SEQ], BF16) for i in range(2)]
        vbuf = [sb(f"vbuf{i}", [128, NK * HD], BF16) for i in range(2)]
        pb_sb = [sb(f"pb_sb{i}", [128, SEQ], BF16) for i in range(2)]
        ptsb = [sb(f"ptsb{i}", [128, NK * 512], BF16) for i in range(2)]
        osb = [sb(f"osb{i}", [128, 512], F32) for i in range(2)]
        xsl = [sb(f"xsl{i}", [128, HD], F32) for i in range(4)]
        xo = [sb(f"xo{i}", [128, HD], F32) for i in range(4)]
        stat2 = sb("stat2", [128, 64], F32)
        idn_sb = sb("idn_sb", [128, 128], BF16)
        idn32 = sb("idn32", [128, 128], F32)

        ps_s = ps("ps_s", [128, SEQ], F32)
        ps_t = [ps(f"ps_t{i}", [128, 1024], BF16) for i in range(2)]
        ps_o = [ps(f"ps_o{i}", [128, 512], F32) for i in range(2)]

        blk = st.enter_context(nc.Block())
        S = {n: sem(n) for n in
             ["cload", "qlo0", "qlo1", "klo0", "klo1", "smmc", "mx", "exq", "tp2",
              "tpc2", "tpc2a", "pvs", "oev", "untr", "sst", "xld0", "xld1", "xld2", "xld3",
              "xst0", "xst1", "xst2", "xst3", "vch", "ach", "psem"]}
        NKC = SEQ // 512
        NSU = NU // 4
        if deq is not None:
            for n in ["dcl0", "dcl1", "dwd", "dst0", "dst1", "dflst"]:
                S[n] = sem(n)
            dq_codes, dq_amx, dq_out = deq["codes"], deq["amx"], deq["wT_sh"]
            dq_fl0, dq_fl1, dq_flc = deq["flW0"], deq["flW1"], deq["flcW"]
            DQ0 = 2 * deq.get("t0", 0)
            dq_cbuf = [sb(f"dq_cbuf{i}", [128, 2048], I32) for i in range(2)]
            dq_mbuf = [sb(f"dq_mbuf{i}", [128, 2048], F32) for i in range(2)]
            dq_cf = sb("dq_cf", [128, 2048], BF16)
            dq_selt = sb("dq_selt", [128, 2048], BF16)
            dq_acc = [sb(f"dq_acc{i}", [128, 2048], BF16) for i in range(2)]
            NDQ = 32

            def dq_srcs(j):
                r, hf = j // 2, j % 2
                return (dq_codes[r * 128:(r + 1) * 128, hf * 2048:(hf + 1) * 2048],
                        bass.AP(dq_amx, (r * 2) * D + hf * 2048,
                                [[D, 2], [0, 64], [1, 2048]]))

        KLN = 64 if fused else 32

        def kv_load(s, g):
            if not fused:
                srck = bass.AP(kT_i, (g * HD) * SEQ,
                               [[SEQ, 128], [128 * SEQ, NHT], [1, SEQ]])
                s.dma_start(out=kbuf[g % 2][:, :], in_=srck).then_inc(S[f"klo{g % 2}"], 16)
                srcv = bass.AP(v_i, g * HD,
                               [[KVD, 128], [128 * KVD, NK], [1, HD]])
                s.dma_start(out=vbuf[g % 2][:, :], in_=srcv).then_inc(S[f"klo{g % 2}"], 16)
            else:
                kb_r = kbuf[g % 2].ap().rearrange("p (d s) -> p d s", s=SEQ)
                for half, ksrc in ((0, k0), (1, k1)):
                    srck = bass.AP(ksrc, (g * HD) * TOK,
                                   [[TOK, 128], [128 * TOK, NHT], [1, TOK]])
                    s.dma_start(out=kb_r[:, :, half * TOK:(half + 1) * TOK],
                                in_=srck).then_inc(S[f"klo{g % 2}"], 16)
                for half, vsrc in ((0, v0), (1, v1)):
                    srcv = bass.AP(vsrc, g * HD,
                                   [[KVD, 128], [128 * KVD, NK // 2], [1, HD]])
                    s.dma_start(out=vbuf[g % 2][:, half * (NK // 2) * HD:(half + 1) * (NK // 2) * HD],
                                in_=srcv).then_inc(S[f"klo{g % 2}"], 16)

        @blk.sync
        def _(s):
            sd_ctx = s.register(pre + "sd") if fused else None
            sd = sd_ctx.__enter__() if fused else None
            if fused:
                s.reg_load(sd, side_sb[0:1, 0:1])

            def dq_store(j):
                r, hf = j // 2, j % 2
                cs = slice(hf * 2048, (hf + 1) * 2048)
                with s.If_eq(sd, 0):
                    s.dma_start(out=dq_out[r * 128:(r + 1) * 128, cs],
                                in_=dq_acc[j % 2][:, :]).then_inc(S[f"qdst{j % 2}"], 16)
                with s.Else():
                    s.dma_start(out=dq_out[2048 + r * 128:2048 + (r + 1) * 128, cs],
                                in_=dq_acc[j % 2][:, :]).then_inc(S[f"qdst{j % 2}"], 16)

            s.dma_start(out=idn_sb[:, :], in_=idn[:, :]).then_inc(S["cload"], 16)
            s.dma_start(out=idn32[:, :], in_=idn_f[:, :]).then_inc(S["cload"], 16)
            for u in range(NU):
                h, qt = u // NT, u % NT
                if u >= 4:
                    s.wait_ge(S["sst"], u - 3)
                s.dma_start(out=xsl[u % 4][:, :],
                            in_=bass.AP(x_i, qt * 128 * D + h * HD,
                                        [[D, 128], [1, HD]])).then_inc(S[f"xld{u % 4}"], 16)
                if u >= 1:
                    up = u - 1
                    hp, qtp = up // NT, up % NT
                    s.wait_ge(S["sst"], up + 1)
                    s.dma_start(out=bass.AP(x1_o, qtp * 128 * D + hp * HD,
                                            [[D, 128], [1, HD]]),
                                in_=xo[up % 4][:, :]).then_inc(S[f"xst{up % 4}"], 16)
            up = NU - 1
            s.wait_ge(S["sst"], up + 1)
            s.dma_start(out=bass.AP(x1_o, (up % NT) * 128 * D + (up // NT) * HD,
                                    [[D, 128], [1, HD]]),
                        in_=xo[up % 4][:, :]).then_inc(S[f"xst{up % 4}"], 16)
            for p_ in range(4):
                s.wait_ge(S[f"xst{p_}"], 16 * (NU // 4))

        @blk.gpsimd
        def _(g):
            dq_sd_ctx = g.register(pre + "gsd") if deq is not None else None
            dq_sd = dq_sd_ctx.__enter__() if deq is not None else None

            def dq_load(t_):
                if t_ - DQ0 >= 2:
                    g.wait_ge(S["dwd"], t_ - DQ0 - 1)
                csrc, msrc = dq_srcs(t_)
                g.dma_start(out=dq_cbuf[t_ % 2][:, :],
                            in_=csrc).then_inc(S[f"dcl{t_ % 2}"], 16)
                g.dma_start(out=dq_mbuf[t_ % 2][:, :],
                            in_=msrc).then_inc(S[f"dcl{t_ % 2}"], 16)

            def dq_store(t_):
                g.wait_ge(S["dwd"], t_ - DQ0 + 1)
                r, hf = t_ // 2, t_ % 2
                cs = slice(hf * 2048, (hf + 1) * 2048)
                with g.If_eq(dq_sd, 0):
                    g.dma_start(out=dq_out[r * 128:(r + 1) * 128, cs],
                                in_=dq_acc[t_ % 2][:, :]).then_inc(S[f"dst{t_ % 2}"], 16)
                with g.Else():
                    g.dma_start(out=dq_out[2048 + r * 128:2048 + (r + 1) * 128, cs],
                                in_=dq_acc[t_ % 2][:, :]).then_inc(S[f"dst{t_ % 2}"], 16)

            if deq is not None:
                g.reg_load(dq_sd, side_sb[0:1, 0:1])
                dq_load(DQ0)
                dq_load(DQ0 + 1)
                for t_ in range(DQ0 + 2, NDQ):
                    dq_load(t_)
                    dq_store(t_ - 2)
                for t_ in range(NDQ - 2, NDQ):
                    dq_store(t_)
                g.wait_ge(S["dst0"], 16 * ((NDQ - DQ0) // 2))
                g.wait_ge(S["dst1"], 16 * ((NDQ - DQ0) // 2))
                with g.If_eq(dq_sd, 0):
                    g.dma_start(out=dq_fl0[:, :],
                                in_=flags_sb[0:1, dq_flc * 16:(dq_flc + 1) * 16]).then_inc(S["dflst"], 16)
                with g.Else():
                    g.dma_start(out=dq_fl1[:, :],
                                in_=flags_sb[0:1, dq_flc * 16:(dq_flc + 1) * 16]).then_inc(S["dflst"], 16)
                g.wait_ge(S["dflst"], 16)
                dq_sd_ctx.__exit__(None, None, None)
            if fused:
                emit_poll(g, S, "psem", side_sb, flags_sb, poll_sb, FLC, fl0, fl1, pre=pre)
            for h in range(NH):
                if h % REP == 0:
                    gg = h // REP
                    if gg >= 2:
                        g.wait_ge(S["smmc"], 4 * GU * (gg - 1))
                        g.wait_ge(S["pvs"], (GU // 4) * NHT * (gg - 1))
                    kv_load(g, gg)
                if h >= 2:
                    g.wait_ge(S["smmc"], 4 * NT * (h - 1))
                srcq = bass.AP(qT_i, (h * NHT * 128) * TOK,
                               [[TOK, 128], [128 * TOK, NHT], [1, TOK]])
                g.dma_start(out=qhb[h % 2][:, :], in_=srcq).then_inc(S[f"qlo{h % 2}"], 16)


        @blk.tensor
        def _(t):
            t.wait_ge(S["cload"], 32)
            for u in range(NU):
                h, qt = u // NT, u % NT
                g_ = h // REP
                t.wait_ge(S[f"klo{g_ % 2}"], KLN * (g_ // 2 + 1))
                t.wait_ge(S[f"qlo{h % 2}"], 16 * (h // 2 + 1))
                for d in range(NHT):
                    for kc in range(NKC):
                        if d == 0 and u >= 1:
                            t.wait_ge(S["exq"], (u - 1) * NKC + kc + 1)
                        ins = t.matmul(ps_s[:, kc * 512:(kc + 1) * 512],
                                       qhb[h % 2][:, d * TOK + qt * 128:d * TOK + (qt + 1) * 128],
                                       kbuf[g_ % 2][:, d * SEQ + kc * 512:d * SEQ + (kc + 1) * 512],
                                       start=(d == 0), stop=(d == NHT - 1))
                        if d == NHT - 1:
                            ins.then_inc(S["smmc"], 1)
                if u >= 5 and (u - 5) % 4 == 0:
                    _b_untr(t, S, (u - 5) // 4, osb, ps_o, idn32, NHT)
                if u >= 1:
                    _b_trans(t, S, u - 1, pb_sb, ps_t, idn_sb, NK)
                    if (u - 1) % 4 == 3:
                        _b_pv(t, S, (u - 1) // 4, ptsb, vbuf, osb, ps_o, idn32,
                              NHT, NK, HD, REP)
            _b_trans(t, S, NU - 1, pb_sb, ps_t, idn_sb, NK)
            _b_pv(t, S, NSU - 1, ptsb, vbuf, osb, ps_o, idn32, NHT, NK, HD, REP)
            _b_untr(t, S, NSU - 1, osb, ps_o, idn32, NHT)

        @blk.scalar
        def _(a):
            ch = Ch(a, S["ach"])
            for u in range(NU):
                ch.wait(S["mx"], u + 1)
                if u >= 2:
                    ch.wait(S["tp2"], u - 1)
                c = (u % 4) * 16
                kw = {} if SCALE is None else {"scale": float(SCALE)}
                for kc in range(4):
                    ch.op_m(S["exq"], u * 4 + kc + 1, a.activation,
                            pb_sb[u % 2][:, kc * 512:(kc + 1) * 512],
                            ps_s[:, kc * 512:(kc + 1) * 512],
                            AF.Exp, bias=stat2[:, c + 4:c + 5],
                            accum_out=stat2[:, c + 5 + kc:c + 6 + kc], **kw)
                if u >= 1:
                    _attn_acopy(a, ch, S, u - 1, ptsb, ps_t)
            _attn_acopy(a, ch, S, NU - 1, ptsb, ps_t)

        @blk.vector
        def _(v):
            ch = Ch(v, S["vch"])

            def deq_gen():
                for t_ in range(DQ0, NDQ):
                    lt = t_ - DQ0
                    ch.wait(S[f"dcl{t_ % 2}"], 32 * (lt // 2 + 1))
                    if lt >= 2:
                        ch.wait(S[f"dst{t_ % 2}"], 16 * (lt // 2))
                    ch.op(v.tensor_copy, dq_cf[:, :], dq_cbuf[t_ % 2][:, :])
                    ch.op(v.tensor_scalar, dq_acc[t_ % 2][:, :], dq_cf[:, :], 0.0,
                          NF4[0], ALU.is_equal, ALU.mult)
                    for tv in range(1, 16):
                        ch.op(v.tensor_scalar, dq_selt[:, :], dq_cf[:, :], float(tv),
                              NF4[tv], ALU.is_equal, ALU.mult)
                        ch.op(v.tensor_tensor, dq_acc[t_ % 2][:, :], dq_acc[t_ % 2][:, :],
                              dq_selt[:, :], ALU.add)
                    ch.op_m(S["dwd"], lt + 1, v.tensor_tensor, dq_acc[t_ % 2][:, :],
                            dq_acc[t_ % 2][:, :], dq_mbuf[t_ % 2][:, :], ALU.mult)

            if deq is not None:
                deq_gen()
            mxn = 0
            sstn = 0
            for u in range(NU):
                c = (u % 4) * 16
                for kc in range(4):
                    ch.wait(S["smmc"], u * 4 + kc + 1)
                    ch.op(v.reduce_max, stat2[:, c + kc:c + kc + 1],
                          ps_s[:, kc * 512:(kc + 1) * 512], axis=AX.X, negate=True)
                mxn += 1
                if SCALE is None:
                    ch.op_m(S["mx"], mxn, v.tensor_reduce, stat2[:, c + 4:c + 5],
                            stat2[:, c:c + 4], AX.X, ALU.min)
                else:
                    ch.op(v.tensor_reduce, stat2[:, c + 9:c + 10],
                          stat2[:, c:c + 4], AX.X, ALU.min)
                    ch.op_m(S["mx"], mxn, v.tensor_scalar, stat2[:, c + 4:c + 5],
                            stat2[:, c + 9:c + 10], float(SCALE), None, ALU.mult)
                if u >= 5 and (u - 5) % 4 == 0:
                    sstn = _attn_vstt(v, ch, S, (u - 5) // 4, ps_o, stat2,
                                      xo, xsl, HD, NHT, sstn)
                if u >= 1:
                    _attn_vcopy(v, ch, S, u - 1, ptsb, ps_t, stat2)
                    if (u - 1) % 4 == 3:
                        _attn_vev(v, ch, S, (u - 1) // 4, osb, ps_o, NHT)
            _attn_vcopy(v, ch, S, NU - 1, ptsb, ps_t, stat2)
            _attn_vev(v, ch, S, NSU - 1, osb, ps_o, NHT)
            sstn = _attn_vstt(v, ch, S, NSU - 1, ps_o, stat2, xo, xsl, HD, NHT, sstn)


def _attn_vcopy(v, ch, S, up, ptsb, ps_t, stat2):
    su, sq = up // 4, up % 4
    cp = (up % 4) * 16
    pt_r = ptsb[su % 2].ap().rearrange("p (k q) -> p k q", q=512)
    ch.wait(S["tp2"], up + 1)
    ch.op_m(S["tpc2"], up + 1, v.tensor_copy,
            pt_r[:, 0:8, sq * 128:(sq + 1) * 128], ps_t[0][:, :])
    ch.wait(S["exq"], 4 * (up + 1))
    ch.op(v.tensor_reduce, stat2[:, cp + 10:cp + 11], stat2[:, cp + 5:cp + 9],
          AX.X, ALU.add)
    ch.op(v.reciprocal, stat2[:, cp + 11:cp + 12], stat2[:, cp + 10:cp + 11])


def _attn_acopy(a, ch, S, up, ptsb, ps_t):
    su, sq = up // 4, up % 4
    pt_r = ptsb[su % 2].ap().rearrange("p (k q) -> p k q", q=512)
    ch.wait(S["tp2"], up + 1)
    ch.op_m(S["tpc2a"], up + 1, a.activation,
            pt_r[:, 8:16, sq * 128:(sq + 1) * 128], ps_t[1][:, :], AF.Copy)


def _attn_vev(v, ch, S, su, osb, ps_o, NHT):
    for dt in range(NHT):
        odt = su * NHT + dt
        ch.wait(S["pvs"], odt + 1)
        ch.op_m(S["oev"], odt + 1, v.tensor_copy, osb[odt % 2][:, :],
                ps_o[odt % 2][:, :])


def _attn_vstt(v, ch, S, su, ps_o, stat2, xo, xsl, HD, NHT, sstn):
    for sq in range(4):
        u = su * 4 + sq
        cp = (u % 4) * 16
        ch.wait(S[f"xld{u % 4}"], 16 * (u // 4 + 1))
        if u >= 4:
            ch.wait(S[f"xst{u % 4}"], 16 * (u // 4))
        for dt in range(NHT):
            odt = su * NHT + dt
            ch.wait(S["untr"], odt + 1)
            sstn += (1 if dt == NHT - 1 else 0)
            if dt == NHT - 1:
                ch.op_m(S["sst"], sstn, v.scalar_tensor_tensor,
                        xo[u % 4][:, dt * 128:(dt + 1) * 128],
                        ps_o[odt % 2][:, sq * 128:(sq + 1) * 128],
                        stat2[:, cp + 11:cp + 12],
                        xsl[u % 4][:, dt * 128:(dt + 1) * 128], ALU.mult, ALU.add)
            else:
                ch.op(v.scalar_tensor_tensor,
                      xo[u % 4][:, dt * 128:(dt + 1) * 128],
                      ps_o[odt % 2][:, sq * 128:(sq + 1) * 128],
                      stat2[:, cp + 11:cp + 12],
                      xsl[u % 4][:, dt * 128:(dt + 1) * 128], ALU.mult, ALU.add)
    return sstn


def _b_trans(t, S, up, pb_sb, ps_t, idn_sb, NKt):
    t.wait_ge(S["exq"], 4 * (up + 1))
    if up >= 1:
        t.wait_ge(S["tpc2"], up)
        t.wait_ge(S["tpc2a"], up)
    for j in range(NKt):
        ins = t.transpose(ps_t[j // 8][:, (j % 8) * 128:(j % 8 + 1) * 128],
                          pb_sb[up % 2][:, j * 128:(j + 1) * 128], idn_sb[:, :])
        if j == NKt - 1:
            ins.then_inc(S["tp2"], 1)


def _b_pv(t, S, su, ptsb, vbuf, osb, ps_o, idn32, NHT, NKt, HD, REP):
    hp = ((su * 4) // NT) // REP
    t.wait_ge(S["tpc2"], 4 * su + 4)
    t.wait_ge(S["tpc2a"], 4 * su + 4)
    for dt in range(NHT):
        odt = su * NHT + dt
        if odt >= 2:
            su2 = (odt - 2) // NHT
            t.wait_ge(S["sst"], 4 * (su2 + 1))
        for kt in range(NKt):
            ins = t.matmul(ps_o[odt % 2][:, :],
                           vbuf[hp % 2][:, kt * HD + dt * 128:kt * HD + (dt + 1) * 128],
                           ptsb[su % 2][:, kt * 512:(kt + 1) * 512],
                           start=(kt == 0), stop=(kt == NKt - 1))
            if kt == NKt - 1:
                ins.then_inc(S["pvs"], 1)

def _b_untr(t, S, su, osb, ps_o, idn32, NHT):
    for dt in range(NHT):
        odt = su * NHT + dt
        t.wait_ge(S["oev"], odt + 1)
        for sq in range(4):
            ins = t.transpose(ps_o[odt % 2][:, sq * 128:(sq + 1) * 128],
                              osb[odt % 2][:, sq * 128:(sq + 1) * 128], idn32[:, :])
            if sq == 3:
                ins.then_inc(S["untr"], 1)




def build_dequant():
    nc = bass.Bass("TRN2", target_bir_lowering=False)
    T = {}
    T["codes"] = nc.dram_tensor("codes", [512, D], I32, kind="ExternalInput")
    T["amx"] = nc.dram_tensor("amx", [8, D], F32, kind="ExternalInput")
    T["wsh"] = nc.dram_tensor("wsh", [512, D], BF16, kind="ExternalOutput")
    with ExitStack() as sst:
        emit_dequant(nc, _semf(nc, sst, "C"), T, fused=False)
    return nc


def emit_dequant(nc, sf, T, fused, pre=""):
    codes, amx = T["codes"], T["amx"]
    NTL = 16 if fused else 4
    if fused:
        wT_sh = T["wT_sh"]
        side_sb, flags_sb = T["side_sb"], T["flags_sb"]
        fl0, fl1, FLC = T["flW0"], T["flW1"], T["flcW"]
    else:
        wsh = T["wsh"]
    with ExitStack() as st:
        sb = lambda nm, sh, dt: st.enter_context(nc.sbuf_tensor(pre + nm, sh, dt))
        sem = sf
        cbuf = [sb(f"cbuf{i}", [128, D], I32) for i in range(2)]
        mbuf = [sb(f"mbuf{i}", [128, D], F32) for i in range(2)]
        cf = sb("cf", [128, D], BF16)
        selt = sb("selt", [128, D], BF16)
        acc = [sb(f"acc{i}", [128, D], BF16) for i in range(2)]
        blk = st.enter_context(nc.Block())
        S = {n: sem(n) for n in ["cl0", "cl1", "wd", "st0", "st1", "vch", "flst"]}

        @blk.sync
        def _(s):
            sd_ctx = s.register(pre + "sd") if fused else None
            sd = sd_ctx.__enter__() if fused else None
            if fused:
                s.reg_load(sd, side_sb[0:1, 0:1])

            def do_store(t):
                s.wait_ge(S["wd"], t + 1)
                if fused:
                    with s.If_eq(sd, 0):
                        s.dma_start(out=wT_sh[t * 128:(t + 1) * 128, :],
                                    in_=acc[t % 2][:, :]).then_inc(S[f"st{t % 2}"], 16)
                    with s.Else():
                        s.dma_start(out=wT_sh[2048 + t * 128:2048 + (t + 1) * 128, :],
                                    in_=acc[t % 2][:, :]).then_inc(S[f"st{t % 2}"], 16)
                else:
                    s.dma_start(out=wsh[t * 128:(t + 1) * 128, :],
                                in_=acc[t % 2][:, :]).then_inc(S[f"st{t % 2}"], 16)

            for t in range(NTL):
                if t >= 2:
                    s.wait_ge(S["wd"], t - 1)
                s.dma_start(out=cbuf[t % 2][:, :],
                            in_=codes[t * 128:(t + 1) * 128, :]).then_inc(S[f"cl{t % 2}"], 16)
                s.dma_start(out=mbuf[t % 2][:, :],
                            in_=bass.AP(amx, (t * 2) * D, [[D, 2], [0, 64], [1, D]])).then_inc(S[f"cl{t % 2}"], 16)
                if t >= 2:
                    do_store(t - 2)
            do_store(NTL - 2)
            do_store(NTL - 1)
            s.wait_ge(S["st0"], 16 * (NTL // 2))
            s.wait_ge(S["st1"], 16 * (NTL // 2))
            if fused:
                with s.If_eq(sd, 0):
                    s.dma_start(out=fl0[:, :], in_=flags_sb[0:1, FLC * 16:(FLC + 1) * 16]).then_inc(S["flst"], 16)
                with s.Else():
                    s.dma_start(out=fl1[:, :], in_=flags_sb[0:1, FLC * 16:(FLC + 1) * 16]).then_inc(S["flst"], 16)
                s.wait_ge(S["flst"], 16)
                sd_ctx.__exit__(None, None, None)

        @blk.vector
        def _(v):
            ch = Ch(v, S["vch"])
            for t in range(NTL):
                ch.wait(S[f"cl{t % 2}"], 32 * (t // 2 + 1))
                if t >= 2:
                    ch.wait(S[f"st{t % 2}"], 16 * (t // 2))
                ch.op(v.tensor_copy, cf[:, :], cbuf[t % 2][:, :])
                ch.op(v.tensor_scalar, acc[t % 2][:, :], cf[:, :], 0.0, NF4[0],
                      ALU.is_equal, ALU.mult)
                for tv in range(1, 16):
                    ch.op(v.tensor_scalar, selt[:, :], cf[:, :], float(tv), NF4[tv],
                          ALU.is_equal, ALU.mult)
                    ch.op(v.tensor_tensor, acc[t % 2][:, :], acc[t % 2][:, :],
                          selt[:, :], ALU.add)
                ch.op_m(S["wd"], t + 1, v.tensor_tensor, acc[t % 2][:, :],
                        acc[t % 2][:, :], mbuf[t % 2][:, :], ALU.mult)


def build_ffn(layer):
    nc = bass.Bass("TRN2", target_bir_lowering=False)
    T = {}
    T["x"] = nc.dram_tensor("x", [TOK, D], F32, kind="ExternalInput")
    if layer == 0:
        T["wT"] = nc.dram_tensor("wT", [D, D], BF16, kind="ExternalInput")
        T["waT"] = nc.dram_tensor("waT", [128, 512], F32, kind="ExternalInput")
        T["wbT"] = nc.dram_tensor("wbT", [16, D], F32, kind="ExternalInput")
    else:
        T["wT"] = nc.dram_tensor("wT", [D, D], F16, kind="ExternalInput")
        T["brow"] = nc.dram_tensor("brow", [1, D], F16, kind="ExternalInput")
    T["idn"] = nc.dram_tensor("idn", [128, 128], BF16, kind="ExternalInput")
    T["xo"] = nc.dram_tensor("xo", [TOK, D], F32, kind="ExternalOutput")
    with ExitStack() as sst:
        emit_ffn(nc, _semf(nc, sst, "D"), T, layer, fused=False)
    return nc


def emit_ffn(nc, sf, T, layer, fused, pre=""):
    x_i, wT, idn, xo_o = T["x"], T["wT"], T["idn"], T["xo"]
    if layer == 0:
        waT, wbT = T["waT"], T["wbT"]
    else:
        brow = T["brow"]
    if fused and layer == 0:
        side_sb, flags_sb, poll_sb = T["side_sb"], T["flags_sb"], T["poll_sb"]
        fl0, fl1, FLC = T["flW0"], T["flW1"], T["flcW"]
    NCH = 8
    CW = 512
    WDT = BF16 if layer == 0 else F16
    import os
    dbg_nolora = bool(os.environ.get("FUSE_DBG_NOLORA"))

    with ExitStack() as st:
        sb = lambda nm, sh, dt: st.enter_context(nc.sbuf_tensor(pre + nm, sh, dt))
        ps = lambda nm, sh, dt: st.enter_context(nc.psum_tensor(pre + nm, sh, dt))
        sem = sf
        xnT = sb("xnT", [128, ND * TOK], BF16)
        xnT_r = xnT.ap().rearrange("p (f t) -> p f t", t=TOK)
        x_t = [sb(f"x_t{i}", [128, D], F32) for i in range(2)]
        stat = sb("stat", [128, 8 * NT], F32)
        bnbuf = sb("bnbuf", [128, 48], F32)
        xn_tok = sb("xn_tok", [128, D], BF16)
        idn_sb = sb("idn_sb", [128, 128], BF16)
        wvbuf = [sb(f"wvbuf{i}", [128, ND * CW], WDT) for i in range(2)]
        xsl = [sb(f"xsl{i}", [128, CW], F32) for i in range(2)]
        xos = [sb(f"xos{i}", [128, CW], F32) for i in range(2)]
        if layer == 0:
            wa_sb = sb("wa_sb", [128, 512], BF16)
            wb_sb = sb("wb_sb", [16, D], BF16)
            zsb = sb("zsb", [16, TOK], BF16)
        else:
            biasb = [sb(f"biasb{i}", [128, CW], F32) for i in range(2)]
            tmpb = sb("tmpb", [128, CW], F32)
        ps_ln = [ps(f"ps_ln{i}", [128, 1024], BF16) for i in range(2)]
        ps_y = [ps(f"ps_y{i}", [128, 512], F32) for i in range(4)]
        if layer == 0:
            ps_z = ps("ps_z", [16, TOK], F32)
        blk = st.enter_context(nc.Block())
        S = {n: sem(n) for n in
             ["cload", "cload2", "vch", "ach", "xload0", "xload1", "var", "sqv", "xn",
              "tp", "tpc", "vwload0", "vwload1", "vmm", "vev",
              "zmm", "zev", "xld0", "xld1", "xst0", "xst1", "psem"]}

        @blk.sync
        def _(s):
            sd_ctx = s.register(pre + "sd") if fused else None
            sd = sd_ctx.__enter__() if fused else None
            if fused:
                s.reg_load(sd, side_sb[0:1, 0:1])

            def dq_store(j):
                r, hf = j // 2, j % 2
                cs = slice(hf * 2048, (hf + 1) * 2048)
                with s.If_eq(sd, 0):
                    s.dma_start(out=dq_out[r * 128:(r + 1) * 128, cs],
                                in_=dq_acc[j % 2][:, :]).then_inc(S[f"qdst{j % 2}"], 16)
                with s.Else():
                    s.dma_start(out=dq_out[2048 + r * 128:2048 + (r + 1) * 128, cs],
                                in_=dq_acc[j % 2][:, :]).then_inc(S[f"qdst{j % 2}"], 16)

            s.dma_start(out=idn_sb[:, :], in_=idn[:, :]).then_inc(S["cload"], 16)
            for i in range(NT):
                if i >= 2:
                    s.wait_ge(S["xn"], i - 1)
                s.dma_start(out=x_t[i % 2][:, :], in_=x_i[i * 128:(i + 1) * 128, :]).then_inc(S[f"xload{i % 2}"], 16)
            NE = NCH * 8
            for e in range(NE):
                chn, tt = e // 8, e % 8
                if e >= 2:
                    s.wait_ge(S["vev"], e - 1)
                s.dma_start(out=xsl[e % 2][:, :],
                            in_=bass.AP(x_i, tt * 128 * D + chn * CW,
                                        [[D, 128], [1, CW]])).then_inc(S[f"xld{e % 2}"], 16)
                if e >= 1:
                    ep = e - 1
                    s.wait_ge(S["vev"], ep + 1)
                    s.dma_start(out=bass.AP(xo_o, (ep % 8) * 128 * D + (ep // 8) * CW,
                                            [[D, 128], [1, CW]]),
                                in_=xos[ep % 2][:, :]).then_inc(S[f"xst{ep % 2}"], 16)
            ep = NE - 1
            s.wait_ge(S["vev"], ep + 1)
            s.dma_start(out=bass.AP(xo_o, (ep % 8) * 128 * D + (ep // 8) * CW,
                                    [[D, 128], [1, CW]]),
                        in_=xos[ep % 2][:, :]).then_inc(S[f"xst{ep % 2}"], 16)
            s.wait_ge(S["xst0"], 16 * (NE // 2))
            s.wait_ge(S["xst1"], 16 * (NE // 2))

        @blk.gpsimd
        def _(g):
            if layer == 0:
                g.dma_start(out=wa_sb[:, :], in_=waT[:, :]).then_inc(S["cload2"], 16)
                g.dma_start(out=wb_sb[:, :], in_=wbT[:, :]).then_inc(S["cload2"], 16)
            import os
            if fused and layer == 0 and not os.environ.get("FUSE_DBG_NOPOLL"):
                emit_poll(g, S, "psem", side_sb, flags_sb, poll_sb, FLC, fl0, fl1, pre=pre)
            for chn in range(NCH):
                if chn >= 2:
                    g.wait_ge(S["vev"], (chn - 1) * 8)
                if layer == 1:
                    g.dma_start(out=biasb[chn % 2][:, :],
                                in_=bass.AP(brow, chn * CW, [[0, 128], [1, CW]])).then_inc(S[f"vwload{chn % 2}"], 16)
                for part in range(4):
                    srcw = bass.AP(wT, (part * 8 * 128) * D + chn * CW,
                                   [[D, 128], [128 * D, 8], [1, CW]])
                    g.dma_start(out=wvbuf[chn % 2][:, part * 8 * CW:(part + 1) * 8 * CW],
                                in_=srcw).then_inc(S[f"vwload{chn % 2}"], 16)

        @blk.vector
        def _(v):
            ch = Ch(v, S["vch"])
            ch.wait(S["cload"], 16)
            xn_n = 0
            tpc_n = 0
            for i in range(NT):
                c = 8 * i
                ch.wait(S[f"xload{i % 2}"], 16 * (i // 2 + 1))
                for ck in range(8):
                    ch.op(v.bn_stats, bnbuf[:, ck * 6:(ck + 1) * 6],
                          x_t[i % 2][:, ck * 512:(ck + 1) * 512])
                ch.op(v.bn_aggr, stat[:, c:c + 2], bnbuf[:, :])
                ch.op_m(S["var"], i + 1, v.tensor_scalar, stat[:, c + 2:c + 3],
                        stat[:, c + 1:c + 2], LN_EPS, None, ALU.add)
                ch.wait(S["sqv"], i + 1)
                ch.op(v.reciprocal, stat[:, c + 7:c + 8], stat[:, c + 6:c + 7])
                if i >= 1:
                    ch.wait(S["tp"], 4 * i)
                xn_n += 1
                ch.op_m(S["xn"], xn_n, v.tensor_scalar, xn_tok[:, :], x_t[i % 2][:, :],
                        stat[:, c:c + 1], stat[:, c + 7:c + 8], ALU.subtract, ALU.mult)
                for gidx in range(4):
                    gi = i * 4 + gidx
                    ch.wait(S["tp"], gi + 1)
                    tpc_n += 1
                    ch.op_m(S["tpc"], tpc_n, v.tensor_copy,
                            xnT_r[:, gidx * 8:(gidx + 1) * 8, i * 128:(i + 1) * 128],
                            ps_ln[gi % 2][:, :])
            vev_n = 0
            for chn in range(NCH):
                for w in range(2):
                    wv_idx = chn * 2 + w
                    ch.wait(S["vmm"], wv_idx + 1)
                    for t4 in range(4):
                        tt = w * 4 + t4
                        e = chn * 8 + tt
                        ch.wait(S[f"xld{e % 2}"], 16 * (e // 2 + 1))
                        if e >= 2:
                            ch.wait(S[f"xst{e % 2}"], 16 * (e // 2))
                        vev_n += 1
                        if layer == 0:
                            ch.op_m(S["vev"], vev_n, v.tensor_tensor, xos[e % 2][:, :],
                                    ps_y[t4][:, 0:CW], xsl[e % 2][:, :], ALU.add)
                        else:
                            ch.op(v.tensor_tensor, tmpb[:, :], ps_y[t4][:, 0:CW],
                                  biasb[chn % 2][:, :], ALU.add)
                            ch.op_m(S["vev"], vev_n, v.tensor_tensor, xos[e % 2][:, :],
                                    tmpb[:, :], xsl[e % 2][:, :], ALU.add)

        @blk.scalar
        def _(a):
            ch = Ch(a, S["ach"])
            for i in range(NT):
                c = 8 * i
                ch.wait(S["var"], i + 1)
                ch.op_m(S["sqv"], i + 1, a.activation, stat[:, c + 6:c + 7],
                        stat[:, c + 2:c + 3], AF.Sqrt)
            if layer == 0 and not dbg_nolora:
                for tc in range(2):
                    ch.wait(S["zmm"], tc + 1)
                    ch.op_m(S["zev"], tc + 1, a.activation,
                            zsb[:, tc * 512:(tc + 1) * 512],
                            ps_z[:, tc * 512:(tc + 1) * 512], AF.Copy,
                            scale=float(FFN_SCALE))

        @blk.tensor
        def _(t):
            t.wait_ge(S["cload"], 16)
            if layer == 0:
                t.wait_ge(S["cload2"], 32)
            for i in range(NT):
                t.wait_ge(S["xn"], i + 1)
                for gidx in range(4):
                    gi = i * 4 + gidx
                    if gi >= 2:
                        t.wait_ge(S["tpc"], gi - 1)
                    for j in range(8):
                        ins = t.transpose(ps_ln[gi % 2][:, j * 128:(j + 1) * 128],
                                          xn_tok[:, (gidx * 8 + j) * 128:(gidx * 8 + j + 1) * 128],
                                          idn_sb[:, :])
                        if j == 7:
                            ins.then_inc(S["tp"], 1)
            t.wait_ge(S["tpc"], 32)
            if layer == 0 and not dbg_nolora:
                for tc in range(2):
                    for in_ in range(ND):
                        ins = t.matmul(ps_z[:, tc * 512:(tc + 1) * 512],
                                       wa_sb[:, in_ * 16:(in_ + 1) * 16],
                                       xnT[:, in_ * TOK + tc * 512:in_ * TOK + (tc + 1) * 512],
                                       start=(in_ == 0), stop=(in_ == ND - 1))
                        if in_ == ND - 1:
                            ins.then_inc(S["zmm"], 1)
            for chn in range(NCH):
                t.wait_ge(S[f"vwload{chn % 2}"], (80 if layer == 1 else 64) * (chn // 2 + 1))
                for w in range(2):
                    wv_idx = chn * 2 + w
                    if wv_idx >= 1:
                        t.wait_ge(S["vev"], wv_idx * 4)
                    last_in = (layer == 1) or dbg_nolora
                    for in_ in range(ND):
                        for t4 in range(4):
                            tt = w * 4 + t4
                            ins = t.matmul(ps_y[t4][:, 0:CW],
                                           xnT[:, in_ * TOK + tt * 128:in_ * TOK + (tt + 1) * 128],
                                           wvbuf[chn % 2][:, in_ * CW:(in_ + 1) * CW],
                                           start=(in_ == 0),
                                           stop=(last_in and in_ == ND - 1))
                            if layer == 1 and in_ == ND - 1 and t4 == 3:
                                ins.then_inc(S["vmm"], 1)
                    if layer == 0 and not dbg_nolora:
                        if chn == 0 and w == 0:
                            t.wait_ge(S["zev"], 2)
                        for t4 in range(4):
                            tt = w * 4 + t4
                            ins = t.matmul(ps_y[t4][:, 0:CW],
                                           zsb[:, tt * 128:(tt + 1) * 128],
                                           wb_sb[:, chn * CW:(chn + 1) * CW],
                                           start=False, stop=True)
                            if t4 == 3:
                                ins.then_inc(S["vmm"], 1)




def build_l1qkv():
    nc = bass.Bass("TRN2", target_bir_lowering=False)
    KVD = L1_KV * L1_HD
    NKT = KVD // 128
    T = {}
    T["x"] = nc.dram_tensor("x", [TOK, D], F32, kind="ExternalInput")
    T["wq1"] = nc.dram_tensor("wq1", [D, D], F32, kind="ExternalInput")
    T["wk1"] = nc.dram_tensor("wk1", [D, KVD], F32, kind="ExternalInput")
    T["wv1"] = nc.dram_tensor("wv1", [D, KVD], F32, kind="ExternalInput")
    T["bq1"] = nc.dram_tensor("bq1", [128, ND], F32, kind="ExternalInput")
    T["bk1"] = nc.dram_tensor("bk1", [128, NKT], F32, kind="ExternalInput")
    T["bv1"] = nc.dram_tensor("bv1", [1, KVD], F32, kind="ExternalInput")
    T["cosx"] = nc.dram_tensor("cosx", [64, TOK], F32, kind="ExternalInput")
    T["sinx"] = nc.dram_tensor("sinx", [64, TOK], F32, kind="ExternalInput")
    T["idn"] = nc.dram_tensor("idn", [128, 128], BF16, kind="ExternalInput")
    T["qT1"] = nc.dram_tensor("qT1", [D, TOK], BF16, kind="ExternalOutput")
    T["kT1"] = nc.dram_tensor("kT1", [KVD, TOK], BF16, kind="ExternalOutput")
    T["v1"] = nc.dram_tensor("v1", [TOK, KVD], BF16, kind="ExternalOutput")
    with ExitStack() as sst:
        emit_l1qkv(nc, _semf(nc, sst, "E"), T, fused=False)
    return nc


def emit_l1qkv(nc, sf, T, fused, pre=""):
    KVD = L1_KV * L1_HD
    NKT = KVD // 128
    x_i = T["x"]
    wq1, wk1, wv1 = T["wq1"], T["wk1"], T["wv1"]
    bq1, bk1, bv1 = T["bq1"], T["bk1"], T["bv1"]
    cosx, sinx, idn = T["cosx"], T["sinx"], T["idn"]
    qT_o = T["qT1"]
    if fused:
        k0, k1, v0, v1 = T["k10"], T["k11"], T["v10"], T["v11"]
        side_sb, flags_sb = T["side_sb"], T["flags_sb"]
        fl0, fl1, FLC = T["flE0"], T["flE1"], T["flcE"]
    else:
        kT_o, v_o = T["kT1"], T["v1"]
    NW = ND + NKT
    NCH = KVD // 512

    with ExitStack() as st:
        sb = lambda nm, sh, dt: st.enter_context(nc.sbuf_tensor(pre + nm, sh, dt))
        ps = lambda nm, sh, dt: st.enter_context(nc.psum_tensor(pre + nm, sh, dt))
        sem = sf
        xnT = sb("xnT", [128, ND * TOK], BF16)
        xnT_r = xnT.ap().rearrange("p (f t) -> p f t", t=TOK)
        x_t = [sb(f"x_t{i}", [128, D], F32) for i in range(2)]
        stat = sb("stat", [128, 8 * NT], F32)
        bnbuf = sb("bnbuf", [128, 48], F32)
        xn_tok = sb("xn_tok", [128, D], BF16)
        idn_sb = sb("idn_sb", [128, 128], BF16)
        bias_sb = sb("bias_sb", [128, NW], F32)
        cs_sb = sb("cs_sb", [128, 2 * TOK], F32)
        rtmp = sb("rtmp", [128, TOK], F32)
        swp = [sb(f"swp{i}", [128, TOK], BF16) for i in range(2)]
        wvbuf = [sb(f"wvbuf{i}", [128, ND * 512], BF16) for i in range(2)]
        wbuf = wvbuf
        stq = [sb(f"stq{i}", [128, TOK], BF16) for i in range(2)]
        stq2 = [sb(f"stq2{i}", [128, TOK], BF16) for i in range(2)]
        bvb = [sb(f"bvb{i}", [128, 512], BF16) for i in range(2)]
        vstage = [sb(f"vstage{i}", [128, 512], BF16) for i in range(8)]
        ps_ln = [ps(f"ps_ln{i}", [128, 1024], BF16) for i in range(2)]
        ps_a = ps("ps_a", [128, 512], F32)
        ps_b = ps("ps_b", [128, 512], F32)
        ps_v = [ps(f"ps_v{i}", [128, 512], F32) for i in range(4)]
        blk = st.enter_context(nc.Block())
        S = {n: sem(n) for n in
             ["cload", "vch", "ach", "xload0", "xload1", "var", "sqv", "xn",
              "tp", "tpc", "wload0", "wload1", "mm", "ev", "rope", "sw0", "sw1",
              "stq_s0", "stq_s1", "vwload0", "vwload1", "vmm", "vev", "flst"]
             + [f"vst{i}" for i in range(8)]}

        @blk.sync
        def _(s):
            s.dma_start(out=idn_sb[:, :], in_=idn[:, :]).then_inc(S["cload"], 16)
            s.dma_start(out=bias_sb[:, 0:ND], in_=bq1[:, :]).then_inc(S["cload"], 16)
            s.dma_start(out=bias_sb[:, ND:NW], in_=bk1[:, :]).then_inc(S["cload"], 16)
            s.dma_start(out=cs_sb[0:64, 0:TOK], in_=cosx[:, :]).then_inc(S["cload"], 16)
            s.dma_start(out=cs_sb[64:128, 0:TOK], in_=cosx[:, :]).then_inc(S["cload"], 16)
            s.dma_start(out=cs_sb[0:64, TOK:2 * TOK], in_=sinx[:, :]).then_inc(S["cload"], 16)
            s.dma_start(out=cs_sb[64:128, TOK:2 * TOK], in_=sinx[:, :]).then_inc(S["cload"], 16)
            for i in range(NT):
                if i >= 2:
                    s.wait_ge(S["xn"], i - 1)
                s.dma_start(out=x_t[i % 2][:, :], in_=x_i[i * 128:(i + 1) * 128, :]).then_inc(S[f"xload{i % 2}"], 16)
            sd_ctx = s.register(pre + "sd") if fused else None
            sd = sd_ctx.__enter__() if fused else None
            if fused:
                s.reg_load(sd, side_sb[0:1, 0:1])

            def qk_store(ot):
                s.wait_ge(S["rope"], ot + 1)
                if ot < ND:
                    s.dma_start(out=qT_o[ot * 128:(ot + 1) * 128, :],
                                in_=stq2[ot % 2][:, :]).then_inc(S[f"stq_s{ot % 2}"], 16)
                elif not fused:
                    s.dma_start(out=kT_o[(ot - ND) * 128:(ot - ND + 1) * 128, :],
                                in_=stq2[ot % 2][:, :]).then_inc(S[f"stq_s{ot % 2}"], 16)
                else:
                    kt = ot - ND
                    with s.If_eq(sd, 0):
                        s.dma_start(out=k0[kt * 128:(kt + 1) * 128, :],
                                    in_=stq2[ot % 2][:, :]).then_inc(S[f"stq_s{ot % 2}"], 16)
                    with s.Else():
                        s.dma_start(out=k1[kt * 128:(kt + 1) * 128, :],
                                    in_=stq2[ot % 2][:, :]).then_inc(S[f"stq_s{ot % 2}"], 16)
            for ot in range(NW):
                s.wait_ge(S["ev"], 2 * ot + 2)
                if ot >= 2:
                    s.wait_ge(S["rope"], ot - 1)
                s.dma_start(out=swp[ot % 2][0:64, :], in_=stq[ot % 2][64:128, :]).then_inc(S[f"sw{ot % 2}"], 16)
                s.dma_start(out=swp[ot % 2][64:128, :], in_=stq[ot % 2][0:64, :]).then_inc(S[f"sw{ot % 2}"], 16)
                if ot >= 1:
                    qk_store(ot - 1)
            qk_store(NW - 1)
            for chn in range(NCH):
                for tt in range(8):
                    s.wait_ge(S["vev"], chn * 8 + tt + 1)
                    if fused:
                        with s.If_eq(sd, 0):
                            s.dma_start(out=v0[tt * 128:(tt + 1) * 128, chn * 512:(chn + 1) * 512],
                                        in_=vstage[tt][:, :]).then_inc(S[f"vst{tt}"], 16)
                        with s.Else():
                            s.dma_start(out=v1[tt * 128:(tt + 1) * 128, chn * 512:(chn + 1) * 512],
                                        in_=vstage[tt][:, :]).then_inc(S[f"vst{tt}"], 16)
                    else:
                        s.dma_start(out=v_o[tt * 128:(tt + 1) * 128, chn * 512:(chn + 1) * 512],
                                    in_=vstage[tt][:, :]).then_inc(S[f"vst{tt}"], 16)
            for tt in range(8):
                s.wait_ge(S[f"vst{tt}"], 16 * NCH)
            for p in range(2):
                s.wait_ge(S[f"stq_s{p}"], 16 * (NW // 2))
            if fused:
                with s.If_eq(sd, 0):
                    s.dma_start(out=fl0[:, :], in_=flags_sb[0:1, FLC * 16:(FLC + 1) * 16]).then_inc(S["flst"], 16)
                with s.Else():
                    s.dma_start(out=fl1[:, :], in_=flags_sb[0:1, FLC * 16:(FLC + 1) * 16]).then_inc(S["flst"], 16)
                s.wait_ge(S["flst"], 16)
                sd_ctx.__exit__(None, None, None)

        @blk.gpsimd
        def _(g):
            for widx in range(NW):
                if widx >= 2:
                    g.wait_ge(S["ev"], 2 * widx - 2)
                wT = wq1 if widx < ND else wk1
                wout = D if widx < ND else KVD
                ot = widx if widx < ND else widx - ND
                for part in range(4):
                    srcw = bass.AP(wT, (part * 8 * 128) * wout + ot * 128,
                                   [[wout, 128], [128 * wout, 8], [1, 128]])
                    g.dma_start(out=wbuf[widx % 2][:, part * 1024:(part + 1) * 1024],
                                in_=srcw).then_inc(S[f"wload{widx % 2}"], 16)
            for chn in range(NCH):
                if chn < 2:
                    g.wait_ge(S["mm"], 2 * NW)
                if chn >= 2:
                    g.wait_ge(S["vev"], (chn - 1) * 8)
                g.dma_start(out=bvb[chn % 2][:, :],
                            in_=bass.AP(bv1, chn * 512, [[0, 128], [1, 512]])).then_inc(S[f"vwload{chn % 2}"], 16)
                for part in range(4):
                    srcw = bass.AP(wv1, (part * 8 * 128) * KVD + chn * 512,
                                   [[KVD, 128], [128 * KVD, 8], [1, 512]])
                    g.dma_start(out=wvbuf[chn % 2][:, part * 8 * 512:(part + 1) * 8 * 512],
                                in_=srcw).then_inc(S[f"vwload{chn % 2}"], 16)

        @blk.vector
        def _(v):
            ch = Ch(v, S["vch"])
            ch.wait(S["cload"], 16 * 7)
            xn_n = 0
            tpc_n = 0
            for i in range(NT):
                c = 8 * i
                ch.wait(S[f"xload{i % 2}"], 16 * (i // 2 + 1))
                for ck in range(8):
                    ch.op(v.bn_stats, bnbuf[:, ck * 6:(ck + 1) * 6],
                          x_t[i % 2][:, ck * 512:(ck + 1) * 512])
                ch.op(v.bn_aggr, stat[:, c:c + 2], bnbuf[:, :])
                ch.op_m(S["var"], i + 1, v.tensor_scalar, stat[:, c + 2:c + 3],
                        stat[:, c + 1:c + 2], LN_EPS, None, ALU.add)
                ch.wait(S["sqv"], i + 1)
                ch.op(v.reciprocal, stat[:, c + 7:c + 8], stat[:, c + 6:c + 7])
                if i >= 1:
                    ch.wait(S["tp"], 4 * i)
                xn_n += 1
                ch.op_m(S["xn"], xn_n, v.tensor_scalar, xn_tok[:, :], x_t[i % 2][:, :],
                        stat[:, c:c + 1], stat[:, c + 7:c + 8], ALU.subtract, ALU.mult)
                for gidx in range(4):
                    gi = i * 4 + gidx
                    ch.wait(S["tp"], gi + 1)
                    tpc_n += 1
                    ch.op_m(S["tpc"], tpc_n, v.tensor_copy,
                            xnT_r[:, gidx * 8:(gidx + 1) * 8, i * 128:(i + 1) * 128],
                            ps_ln[gi % 2][:, :])
            for ot in range(NW):
                ch.wait(S["ev"], 2 * ot + 2)
                ch.wait(S[f"sw{ot % 2}"], 32 * (ot // 2 + 1))
                if ot >= 2:
                    ch.wait(S[f"stq_s{ot % 2}"], 16 * (ot // 2))
                sg, sg2, sw = stq[ot % 2], stq2[ot % 2], swp[ot % 2]
                cosb, sinb = cs_sb[:, 0:TOK], cs_sb[:, TOK:2 * TOK]
                ch.op(v.tensor_tensor, rtmp[0:64, :], sg[0:64, :], cosb[0:64, :], ALU.mult)
                ch.op(v.tensor_tensor, sg2[0:64, :], sw[0:64, :], sinb[0:64, :], ALU.mult)
                ch.op(v.tensor_tensor, sg2[0:64, :], rtmp[0:64, :], sg2[0:64, :], ALU.subtract)
                ch.op(v.tensor_tensor, rtmp[64:128, :], sw[64:128, :], sinb[64:128, :], ALU.mult)
                ch.op(v.tensor_tensor, sg2[64:128, :], sg[64:128, :], cosb[64:128, :], ALU.mult)
                rp = ot + 1
                ch.op_m(S["rope"], rp, v.tensor_tensor, sg2[64:128, :],
                        rtmp[64:128, :], sg2[64:128, :], ALU.add)
            vev_n = 0
            for chn in range(NCH):
                for w in range(2):
                    wv_idx = chn * 2 + w
                    ch.wait(S["vmm"], wv_idx + 1)
                    for t4 in range(4):
                        tt = w * 4 + t4
                        if chn >= 1:
                            ch.wait(S[f"vst{tt}"], 16 * chn)
                        vev_n += 1
                        ch.op_m(S["vev"], vev_n, v.tensor_tensor, vstage[tt][:, :],
                                ps_v[t4][:, 0:512], bvb[chn % 2][:, :], ALU.add)

        @blk.scalar
        def _(a):
            ch = Ch(a, S["ach"])
            for i in range(NT):
                c = 8 * i
                ch.wait(S["var"], i + 1)
                ch.op_m(S["sqv"], i + 1, a.activation, stat[:, c + 6:c + 7],
                        stat[:, c + 2:c + 3], AF.Sqrt)
            ev_n = 0
            for ot in range(NW):
                for tc in range(2):
                    gidx = ot * 2 + tc
                    ch.wait(S["mm"], gidx + 1)
                    if ot >= 2 and tc == 0:
                        ch.wait(S["rope"], ot - 1)
                    ev_n += 1
                    ch.op_m(S["ev"], ev_n, a.activation,
                            stq[ot % 2][:, tc * 512:(tc + 1) * 512],
                            (ps_a if gidx % 2 == 0 else ps_b)[:, :],
                            AF.Identity, bias=bias_sb[:, ot:ot + 1])

        @blk.tensor
        def _(t):
            t.wait_ge(S["cload"], 16 * 7)
            for i in range(NT):
                t.wait_ge(S["xn"], i + 1)
                for gidx in range(4):
                    gi = i * 4 + gidx
                    if gi >= 2:
                        t.wait_ge(S["tpc"], gi - 1)
                    for j in range(8):
                        ins = t.transpose(ps_ln[gi % 2][:, j * 128:(j + 1) * 128],
                                          xn_tok[:, (gidx * 8 + j) * 128:(gidx * 8 + j + 1) * 128],
                                          idn_sb[:, :])
                        if j == 7:
                            ins.then_inc(S["tp"], 1)
            t.wait_ge(S["tpc"], 32)
            for widx in range(NW):
                t.wait_ge(S[f"wload{widx % 2}"], 64 * (widx // 2 + 1))
                for tc in range(2):
                    gidx = widx * 2 + tc
                    if gidx >= 2:
                        t.wait_ge(S["ev"], gidx - 1)
                    for in_ in range(ND):
                        ins = t.matmul((ps_a if gidx % 2 == 0 else ps_b)[:, :],
                                       wbuf[widx % 2][:, in_ * 128:(in_ + 1) * 128],
                                       xnT[:, in_ * TOK + tc * 512:in_ * TOK + (tc + 1) * 512],
                                       start=(in_ == 0), stop=(in_ == ND - 1))
                        if in_ == ND - 1:
                            ins.then_inc(S["mm"], 1)
            t.wait_ge(S["ev"], 2 * NW)
            for chn in range(NCH):
                t.wait_ge(S[f"vwload{chn % 2}"], 80 * (chn // 2 + 1))
                for w in range(2):
                    wv_idx = chn * 2 + w
                    if wv_idx >= 1:
                        t.wait_ge(S["vev"], wv_idx * 4)
                    for in_ in range(ND):
                        for t4 in range(4):
                            tt = w * 4 + t4
                            ins = t.matmul(ps_v[t4][:, 0:512],
                                           xnT[:, in_ * TOK + tt * 128:in_ * TOK + (tt + 1) * 128],
                                           wvbuf[chn % 2][:, in_ * 512:(in_ + 1) * 512],
                                           start=(in_ == 0), stop=(in_ == ND - 1))
                            if in_ == ND - 1 and t4 == 3:
                                ins.then_inc(S["vmm"], 1)


def _prep_host(inputs):
    d = {}
    x = np.asarray(inputs["L0_attn_x"], np.float32).reshape(NCORES * TOK, D)
    d["x_sh"] = [np.ascontiguousarray(x[c * TOK:(c + 1) * TOK]) for c in range(NCORES)]
    d["wqT"] = np.ascontiguousarray(np.asarray(inputs["L0_attn_Wq"], np.float32).T)
    d["wkT"] = np.ascontiguousarray(np.asarray(inputs["L0_attn_Wk"], np.float32).T)
    d["wvT"] = np.ascontiguousarray(np.asarray(inputs["L0_attn_Wv"], np.float32).T)
    d["bq"] = np.ascontiguousarray(np.asarray(inputs["L0_attn_bq"], np.float32).reshape(ND, 128).T)
    d["bk"] = np.ascontiguousarray(np.asarray(inputs["L0_attn_bk"], np.float32).reshape(ND, 128).T)
    d["bv"] = np.ascontiguousarray(np.asarray(inputs["L0_attn_bv"], np.float32).reshape(1, D))
    d["bwf"] = np.ascontiguousarray(np.asarray(inputs["L0_attn_bias_weight"], np.float32).reshape(1, 48))
    d["kdkg"] = _geo_consts()
    d["idn"] = np.eye(128, dtype=BF)
    d["idnf"] = np.eye(128, dtype=np.float32)
    codesT = np.ascontiguousarray(np.asarray(inputs["L0_ffn_w_codes"], np.int32).T)
    d["codes_sh"] = [np.ascontiguousarray(codesT[c * 512:(c + 1) * 512]) for c in range(NCORES)]
    amxT = np.ascontiguousarray(np.asarray(inputs["L0_ffn_absmax"], np.float32).T)
    d["amx_sh"] = [np.ascontiguousarray(amxT[c * 8:(c + 1) * 8]) for c in range(NCORES)]
    wa = np.asarray(inputs["L0_ffn_W_a"], np.float32)
    d["waT"] = np.ascontiguousarray(wa.reshape(ND, 128, 16).transpose(1, 0, 2).reshape(128, 512))
    d["wbT"] = np.ascontiguousarray(np.asarray(inputs["L0_ffn_W_b"], np.float32))
    d["wq1"] = np.ascontiguousarray(np.asarray(inputs["L1_attn_wq"], np.float32))
    d["wk1"] = np.ascontiguousarray(np.asarray(inputs["L1_attn_wk"], np.float32))
    d["wv1"] = np.ascontiguousarray(np.asarray(inputs["L1_attn_wv"], np.float32))
    d["bq1"] = np.ascontiguousarray(np.asarray(inputs["L1_attn_bq"], np.float32).reshape(ND, 128).T)
    d["bk1"] = np.ascontiguousarray(np.asarray(inputs["L1_attn_bk"], np.float32).reshape(8, 128).T)
    d["bv1"] = np.ascontiguousarray(np.asarray(inputs["L1_attn_bv"], np.float32).reshape(1, 1024))
    cosT, sinT = _rope_tables()
    d["cos_sh"] = [np.ascontiguousarray(cosT[:, (c % 2) * TOK:(c % 2 + 1) * TOK]) for c in range(NCORES)]
    d["sin_sh"] = [np.ascontiguousarray(sinT[:, (c % 2) * TOK:(c % 2 + 1) * TOK]) for c in range(NCORES)]
    d["w1T"] = np.ascontiguousarray(np.asarray(inputs["L1_ffn_weight"], np.float16).T)
    d["b1row"] = np.ascontiguousarray(np.asarray(inputs["L1_ffn_bias"], np.float16).reshape(1, D))
    return d


def _pair_concat(arrs, ncores, axis):
    out = []
    for c in range(ncores):
        b = c // 2
        out.append(np.concatenate([arrs[2 * b], arrs[2 * b + 1]], axis=axis))
    return out


def _run_phases(d, runner, ncores=NCORES):
    idn = d["idn"]
    resA = runner(build_phaseA(), [
        {"x": d["x_sh"][c], "wqT": d["wqT"], "wkT": d["wkT"], "wvT": d["wvT"],
         "bq": d["bq"], "bk": d["bk"], "bv": d["bv"], "bwf": d["bwf"],
         "kdkg": d["kdkg"], "idn": idn} for c in range(ncores)])
    kT_f = _pair_concat([resA[c]["kT"] for c in range(ncores)], ncores, 1)
    v_f = _pair_concat([resA[c]["v"] for c in range(ncores)], ncores, 0)
    resB = runner(build_attn(0), [
        {"qT": resA[c]["qT"], "kT": kT_f[c], "v": v_f[c],
         "x": d["x_sh"][c], "idn": idn} for c in range(ncores)])
    resC = runner(build_dequant(), [
        {"codes": d["codes_sh"][c], "amx": d["amx_sh"][c]} for c in range(ncores)])
    wTfull = np.concatenate([resC[c]["wsh"] for c in range(ncores)], axis=0)
    if ncores < NCORES:
        pass
    resD = runner(build_ffn(0), [
        {"x": resB[c]["x1"], "wT": wTfull, "waT": d["waT"], "wbT": d["wbT"],
         "idn": idn} for c in range(ncores)])
    resE = runner(build_l1qkv(), [
        {"x": resD[c]["xo"], "wq1": d["wq1"], "wk1": d["wk1"], "wv1": d["wv1"],
         "bq1": d["bq1"], "bk1": d["bk1"], "bv1": d["bv1"],
         "cosx": d["cos_sh"][c], "sinx": d["sin_sh"][c], "idn": idn}
        for c in range(ncores)])
    kT1_f = _pair_concat([resE[c]["kT1"] for c in range(ncores)], ncores, 1)
    v1_f = _pair_concat([resE[c]["v1"] for c in range(ncores)], ncores, 0)
    resF = runner(build_attn(1), [
        {"qT": resE[c]["qT1"], "kT": kT1_f[c], "v": v1_f[c],
         "x": resD[c]["xo"], "idn": idn} for c in range(ncores)])
    resG = runner(build_ffn(1), [
        {"x": resF[c]["x1"], "wT": d["w1T"], "brow": d["b1row"], "idn": idn}
        for c in range(ncores)])
    return [resG[c]["xo"] for c in range(ncores)]


def _device_runner(nc, in_maps):
    res = run_bass_kernel_spmd(nc, in_maps, core_ids=list(range(len(in_maps))))
    return res.results




def build_fused(upto=7, detect_races=True):
    import os
    _mask = os.environ.get("FUSE_PHASES")
    _sel = set(int(x) for x in _mask.split(",")) if _mask else None
    def _on(i):
        return (i in _sel) if _sel is not None else True
    nc = bass.Bass("TRN2", target_bir_lowering=False,
                   detect_race_conditions=detect_races)
    KVD = L1_KV * L1_HD
    t = {}
    EI = lambda n, sh, dt: nc.dram_tensor(n, sh, dt, kind="ExternalInput")
    t["x"] = EI("x", [TOK, D], F32)
    t["wqT"] = EI("wqT", [D, D], F32)
    t["wkT"] = EI("wkT", [D, D], F32)
    t["wvT"] = EI("wvT", [D, D], F32)
    t["bq"] = EI("bq", [128, ND], F32)
    t["bk"] = EI("bk", [128, ND], F32)
    t["bv"] = EI("bv", [1, D], F32)
    t["bwf"] = EI("bwf", [1, 48], F32)
    t["kdkg"] = EI("kdkg", [512, 128], F32)
    t["idn"] = EI("idn", [128, 128], BF16)
    t["idnf"] = EI("idnf", [128, 128], F32)
    t["codes"] = EI("codes", [2048, D], I32)
    t["amx"] = EI("amx", [32, D], F32)
    t["waT"] = EI("waT", [128, 512], F32)
    t["wbT"] = EI("wbT", [16, D], F32)
    t["wq1"] = EI("wq1", [D, D], F32)
    t["wk1"] = EI("wk1", [D, KVD], F32)
    t["wv1"] = EI("wv1", [D, KVD], F32)
    t["bq1"] = EI("bq1", [128, ND], F32)
    t["bk1"] = EI("bk1", [128, 8], F32)
    t["bv1"] = EI("bv1", [1, KVD], F32)
    t["cosx"] = EI("cosx", [64, TOK], F32)
    t["sinx"] = EI("sinx", [64, TOK], F32)
    t["w1T"] = EI("w1T", [D, D], F16)
    t["b1row"] = EI("b1row", [1, D], F16)
    t["side"] = EI("side", [1, 1], I32)
    t["flags"] = EI("flags", [1, 64], I32)
    t["out"] = nc.dram_tensor("out", [TOK, D], F32, kind="ExternalOutput")
    t["qT"] = nc.dram_tensor("qT", [D, TOK], BF16)
    t["x1"] = nc.dram_tensor("x1", [TOK, D], F32)
    t["x2"] = nc.dram_tensor("x2", [TOK, D], F32)
    t["x3"] = nc.dram_tensor("x3", [TOK, D], F32)
    t["qT1"] = nc.dram_tensor("qT1", [D, TOK], BF16)
    SH = lambda n, sh, dt: nc.dram_tensor(n, sh, dt, addr_space="Shared")
    t["k0"] = SH("k0", [D, TOK], BF16)
    t["k1"] = SH("k1", [D, TOK], BF16)
    t["v0"] = SH("v0", [TOK, D], BF16)
    t["v1"] = SH("v1", [TOK, D], BF16)
    t["wT_sh"] = SH("wT_sh", [D, D], BF16)
    t["k10"] = SH("k10", [KVD, TOK], BF16)
    t["k11"] = SH("k11", [KVD, TOK], BF16)
    t["v10"] = SH("v10", [TOK, KVD], BF16)
    t["v11"] = SH("v11", [TOK, KVD], BF16)
    for n in ["flA0", "flA1", "flW0", "flW1", "flE0", "flE1"]:
        t[n] = SH(n, [1, 16], I32)

    with ExitStack() as sst:
        side_sb = sst.enter_context(nc.sbuf_tensor("side_sb", [1, 1], I32))
        flags_sb = sst.enter_context(nc.sbuf_tensor("flags_sb", [1, 64], I32))
        poll_sb = sst.enter_context(nc.sbuf_tensor("poll_sb", [1, 16], I32))
        isem = sst.enter_context(nc.semaphore("init_sem"))
        pool = {}
        uniq = bool(os.environ.get("FUSE_DBG_UNIQSEM"))
        phase_tag = [""]
        SWDGE_NAMES = {"wload0", "wload1", "vwload0", "vwload1", "klo0", "klo1",
                       "psem", "cload2", "qlo0", "qlo1", "dcl0", "dcl1",
                       "dst0", "dst1", "dflst"}

        def sfp(name):
            key = (phase_tag[0] + name) if (uniq or name in SWDGE_NAMES) else name
            if key not in pool:
                pool[key] = sst.enter_context(nc.semaphore("P_" + key))
            return pool[key]

        def clear_sems():
            if uniq:
                return
            with nc.Block() as bc:
                @bc.sync
                def _(s):
                    for k, h in pool.items():
                        if not any(k.endswith(n) and len(k) == len(n) + 1 or k == n
                                   for n in SWDGE_NAMES):
                            s.sem_clear(h)
        with nc.Block() as b0:
            @b0.sync
            def _(s):
                s.dma_start(out=side_sb[:, :], in_=t["side"][:, :]).then_inc(isem, 16)
                s.dma_start(out=flags_sb[:, :], in_=t["flags"][:, :]).then_inc(isem, 16)
                s.wait_ge(isem, 32)
        com = {"side_sb": side_sb, "flags_sb": flags_sb, "poll_sb": poll_sb}
        TA = {**com, "x": t["x"], "wqT": t["wqT"], "wkT": t["wkT"], "wvT": t["wvT"],
              "bq": t["bq"], "bk": t["bk"], "bv": t["bv"], "bwf": t["bwf"],
              "kdkg": t["kdkg"], "idn": t["idn"], "qT": t["qT"],
              "k0": t["k0"], "k1": t["k1"], "v0": t["v0"], "v1": t["v1"],
              "flA0": t["flA0"], "flA1": t["flA1"], "flcA": 0}
        TCA = {"codes": t["codes"], "amx": t["amx"], "wT_sh": t["wT_sh"]}
        if upto >= 1 and _on(1):
            phase_tag[0] = "A"
            emit_phaseA(nc, sfp, TA, fused=True, pre="A_", deq=TCA)
        TB = {**com, "qT": t["qT"], "x": t["x"], "idn": t["idn"], "idnf": t["idnf"], "x1": t["x1"],
              "k0": t["k0"], "k1": t["k1"], "v0": t["v0"], "v1": t["v1"],
              "fl0": t["flA0"], "fl1": t["flA1"], "flc": 0}
        TC = {"codes": t["codes"], "amx": t["amx"], "wT_sh": t["wT_sh"],
              "flW0": t["flW0"], "flW1": t["flW1"], "flcW": 1}
        if upto >= 2 and _on(2):
            phase_tag[0] = "B"
            clear_sems()
            emit_attn(nc, sfp, TB, 0, fused=True, pre="B_", deq={**TC, "t0": 8})
        import os
        if os.environ.get("FUSE_DBG_LOCALW"):
            t["wtest"] = nc.dram_tensor("wtest", [D, D], BF16)
            _wD = t["wtest"]
        else:
            _wD = t["wT_sh"]
        TD = {**com, "x": t["x1"], "wT": _wD, "waT": t["waT"], "wbT": t["wbT"],
              "idn": t["idn"], "xo": t["x2"],
              "flW0": t["flW0"], "flW1": t["flW1"], "flcW": 1}
        if upto >= 4 and _on(4):
            phase_tag[0] = "D"
            clear_sems()
        if upto >= 4 and _on(4):
            emit_ffn(nc, sfp, TD, 0, fused=True, pre="D_")
        TE = {**com, "x": t["x2"], "wq1": t["wq1"], "wk1": t["wk1"], "wv1": t["wv1"],
              "bq1": t["bq1"], "bk1": t["bk1"], "bv1": t["bv1"],
              "cosx": t["cosx"], "sinx": t["sinx"], "idn": t["idn"],
              "qT1": t["qT1"], "k10": t["k10"], "k11": t["k11"],
              "v10": t["v10"], "v11": t["v11"],
              "flE0": t["flE0"], "flE1": t["flE1"], "flcE": 2}
        if upto >= 5 and _on(5):
            phase_tag[0] = "E"
            clear_sems()
        if upto >= 5 and _on(5):
            emit_l1qkv(nc, sfp, TE, fused=True, pre="E_")
        TF = {**com, "qT": t["qT1"], "x": t["x2"], "idn": t["idn"], "idnf": t["idnf"], "x1": t["x3"],
              "k0": t["k10"], "k1": t["k11"], "v0": t["v10"], "v1": t["v11"],
              "fl0": t["flE0"], "fl1": t["flE1"], "flc": 2}
        if upto >= 6 and _on(6):
            phase_tag[0] = "F"
            clear_sems()
        if upto >= 6 and _on(6):
            emit_attn(nc, sfp, TF, 1, fused=True, pre="F_")
        TG = {"x": t["x3"], "wT": t["w1T"], "brow": t["b1row"],
              "idn": t["idn"], "xo": t["out"]}
        if upto >= 7 and _on(7):
            phase_tag[0] = "G"
            clear_sems()
        if upto >= 7 and _on(7):
            emit_ffn(nc, sfp, TG, 1, fused=False, pre="G_")
    return nc


def _fused_in_maps(d):
    rng = np.random.default_rng()
    nonce = int(rng.integers(1, 2 ** 30))
    flags = np.zeros((1, 64), np.int32)
    for k in range(4):
        flags[0, k * 16:(k + 1) * 16] = nonce + k + 1
    maps = []
    for c in range(NCORES):
        s = c % 2
        maps.append({
            "x": d["x_sh"][c], "wqT": d["wqT"], "wkT": d["wkT"], "wvT": d["wvT"],
            "bq": d["bq"], "bk": d["bk"], "bv": d["bv"], "bwf": d["bwf"],
            "kdkg": d["kdkg"], "idn": d["idn"],
            "codes": d["codes_pair"][s], "amx": d["amx_pair"][s],
            "waT": d["waT"], "wbT": d["wbT"],
            "wq1": d["wq1"], "wk1": d["wk1"], "wv1": d["wv1"],
            "bq1": d["bq1"], "bk1": d["bk1"], "bv1": d["bv1"],
            "cosx": d["cos_sh"][c], "sinx": d["sin_sh"][c],
            "w1T": d["w1T"], "b1row": d["b1row"], "idnf": d["idnf"],
            "side": np.array([[s]], np.int32), "flags": flags,
        })
    return maps


def kernel(**inputs):
    d = _prep_host(inputs)
    codesT = np.concatenate(d["codes_sh"], axis=0)
    amxT = np.concatenate(d["amx_sh"], axis=0)
    d["codes_pair"] = [np.ascontiguousarray(codesT[s * 2048:(s + 1) * 2048]) for s in range(2)]
    d["amx_pair"] = [np.ascontiguousarray(amxT[s * 32:(s + 1) * 32]) for s in range(2)]
    nc = build_fused()
    res = run_bass_kernel_spmd(nc, _fused_in_maps(d), core_ids=list(range(NCORES)))
    out = np.concatenate([np.asarray(res.results[c]["out"], np.float32) for c in range(NCORES)], axis=0)
    return out.reshape(4, SEQ, D)


def kernel_unfused(**inputs):
    d = _prep_host(inputs)
    shards = _run_phases(d, _device_runner)
    out = np.concatenate([np.asarray(s, np.float32) for s in shards], axis=0)
    return out.reshape(4, SEQ, D)



# version 15
# speedup vs baseline: 1.5255x; 1.3377x over previous
import numpy as np
import ml_dtypes
import concourse.bass as bass
import concourse.mybir as mybir
from concourse.bass_utils import run_bass_kernel_spmd
from contextlib import ExitStack

F32 = mybir.dt.float32
BF16 = mybir.dt.bfloat16
F16 = mybir.dt.float16
I32 = mybir.dt.int32
AF = mybir.ActivationFunctionType
ALU = mybir.AluOpType
AX = mybir.AxisListType

NCORES = 8
TOK = 1024
D = 4096
NT = TOK // 128
ND = D // 128
SEQ = 2048
NK = SEQ // 128

L0_NH = 16
L0_HD = 256
L0_SCALE = 1.0 / 16.0
L1_H = 32
L1_KV = 8
L1_HD = 128

NF4 = [-1.0, -0.6961928009986877, -0.5250730514526367, -0.39491748809814453,
       -0.28444138169288635, -0.18477343022823334, -0.09105003625154495, 0.0,
       0.07958029955625534, 0.16093020141124725, 0.24611230194568634,
       0.33791524171829224, 0.44070982933044434, 0.5626170039176941,
       0.7229568362236023, 1.0]
FFN_SCALE = 2.0
LN_EPS = 1e-5
BF = ml_dtypes.bfloat16


def _geo_consts():
    g = [1.0, 1.0, 1.0]
    dim, n = 3, 8
    masks = sorted(range(n), key=lambda m: (bin(m).count("1"), m))
    inv = {m: i for i, m in enumerate(masks)}
    C = np.zeros((n, n, n), dtype=np.float64)
    for i, a in enumerate(masks):
        for j, b in enumerate(masks):
            sw, at = 0, a >> 1
            while at:
                sw += bin(at & b).count("1")
                at >>= 1
            s = -1.0 if sw & 1 else 1.0
            for t in range(dim):
                if a >> t & 1 and b >> t & 1:
                    s *= g[t]
            C[i, j, inv[a ^ b]] = s
    grades = np.array([bin(m).count("1") for m in masks])
    rev = np.where(grades * (grades - 1) // 2 % 2 == 1, -1.0, 1.0)
    met = np.array([np.prod([g[t] for t in range(dim) if m >> t & 1] + [1.0])
                    for m in masks])
    bivk = C[:, :, [4, 5, 6]]
    KD = np.kron(np.eye(16), L0_SCALE * np.diag(rev * met)).astype(np.float32)
    KG = [np.kron(np.eye(16), np.diag(rev) @ bivk[:, :, c]).astype(np.float32)
          for c in range(3)]
    return np.concatenate([KD] + KG, axis=0)


def _rope_tables():
    hd = L1_HD
    inv_freq = 1.0 / (10000.0 ** (np.arange(0, hd, 2, dtype=np.float64) / hd))
    ang = np.arange(SEQ, dtype=np.float64)[:, None] * inv_freq[None, :]
    return (np.ascontiguousarray(np.cos(ang).T).astype(np.float32),
            np.ascontiguousarray(np.sin(ang).T).astype(np.float32))




class Ch:

    def __init__(self, eng, sem):
        self.e, self.sem, self.n, self.last = eng, sem, 0, None

    def wait(self, sem, val):
        self.e.wait_ge(sem, val)

    def op(self, f, *a, **kw):
        if self.last is not None:
            self.e.wait_ge(*self.last)
        ins = f(*a, **kw)
        self.n += 1
        ins.then_inc(self.sem, 1)
        self.last = (self.sem, self.n)
        return ins

    def op_m(self, mile_sem, mile_val, f, *a, **kw):
        if self.last is not None:
            self.e.wait_ge(*self.last)
        ins = f(*a, **kw)
        ins.then_inc(mile_sem, 1)
        self.last = (mile_sem, mile_val)
        return ins


def _semf(nc, sst, prefix):
    def sf(name):
        return sst.enter_context(nc.semaphore(prefix + name))
    return sf


def build_phaseA():
    nc = bass.Bass("TRN2", target_bir_lowering=False)
    T = {}
    T["x"] = nc.dram_tensor("x", [TOK, D], F32, kind="ExternalInput")
    T["wqT"] = nc.dram_tensor("wqT", [D, D], F32, kind="ExternalInput")
    T["wkT"] = nc.dram_tensor("wkT", [D, D], F32, kind="ExternalInput")
    T["wvT"] = nc.dram_tensor("wvT", [D, D], F32, kind="ExternalInput")
    T["bq"] = nc.dram_tensor("bq", [128, ND], F32, kind="ExternalInput")
    T["bk"] = nc.dram_tensor("bk", [128, ND], F32, kind="ExternalInput")
    T["bv"] = nc.dram_tensor("bv", [1, D], F32, kind="ExternalInput")
    T["bwf"] = nc.dram_tensor("bwf", [1, 48], F32, kind="ExternalInput")
    T["kdkg"] = nc.dram_tensor("kdkg", [512, 128], F32, kind="ExternalInput")
    T["idn"] = nc.dram_tensor("idn", [128, 128], BF16, kind="ExternalInput")
    T["qT"] = nc.dram_tensor("qT", [D, TOK], BF16, kind="ExternalOutput")
    T["kT"] = nc.dram_tensor("kT", [D, TOK], BF16, kind="ExternalOutput")
    T["v"] = nc.dram_tensor("v", [TOK, D], BF16, kind="ExternalOutput")
    with ExitStack() as sst:
        emit_phaseA(nc, _semf(nc, sst, "A"), T, fused=False)
    return nc


def emit_phaseA(nc, sf, T, fused, pre="", deq=None):
    x_in, wqT, wkT, wvT = T["x"], T["wqT"], T["wkT"], T["wvT"]
    bq, bk, bv, bwf, kdkg, idn = T["bq"], T["bk"], T["bv"], T["bwf"], T["kdkg"], T["idn"]
    qT_o = T["qT"]
    if fused:
        k0, k1, v0, v1 = T["k0"], T["k1"], T["v0"], T["v1"]
        side_sb, flags_sb, fl0, fl1 = T["side_sb"], T["flags_sb"], T["flA0"], T["flA1"]
        FLC = T["flcA"]
    else:
        kT_o, v_o = T["kT"], T["v"]

    with ExitStack() as st:
        sb = lambda nm, sh, dt: st.enter_context(nc.sbuf_tensor(pre + nm, sh, dt))
        ps = lambda nm, sh, dt: st.enter_context(nc.psum_tensor(pre + nm, sh, dt))
        sem = sf

        xnT = sb("xnT", [128, ND * TOK], BF16)
        xnT_r = xnT.ap().rearrange("p (f t) -> p f t", t=TOK)
        x_t = [sb(f"x_t{i}", [128, D], F32) for i in range(2)]
        if deq is not None:
            xa0 = nc.lookup_mloc(pre + "x_t0").addr
            xa1 = nc.lookup_mloc(pre + "x_t1").addr
            dq_cbuf = [nc.alloc_sbuf_tensor_at(pre + f"dq_cbuf{i}", [128, 2048], I32,
                                               offset=xa0 + i * 8192) for i in range(2)]
            dq_mbuf = [nc.alloc_sbuf_tensor_at(pre + f"dq_mbuf{i}", [128, 2048], F32,
                                               offset=xa1 + i * 8192) for i in range(2)]
            dq_acc = [sb(f"dq_acc{i}", [128, 2048], BF16) for i in range(2)]
            dq_codes, dq_amx, dq_out = deq["codes"], deq["amx"], deq["wT_sh"]
        stat = sb("stat", [128, 8 * NT], F32)
        bnbuf = sb("bnbuf", [128, 48], F32)
        xn_tok = sb("xn_tok", [128, D], BF16)
        if deq is not None:
            xn_addr = nc.lookup_mloc(pre + "xn_tok").addr
            dq_cf = nc.alloc_sbuf_tensor_at(pre + "dq_cf", [128, 2048], BF16,
                                            offset=xn_addr)
            dq_selt = nc.alloc_sbuf_tensor_at(pre + "dq_selt", [128, 2048], BF16,
                                              offset=xn_addr + 4096)
        idn_sb = sb("idn_sb", [128, 128], BF16)
        bias_sb = sb("bias_sb", [128, 2 * ND], F32)
        bvb = [sb(f"bvb{i}", [128, 512], BF16) for i in range(2)]
        bw_sb = sb("bw_sb", [128, 48], F32)
        if deq is None:
            kdkg_sb = sb("kdkg_sb", [128, 512], F32)
            btmp = sb("btmp", [128, 256], F32)
        bmat = sb("bmat", [128, L0_NH * 128], BF16)
        wvbuf = [sb(f"wvbuf{i}", [128, ND * 512], BF16) for i in range(2)]
        wbuf = [wvbuf[i][:, 0:D] if False else wvbuf[i] for i in range(2)]
        stq = [sb(f"stq{i}", [128, TOK], BF16) for i in range(2)]
        stq2 = [sb(f"stq2{i}", [128, TOK], BF16) for i in range(2)]
        stk = [sb(f"stk{i}", [128, TOK], BF16) for i in range(2)]
        vstage = [sb(f"vstage{i}", [128, 512], BF16) for i in range(8)]
        if deq is not None:
            btmp = nc.alloc_sbuf_tensor_at(pre + "btmp", [128, 256], F32,
                                           offset=nc.lookup_mloc(pre + "vstage0").addr)
            kdkg_sb = nc.alloc_sbuf_tensor_at(pre + "kdkg_sb", [128, 512], F32,
                                              offset=nc.lookup_mloc(pre + "vstage2").addr)

        ps_ln = [ps(f"ps_ln{i}", [128, 1024], BF16) for i in range(2)]
        ps_a = ps("ps_a", [128, 512], F32)
        ps_b = ps("ps_b", [128, 512], F32)
        ps_v = [ps(f"ps_v{i}", [128, 512], F32) for i in range(4)]

        blk = st.enter_context(nc.Block())
        S = {n: sem(n) for n in
             ["cload", "vch", "ach", "xload0", "xload1", "var", "sqv", "xn",
              "tp", "tpc", "wload0", "wload1", "mm", "ev", "foldmm", "fev",
              "stq_s0", "stq_s1", "stk_s0", "stk_s1", "vwload0", "vwload1",
              "vmm", "vev", "flst"]
             + [f"vst{i}" for i in range(8)]}
        if deq is not None:
            for n in ["dwd", "qdcl0", "qdcl1", "qdst0", "qdst1", "dcl0", "dcl1"]:
                S[n] = sem(n)

        def dq_srcs(j):
            r, hf = j // 2, j % 2
            return (dq_codes[r * 128:(r + 1) * 128, hf * 2048:(hf + 1) * 2048],
                    bass.AP(dq_amx, (r * 2) * D + hf * 2048,
                            [[D, 2], [0, 64], [1, 2048]]))

        @blk.sync
        def _(s):
            sd_ctx = s.register(pre + "sd") if fused else None
            sd = sd_ctx.__enter__() if fused else None
            if fused:
                s.reg_load(sd, side_sb[0:1, 0:1])

            def dq_store(j):
                r, hf = j // 2, j % 2
                cs = slice(hf * 2048, (hf + 1) * 2048)
                with s.If_eq(sd, 0):
                    s.dma_start(out=dq_out[r * 128:(r + 1) * 128, cs],
                                in_=dq_acc[j % 2][:, :]).then_inc(S[f"qdst{j % 2}"], 16)
                with s.Else():
                    s.dma_start(out=dq_out[2048 + r * 128:2048 + (r + 1) * 128, cs],
                                in_=dq_acc[j % 2][:, :]).then_inc(S[f"qdst{j % 2}"], 16)

            s.dma_start(out=idn_sb[:, :], in_=idn[:, :]).then_inc(S["cload"], 16)
            s.dma_start(out=bw_sb[:, :], in_=bass.AP(bwf, 0, [[0, 128], [1, 48]])).then_inc(S["cload"], 16)
            s.dma_start(out=kdkg_sb[:, :], in_=bass.AP(kdkg, 0, [[128, 128], [128 * 128, 4], [1, 128]])).then_inc(S["cload"], 16)
            s.dma_start(out=bias_sb[:, 0:ND], in_=bq[:, :]).then_inc(S["cload"], 16)
            s.dma_start(out=bias_sb[:, ND:2 * ND], in_=bk[:, :]).then_inc(S["cload"], 16)
            for i in range(NT):
                if i >= 2:
                    s.wait_ge(S["xn"], i - 1)
                s.dma_start(out=x_t[i % 2][:, :], in_=x_in[i * 128:(i + 1) * 128, :]).then_inc(S[f"xload{i % 2}"], 16)
            if deq is not None:
                s.wait_ge(S["xn"], NT)
                for j in range(2):
                    csrc, msrc = dq_srcs(j)
                    s.dma_start(out=dq_cbuf[j][:, :], in_=csrc).then_inc(S[f"qdcl{j}"], 16)
                    s.dma_start(out=dq_mbuf[j][:, :], in_=msrc).then_inc(S[f"qdcl{j}"], 16)
                for j in range(2, 16):
                    s.wait_ge(S["dwd"], j - 1)
                    csrc, msrc = dq_srcs(j)
                    s.dma_start(out=dq_cbuf[j % 2][:, :],
                                in_=csrc).then_inc(S[f"dcl{j % 2}"], 16)
                    s.dma_start(out=dq_mbuf[j % 2][:, :],
                                in_=msrc).then_inc(S[f"dcl{j % 2}"], 16)
                    dq_store(j - 2)
                s.wait_ge(S["dwd"], 15)
                dq_store(14)
                s.wait_ge(S["dwd"], 16)
                dq_store(15)
            def kdst(ot):
                return slice(ot * 128, (ot + 1) * 128)
            for ot in range(ND):
                s.wait_ge(S["ev"], 2 * ND + 2 * ot + 2)
                if fused:
                    with s.If_eq(sd, 0):
                        s.dma_start(out=k0[kdst(ot), :], in_=stk[ot % 2][:, :]).then_inc(S[f"stk_s{ot % 2}"], 16)
                    with s.Else():
                        s.dma_start(out=k1[kdst(ot), :], in_=stk[ot % 2][:, :]).then_inc(S[f"stk_s{ot % 2}"], 16)
                else:
                    s.dma_start(out=kT_o[kdst(ot), :], in_=stk[ot % 2][:, :]).then_inc(S[f"stk_s{ot % 2}"], 16)
            for ch in range(8):
                for tt in range(8):
                    s.wait_ge(S["vev"], ch * 8 + tt + 1)
                    if fused:
                        with s.If_eq(sd, 0):
                            s.dma_start(out=v0[tt * 128:(tt + 1) * 128, ch * 512:(ch + 1) * 512],
                                        in_=vstage[tt][:, :]).then_inc(S[f"vst{tt}"], 16)
                        with s.Else():
                            s.dma_start(out=v1[tt * 128:(tt + 1) * 128, ch * 512:(ch + 1) * 512],
                                        in_=vstage[tt][:, :]).then_inc(S[f"vst{tt}"], 16)
                    else:
                        s.dma_start(out=v_o[tt * 128:(tt + 1) * 128, ch * 512:(ch + 1) * 512],
                                    in_=vstage[tt][:, :]).then_inc(S[f"vst{tt}"], 16)
            if deq is not None:
                for p in range(2):
                    s.wait_ge(S[f"qdst{p}"], 16 * 8)
            for tt in range(8):
                s.wait_ge(S[f"vst{tt}"], 16 * 8)
            for p in range(2):
                s.wait_ge(S[f"stq_s{p}"], 16 * 16)
                s.wait_ge(S[f"stk_s{p}"], 16 * 16)
            if fused:
                with s.If_eq(sd, 0):
                    s.dma_start(out=fl0[:, :], in_=flags_sb[0:1, FLC * 16:(FLC + 1) * 16]).then_inc(S["flst"], 16)
                with s.Else():
                    s.dma_start(out=fl1[:, :], in_=flags_sb[0:1, FLC * 16:(FLC + 1) * 16]).then_inc(S["flst"], 16)
                s.wait_ge(S["flst"], 16)
                sd_ctx.__exit__(None, None, None)

        @blk.gpsimd
        def _(g):
            for widx in range(2 * ND):
                wT = wqT if widx < ND else wkT
                ot = widx % ND
                if widx >= 2:
                    g.wait_ge(S["ev"], 2 * widx - 2)
                for part in range(4):
                    src = bass.AP(wT, (part * 8 * 128) * D + ot * 128,
                                  [[D, 128], [128 * D, 8], [1, 128]])
                    g.dma_start(out=wbuf[widx % 2][:, part * 1024:(part + 1) * 1024],
                                in_=src).then_inc(S[f"wload{widx % 2}"], 16)
                if 2 <= widx < ND + 2:
                    qot = widx - 2
                    g.wait_ge(S["fev"], 2 * qot + 2)
                    g.dma_start(out=qT_o[qot * 128:(qot + 1) * 128, :],
                                in_=stq2[qot % 2][:, :]).then_inc(S[f"stq_s{qot % 2}"], 16)
            for ch in range(8):
                if ch < 2:
                    g.wait_ge(S["mm"], 4 * ND)
                if ch >= 2:
                    g.wait_ge(S["vev"], (ch - 1) * 8)
                g.dma_start(out=bvb[ch % 2][:, :],
                            in_=bass.AP(bv, ch * 512, [[0, 128], [1, 512]])).then_inc(S[f"vwload{ch % 2}"], 16)
                for part in range(4):
                    src = bass.AP(wvT, (part * 8 * 128) * D + ch * 512,
                                  [[D, 128], [128 * D, 8], [1, 512]])
                    g.dma_start(out=wvbuf[ch % 2][:, part * 8 * 512:(part + 1) * 8 * 512],
                                in_=src).then_inc(S[f"vwload{ch % 2}"], 16)

        @blk.vector
        def _(v):
            ch = Ch(v, S["vch"])
            ch.wait(S["cload"], 16 * 5)
            for h in range(L0_NH):
                c0 = h * 3
                ch.op(v.scalar_tensor_tensor, btmp[:, 0:128], kdkg_sb[:, 128:256],
                      bw_sb[:, c0:c0 + 1], kdkg_sb[:, 0:128], ALU.mult, ALU.add)
                ch.op(v.scalar_tensor_tensor, btmp[:, 128:256], kdkg_sb[:, 256:384],
                      bw_sb[:, c0 + 1:c0 + 2], btmp[:, 0:128], ALU.mult, ALU.add)
                ch.op(v.scalar_tensor_tensor, bmat[:, h * 128:(h + 1) * 128], kdkg_sb[:, 384:512],
                      bw_sb[:, c0 + 2:c0 + 3], btmp[:, 128:256], ALU.mult, ALU.add)
            xn_n = 0
            tpc_n = 0
            for i in range(NT):
                c = 8 * i
                ch.wait(S[f"xload{i % 2}"], 16 * (i // 2 + 1))
                for ck in range(8):
                    ch.op(v.bn_stats, bnbuf[:, ck * 6:(ck + 1) * 6],
                          x_t[i % 2][:, ck * 512:(ck + 1) * 512])
                ch.op(v.bn_aggr, stat[:, c:c + 2], bnbuf[:, :])
                ch.op_m(S["var"], i + 1, v.tensor_scalar, stat[:, c + 2:c + 3],
                        stat[:, c + 1:c + 2], LN_EPS, None, ALU.add)
                ch.wait(S["sqv"], i + 1)
                ch.op(v.reciprocal, stat[:, c + 7:c + 8], stat[:, c + 6:c + 7])
                if i >= 1:
                    ch.wait(S["tp"], 4 * i)
                xn_n += 1
                ch.op_m(S["xn"], xn_n, v.tensor_scalar, xn_tok[:, :], x_t[i % 2][:, :],
                        stat[:, c:c + 1], stat[:, c + 7:c + 8], ALU.subtract, ALU.mult)
                for gidx in range(4):
                    gi = i * 4 + gidx
                    ch.wait(S["tp"], gi + 1)
                    tpc_n += 1
                    ch.op_m(S["tpc"], tpc_n, v.tensor_copy,
                            xnT_r[:, gidx * 8:(gidx + 1) * 8, i * 128:(i + 1) * 128],
                            ps_ln[gi % 2][:, :])
            def deq_tile(j):
                if j < 2:
                    ch.wait(S[f"qdcl{j}"], 32)
                else:
                    ch.wait(S[f"dcl{j % 2}"], 32 * ((j - 2) // 2 + 1))
                if j >= 2:
                    ch.wait(S[f"qdst{j % 2}"], 16 * (j // 2))
                ch.op(v.tensor_copy, dq_cf[:, :], dq_cbuf[j % 2][:, :])
                ch.op(v.tensor_scalar, dq_acc[j % 2][:, :], dq_cf[:, :], 0.0,
                      NF4[0], ALU.is_equal, ALU.mult)
                for tv in range(1, 16):
                    ch.op(v.tensor_scalar, dq_selt[:, :], dq_cf[:, :], float(tv),
                          NF4[tv], ALU.is_equal, ALU.mult)
                    ch.op(v.tensor_tensor, dq_acc[j % 2][:, :], dq_acc[j % 2][:, :],
                          dq_selt[:, :], ALU.add)
                ch.op_m(S["dwd"], j + 1, v.tensor_tensor, dq_acc[j % 2][:, :],
                        dq_acc[j % 2][:, :], dq_mbuf[j % 2][:, :], ALU.mult)

            if deq is not None:
                for j in range(16):
                    deq_tile(j)
            vev_n = 0
            for ch_i in range(8):
                for w in range(2):
                    wv_idx = ch_i * 2 + w
                    ch.wait(S["vmm"], wv_idx + 1)
                    for t4 in range(4):
                        tt = w * 4 + t4
                        if ch_i >= 1:
                            ch.wait(S[f"vst{tt}"], 16 * ch_i)
                        vev_n += 1
                        ch.op_m(S["vev"], vev_n, v.tensor_tensor, vstage[tt][:, :],
                                ps_v[t4][:, 0:512], bvb[ch_i % 2][:, :], ALU.add)

        @blk.scalar
        def _(a):
            ch = Ch(a, S["ach"])
            for i in range(NT):
                c = 8 * i
                ch.wait(S["var"], i + 1)
                ch.op_m(S["sqv"], i + 1, a.activation, stat[:, c + 6:c + 7],
                        stat[:, c + 2:c + 3], AF.Sqrt)
            ev_n = 0
            fev_n = 0
            for ot in range(ND):
                for tc in range(2):
                    gidx = ot * 2 + tc
                    ch.wait(S["mm"], gidx + 1)
                    if ot >= 2 and tc == 0:
                        ch.wait(S["foldmm"], ot - 1)
                    ev_n += 1
                    ch.op_m(S["ev"], ev_n, a.activation,
                            stq[ot % 2][:, tc * 512:(tc + 1) * 512],
                            (ps_a if gidx % 2 == 0 else ps_b)[:, :],
                            AF.Identity, bias=bias_sb[:, ot:ot + 1])
                for fc in range(2):
                    ch.wait(S["foldmm"], ot + 1)
                    if ot >= 2:
                        ch.wait(S[f"stq_s{ot % 2}"], 16 * (ot // 2))
                    fev_n += 1
                    ch.op_m(S["fev"], fev_n, a.activation,
                            stq2[ot % 2][:, fc * 512:(fc + 1) * 512],
                            ps_v[fc][:, :], AF.Copy)
            for ot in range(ND):
                for tc in range(2):
                    gidx = 2 * ND + ot * 2 + tc
                    ch.wait(S["mm"], gidx + 1)
                    if ot >= 2 and tc == 0:
                        ch.wait(S[f"stk_s{ot % 2}"], 16 * (ot // 2))
                    ev_n += 1
                    ch.op_m(S["ev"], ev_n, a.activation,
                            stk[ot % 2][:, tc * 512:(tc + 1) * 512],
                            (ps_a if gidx % 2 == 0 else ps_b)[:, :],
                            AF.Identity, bias=bias_sb[:, ND + ot:ND + ot + 1])

        @blk.tensor
        def _(t):
            t.wait_ge(S["cload"], 16 * 5)
            for i in range(NT):
                t.wait_ge(S["xn"], i + 1)
                for gidx in range(4):
                    gi = i * 4 + gidx
                    if gi >= 2:
                        t.wait_ge(S["tpc"], gi - 1)
                    for j in range(8):
                        ins = t.transpose(ps_ln[gi % 2][:, j * 128:(j + 1) * 128],
                                          xn_tok[:, (gidx * 8 + j) * 128:(gidx * 8 + j + 1) * 128],
                                          idn_sb[:, :])
                        if j == 7:
                            ins.then_inc(S["tp"], 1)
            t.wait_ge(S["tpc"], 32)
            for widx in range(2 * ND):
                ot = widx % ND
                is_q = widx < ND
                if is_q and ot >= 1:
                    _emit_fold(t, S, bmat, stq, ps_v, ot - 1)
                if widx == ND:
                    _emit_fold(t, S, bmat, stq, ps_v, ND - 1)
                t.wait_ge(S[f"wload{widx % 2}"], 64 * (widx // 2 + 1))
                for tc in range(2):
                    gidx = widx * 2 + tc
                    if gidx >= 2:
                        t.wait_ge(S["ev"], gidx - 1)
                    for in_ in range(ND):
                        ins = t.matmul((ps_a if gidx % 2 == 0 else ps_b)[:, :],
                                       wbuf[widx % 2][:, in_ * 128:(in_ + 1) * 128],
                                       xnT[:, in_ * TOK + tc * 512:in_ * TOK + (tc + 1) * 512],
                                       start=(in_ == 0), stop=(in_ == ND - 1))
                        if in_ == ND - 1:
                            ins.then_inc(S["mm"], 1)

            t.wait_ge(S["fev"], 2 * ND)
            for ch in range(8):
                t.wait_ge(S[f"vwload{ch % 2}"], 80 * (ch // 2 + 1))
                for w in range(2):
                    wv_idx = ch * 2 + w
                    if wv_idx >= 1:
                        t.wait_ge(S["vev"], wv_idx * 4)
                    for in_ in range(ND):
                        for t4 in range(4):
                            tt = w * 4 + t4
                            ins = t.matmul(ps_v[t4][:, 0:512],
                                           xnT[:, in_ * TOK + tt * 128:in_ * TOK + (tt + 1) * 128],
                                           wvbuf[ch % 2][:, in_ * 512:(in_ + 1) * 512],
                                           start=(in_ == 0), stop=(in_ == ND - 1))
                            if in_ == ND - 1 and t4 == 3:
                                ins.then_inc(S["vmm"], 1)


def _emit_fold(t, S, bmat, stq, ps_v, ot):
    h = ot // 2
    t.wait_ge(S["vch"], 48)
    t.wait_ge(S["ev"], 2 * ot + 2)
    if ot >= 1:
        t.wait_ge(S["fev"], 2 * (ot - 1) + 2)
    for fc in range(2):
        ins = t.matmul(ps_v[fc][:, :], bmat[:, h * 128:(h + 1) * 128],
                       stq[ot % 2][:, fc * 512:(fc + 1) * 512],
                       start=True, stop=True)
        if fc == 1:
            ins.then_inc(S["foldmm"], 1)




def emit_poll(g, S, psem_name, side_sb, flags_sb, poll_sb, flc, fl0, fl1, pre=""):
    with (g.register(pre + "sdp") as sd, g.register(pre + "cont") as cont,
          g.register(pre + "fv") as fv, g.register(pre + "exp") as exp,
          g.register(pre + "pc") as pc):
        g.reg_load(sd, side_sb[0:1, 0:1])
        g.reg_load(exp, flags_sb[0:1, flc * 16:flc * 16 + 1])
        g.reg_mov(pc, 0)

        def loop(fl):
            g.reg_mov(cont, 1)
            with g.While(cont):
                g.reg_add(pc, pc, 16)
                g.dma_start(out=poll_sb[:, :], in_=fl[0:1, :]).then_inc(S[psem_name], 16)
                g.wait_ge(S[psem_name], pc)
                g.reg_load(fv, poll_sb[0:1, 0:1])
                g.reg_mov(cont, 1)
                with g.If_eq(fv, exp):
                    g.reg_mov(cont, 0)

        with g.If_eq(sd, 0):
            loop(fl1)
        with g.Else():
            loop(fl0)


def build_attn(layer):
    nc = bass.Bass("TRN2", target_bir_lowering=False)
    KVD = D if layer == 0 else L1_KV * L1_HD
    T = {}
    T["qT"] = nc.dram_tensor("qT", [D, TOK], BF16, kind="ExternalInput")
    T["kT"] = nc.dram_tensor("kT", [KVD, SEQ], BF16, kind="ExternalInput")
    T["v"] = nc.dram_tensor("v", [SEQ, KVD], BF16, kind="ExternalInput")
    T["x"] = nc.dram_tensor("x", [TOK, D], F32, kind="ExternalInput")
    T["idn"] = nc.dram_tensor("idn", [128, 128], BF16, kind="ExternalInput")
    T["idnf"] = nc.dram_tensor("idnf", [128, 128], F32, kind="ExternalInput")
    T["x1"] = nc.dram_tensor("x1", [TOK, D], F32, kind="ExternalOutput")
    with ExitStack() as sst:
        emit_attn(nc, _semf(nc, sst, "B"), T, layer, fused=False)
    return nc


def emit_attn(nc, sf, T, layer, fused, pre="", deq=None):
    if layer == 0:
        NH, HD, REP, SCALE = L0_NH, L0_HD, 1, None
        KVD = D
    else:
        NH, HD, REP, SCALE = L1_H, L1_HD, 4, 1.0 / np.sqrt(L1_HD)
        KVD = L1_KV * L1_HD
    qT_i, x_i, idn, x1_o = T["qT"], T["x"], T["idn"], T["x1"]
    idn_f = T["idnf"]
    if fused:
        k0, k1, v0, v1 = T["k0"], T["k1"], T["v0"], T["v1"]
        side_sb, flags_sb, poll_sb = T["side_sb"], T["flags_sb"], T["poll_sb"]
        fl0, fl1, FLC = T["fl0"], T["fl1"], T["flc"]
    else:
        kT_i, v_i = T["kT"], T["v"]
    NHT = HD // 128
    NU = NH * NT
    NG = NH // REP
    GU = REP * NT

    with ExitStack() as st:
        sb = lambda nm, sh, dt: st.enter_context(nc.sbuf_tensor(pre + nm, sh, dt))
        ps = lambda nm, sh, dt: st.enter_context(nc.psum_tensor(pre + nm, sh, dt))
        sem = sf

        qhb = [sb(f"qhb{i}", [128, NHT * TOK], BF16) for i in range(2)]
        kbuf = [sb(f"kbuf{i}", [128, NHT * SEQ], BF16) for i in range(2)]
        vbuf = [sb(f"vbuf{i}", [128, NK * HD], BF16) for i in range(2)]
        pb_sb = [sb(f"pb_sb{i}", [128, SEQ], BF16) for i in range(2)]
        ptsb = [sb(f"ptsb{i}", [128, NK * 512], BF16) for i in range(2)]
        osb = [sb(f"osb{i}", [128, 512], F32) for i in range(2)]
        xsl = [sb(f"xsl{i}", [128, HD], F32) for i in range(4)]
        xo = [sb(f"xo{i}", [128, HD], F32) for i in range(4)]
        stat2 = sb("stat2", [128, 64], F32)
        idn_sb = sb("idn_sb", [128, 128], BF16)
        idn32 = sb("idn32", [128, 128], F32)

        ps_s = ps("ps_s", [128, SEQ], F32)
        ps_t = [ps(f"ps_t{i}", [128, 1024], BF16) for i in range(2)]
        ps_o = [ps(f"ps_o{i}", [128, 512], F32) for i in range(2)]

        blk = st.enter_context(nc.Block())
        S = {n: sem(n) for n in
             ["cload", "qlo0", "qlo1", "klo0", "klo1", "smmc", "mx", "exq", "tp2",
              "tpc2", "tpc2a", "pvs", "oev", "untr", "sst", "xld0", "xld1", "xld2", "xld3",
              "xst0", "xst1", "xst2", "xst3", "vch", "ach", "psem"]}
        NKC = SEQ // 512
        NSU = NU // 4
        if deq is not None:
            for n in ["dcl0", "dcl1", "dwd", "dst0", "dst1", "dflst"]:
                S[n] = sem(n)
            dq_codes, dq_amx, dq_out = deq["codes"], deq["amx"], deq["wT_sh"]
            dq_fl0, dq_fl1, dq_flc = deq["flW0"], deq["flW1"], deq["flcW"]
            DQ0 = 2 * deq.get("t0", 0)
            dq_cbuf = [sb(f"dq_cbuf{i}", [128, 2048], I32) for i in range(2)]
            dq_mbuf = [sb(f"dq_mbuf{i}", [128, 2048], F32) for i in range(2)]
            dq_cf = sb("dq_cf", [128, 2048], BF16)
            dq_selt = sb("dq_selt", [128, 2048], BF16)
            dq_acc = [sb(f"dq_acc{i}", [128, 2048], BF16) for i in range(2)]
            NDQ = 32

            def dq_srcs(j):
                r, hf = j // 2, j % 2
                return (dq_codes[r * 128:(r + 1) * 128, hf * 2048:(hf + 1) * 2048],
                        bass.AP(dq_amx, (r * 2) * D + hf * 2048,
                                [[D, 2], [0, 64], [1, 2048]]))

        KLN = 64 if fused else 32

        def kv_load(s, g):
            if not fused:
                srck = bass.AP(kT_i, (g * HD) * SEQ,
                               [[SEQ, 128], [128 * SEQ, NHT], [1, SEQ]])
                s.dma_start(out=kbuf[g % 2][:, :], in_=srck).then_inc(S[f"klo{g % 2}"], 16)
                srcv = bass.AP(v_i, g * HD,
                               [[KVD, 128], [128 * KVD, NK], [1, HD]])
                s.dma_start(out=vbuf[g % 2][:, :], in_=srcv).then_inc(S[f"klo{g % 2}"], 16)
            else:
                kb_r = kbuf[g % 2].ap().rearrange("p (d s) -> p d s", s=SEQ)
                for half, ksrc in ((0, k0), (1, k1)):
                    srck = bass.AP(ksrc, (g * HD) * TOK,
                                   [[TOK, 128], [128 * TOK, NHT], [1, TOK]])
                    s.dma_start(out=kb_r[:, :, half * TOK:(half + 1) * TOK],
                                in_=srck).then_inc(S[f"klo{g % 2}"], 16)
                for half, vsrc in ((0, v0), (1, v1)):
                    srcv = bass.AP(vsrc, g * HD,
                                   [[KVD, 128], [128 * KVD, NK // 2], [1, HD]])
                    s.dma_start(out=vbuf[g % 2][:, half * (NK // 2) * HD:(half + 1) * (NK // 2) * HD],
                                in_=srcv).then_inc(S[f"klo{g % 2}"], 16)

        @blk.sync
        def _(s):
            sd_ctx = s.register(pre + "sd") if fused else None
            sd = sd_ctx.__enter__() if fused else None
            if fused:
                s.reg_load(sd, side_sb[0:1, 0:1])

            def dq_store(j):
                r, hf = j // 2, j % 2
                cs = slice(hf * 2048, (hf + 1) * 2048)
                with s.If_eq(sd, 0):
                    s.dma_start(out=dq_out[r * 128:(r + 1) * 128, cs],
                                in_=dq_acc[j % 2][:, :]).then_inc(S[f"qdst{j % 2}"], 16)
                with s.Else():
                    s.dma_start(out=dq_out[2048 + r * 128:2048 + (r + 1) * 128, cs],
                                in_=dq_acc[j % 2][:, :]).then_inc(S[f"qdst{j % 2}"], 16)

            s.dma_start(out=idn_sb[:, :], in_=idn[:, :]).then_inc(S["cload"], 16)
            s.dma_start(out=idn32[:, :], in_=idn_f[:, :]).then_inc(S["cload"], 16)
            for u in range(NU):
                h, qt = u // NT, u % NT
                if u >= 4:
                    s.wait_ge(S["sst"], u - 3)
                s.dma_start(out=xsl[u % 4][:, :],
                            in_=bass.AP(x_i, qt * 128 * D + h * HD,
                                        [[D, 128], [1, HD]])).then_inc(S[f"xld{u % 4}"], 16)
                if u >= 1:
                    up = u - 1
                    hp, qtp = up // NT, up % NT
                    s.wait_ge(S["sst"], up + 1)
                    s.dma_start(out=bass.AP(x1_o, qtp * 128 * D + hp * HD,
                                            [[D, 128], [1, HD]]),
                                in_=xo[up % 4][:, :]).then_inc(S[f"xst{up % 4}"], 16)
            up = NU - 1
            s.wait_ge(S["sst"], up + 1)
            s.dma_start(out=bass.AP(x1_o, (up % NT) * 128 * D + (up // NT) * HD,
                                    [[D, 128], [1, HD]]),
                        in_=xo[up % 4][:, :]).then_inc(S[f"xst{up % 4}"], 16)
            for p_ in range(4):
                s.wait_ge(S[f"xst{p_}"], 16 * (NU // 4))

        @blk.gpsimd
        def _(g):
            dq_sd_ctx = g.register(pre + "gsd") if deq is not None else None
            dq_sd = dq_sd_ctx.__enter__() if deq is not None else None

            def dq_load(t_):
                if t_ - DQ0 >= 2:
                    g.wait_ge(S["dwd"], t_ - DQ0 - 1)
                csrc, msrc = dq_srcs(t_)
                g.dma_start(out=dq_cbuf[t_ % 2][:, :],
                            in_=csrc).then_inc(S[f"dcl{t_ % 2}"], 16)
                g.dma_start(out=dq_mbuf[t_ % 2][:, :],
                            in_=msrc).then_inc(S[f"dcl{t_ % 2}"], 16)

            def dq_store(t_):
                g.wait_ge(S["dwd"], t_ - DQ0 + 1)
                r, hf = t_ // 2, t_ % 2
                cs = slice(hf * 2048, (hf + 1) * 2048)
                with g.If_eq(dq_sd, 0):
                    g.dma_start(out=dq_out[r * 128:(r + 1) * 128, cs],
                                in_=dq_acc[t_ % 2][:, :]).then_inc(S[f"dst{t_ % 2}"], 16)
                with g.Else():
                    g.dma_start(out=dq_out[2048 + r * 128:2048 + (r + 1) * 128, cs],
                                in_=dq_acc[t_ % 2][:, :]).then_inc(S[f"dst{t_ % 2}"], 16)

            if deq is not None:
                g.reg_load(dq_sd, side_sb[0:1, 0:1])
                dq_load(DQ0)
                dq_load(DQ0 + 1)
                for t_ in range(DQ0 + 2, NDQ):
                    dq_load(t_)
                    dq_store(t_ - 2)
                for t_ in range(NDQ - 2, NDQ):
                    dq_store(t_)
                g.wait_ge(S["dst0"], 16 * ((NDQ - DQ0) // 2))
                g.wait_ge(S["dst1"], 16 * ((NDQ - DQ0) // 2))
                with g.If_eq(dq_sd, 0):
                    g.dma_start(out=dq_fl0[:, :],
                                in_=flags_sb[0:1, dq_flc * 16:(dq_flc + 1) * 16]).then_inc(S["dflst"], 16)
                with g.Else():
                    g.dma_start(out=dq_fl1[:, :],
                                in_=flags_sb[0:1, dq_flc * 16:(dq_flc + 1) * 16]).then_inc(S["dflst"], 16)
                g.wait_ge(S["dflst"], 16)
                dq_sd_ctx.__exit__(None, None, None)
            if fused:
                emit_poll(g, S, "psem", side_sb, flags_sb, poll_sb, FLC, fl0, fl1, pre=pre)
            for h in range(NH):
                if h % REP == 0:
                    gg = h // REP
                    if gg >= 2:
                        g.wait_ge(S["smmc"], 4 * GU * (gg - 1))
                        g.wait_ge(S["pvs"], (GU // 4) * NHT * (gg - 1))
                    kv_load(g, gg)
                if h >= 2:
                    g.wait_ge(S["smmc"], 4 * NT * (h - 1))
                srcq = bass.AP(qT_i, (h * NHT * 128) * TOK,
                               [[TOK, 128], [128 * TOK, NHT], [1, TOK]])
                g.dma_start(out=qhb[h % 2][:, :], in_=srcq).then_inc(S[f"qlo{h % 2}"], 16)


        @blk.tensor
        def _(t):
            t.wait_ge(S["cload"], 32)
            for u in range(NU):
                h, qt = u // NT, u % NT
                g_ = h // REP
                t.wait_ge(S[f"klo{g_ % 2}"], KLN * (g_ // 2 + 1))
                t.wait_ge(S[f"qlo{h % 2}"], 16 * (h // 2 + 1))
                for d in range(NHT):
                    for kc in range(NKC):
                        if d == 0 and u >= 1:
                            t.wait_ge(S["exq"], (u - 1) * NKC + kc + 1)
                        ins = t.matmul(ps_s[:, kc * 512:(kc + 1) * 512],
                                       qhb[h % 2][:, d * TOK + qt * 128:d * TOK + (qt + 1) * 128],
                                       kbuf[g_ % 2][:, d * SEQ + kc * 512:d * SEQ + (kc + 1) * 512],
                                       start=(d == 0), stop=(d == NHT - 1))
                        if d == NHT - 1:
                            ins.then_inc(S["smmc"], 1)
                if u >= 5 and (u - 5) % 4 == 0:
                    _b_untr(t, S, (u - 5) // 4, osb, ps_o, idn32, NHT)
                if u >= 1:
                    _b_trans(t, S, u - 1, pb_sb, ps_t, idn_sb, NK)
                    if (u - 1) % 4 == 3:
                        _b_pv(t, S, (u - 1) // 4, ptsb, vbuf, osb, ps_o, idn32,
                              NHT, NK, HD, REP)
            _b_trans(t, S, NU - 1, pb_sb, ps_t, idn_sb, NK)
            _b_pv(t, S, NSU - 1, ptsb, vbuf, osb, ps_o, idn32, NHT, NK, HD, REP)
            _b_untr(t, S, NSU - 1, osb, ps_o, idn32, NHT)

        @blk.scalar
        def _(a):
            ch = Ch(a, S["ach"])
            for u in range(NU):
                ch.wait(S["mx"], u + 1)
                if u >= 2:
                    ch.wait(S["tp2"], u - 1)
                c = (u % 4) * 16
                kw = {} if SCALE is None else {"scale": float(SCALE)}
                for kc in range(4):
                    ch.op_m(S["exq"], u * 4 + kc + 1, a.activation,
                            pb_sb[u % 2][:, kc * 512:(kc + 1) * 512],
                            ps_s[:, kc * 512:(kc + 1) * 512],
                            AF.Exp, bias=stat2[:, c + 4:c + 5],
                            accum_out=stat2[:, c + 5 + kc:c + 6 + kc], **kw)
                if u >= 1:
                    _attn_acopy(a, ch, S, u - 1, ptsb, ps_t)
            _attn_acopy(a, ch, S, NU - 1, ptsb, ps_t)

        @blk.vector
        def _(v):
            ch = Ch(v, S["vch"])

            def deq_gen():
                for t_ in range(DQ0, NDQ):
                    lt = t_ - DQ0
                    ch.wait(S[f"dcl{t_ % 2}"], 32 * (lt // 2 + 1))
                    if lt >= 2:
                        ch.wait(S[f"dst{t_ % 2}"], 16 * (lt // 2))
                    ch.op(v.tensor_copy, dq_cf[:, :], dq_cbuf[t_ % 2][:, :])
                    ch.op(v.tensor_scalar, dq_acc[t_ % 2][:, :], dq_cf[:, :], 0.0,
                          NF4[0], ALU.is_equal, ALU.mult)
                    for tv in range(1, 16):
                        ch.op(v.tensor_scalar, dq_selt[:, :], dq_cf[:, :], float(tv),
                              NF4[tv], ALU.is_equal, ALU.mult)
                        ch.op(v.tensor_tensor, dq_acc[t_ % 2][:, :], dq_acc[t_ % 2][:, :],
                              dq_selt[:, :], ALU.add)
                    ch.op_m(S["dwd"], lt + 1, v.tensor_tensor, dq_acc[t_ % 2][:, :],
                            dq_acc[t_ % 2][:, :], dq_mbuf[t_ % 2][:, :], ALU.mult)

            if deq is not None:
                deq_gen()
            mxn = 0
            sstn = 0
            for u in range(NU):
                c = (u % 4) * 16
                for kc in range(4):
                    ch.wait(S["smmc"], u * 4 + kc + 1)
                    ch.op(v.reduce_max, stat2[:, c + kc:c + kc + 1],
                          ps_s[:, kc * 512:(kc + 1) * 512], axis=AX.X, negate=True)
                mxn += 1
                if SCALE is None:
                    ch.op_m(S["mx"], mxn, v.tensor_reduce, stat2[:, c + 4:c + 5],
                            stat2[:, c:c + 4], AX.X, ALU.min)
                else:
                    ch.op(v.tensor_reduce, stat2[:, c + 9:c + 10],
                          stat2[:, c:c + 4], AX.X, ALU.min)
                    ch.op_m(S["mx"], mxn, v.tensor_scalar, stat2[:, c + 4:c + 5],
                            stat2[:, c + 9:c + 10], float(SCALE), None, ALU.mult)
                if u >= 5 and (u - 5) % 4 == 0:
                    sstn = _attn_vstt(v, ch, S, (u - 5) // 4, ps_o, stat2,
                                      xo, xsl, HD, NHT, sstn)
                if u >= 1:
                    _attn_vcopy(v, ch, S, u - 1, ptsb, ps_t, stat2)
                    if (u - 1) % 4 == 3:
                        _attn_vev(v, ch, S, (u - 1) // 4, osb, ps_o, NHT)
            _attn_vcopy(v, ch, S, NU - 1, ptsb, ps_t, stat2)
            _attn_vev(v, ch, S, NSU - 1, osb, ps_o, NHT)
            sstn = _attn_vstt(v, ch, S, NSU - 1, ps_o, stat2, xo, xsl, HD, NHT, sstn)


def _attn_vcopy(v, ch, S, up, ptsb, ps_t, stat2):
    su, sq = up // 4, up % 4
    cp = (up % 4) * 16
    pt_r = ptsb[su % 2].ap().rearrange("p (k q) -> p k q", q=512)
    ch.wait(S["tp2"], up + 1)
    ch.op_m(S["tpc2"], up + 1, v.tensor_copy,
            pt_r[:, 0:8, sq * 128:(sq + 1) * 128], ps_t[0][:, :])
    ch.wait(S["exq"], 4 * (up + 1))
    ch.op(v.tensor_reduce, stat2[:, cp + 10:cp + 11], stat2[:, cp + 5:cp + 9],
          AX.X, ALU.add)
    ch.op(v.reciprocal, stat2[:, cp + 11:cp + 12], stat2[:, cp + 10:cp + 11])


def _attn_acopy(a, ch, S, up, ptsb, ps_t):
    su, sq = up // 4, up % 4
    pt_r = ptsb[su % 2].ap().rearrange("p (k q) -> p k q", q=512)
    ch.wait(S["tp2"], up + 1)
    ch.op_m(S["tpc2a"], up + 1, a.activation,
            pt_r[:, 8:16, sq * 128:(sq + 1) * 128], ps_t[1][:, :], AF.Copy)


def _attn_vev(v, ch, S, su, osb, ps_o, NHT):
    for dt in range(NHT):
        odt = su * NHT + dt
        ch.wait(S["pvs"], odt + 1)
        ch.op_m(S["oev"], odt + 1, v.tensor_copy, osb[odt % 2][:, :],
                ps_o[odt % 2][:, :])


def _attn_vstt(v, ch, S, su, ps_o, stat2, xo, xsl, HD, NHT, sstn):
    for sq in range(4):
        u = su * 4 + sq
        cp = (u % 4) * 16
        ch.wait(S[f"xld{u % 4}"], 16 * (u // 4 + 1))
        if u >= 4:
            ch.wait(S[f"xst{u % 4}"], 16 * (u // 4))
        for dt in range(NHT):
            odt = su * NHT + dt
            ch.wait(S["untr"], odt + 1)
            sstn += (1 if dt == NHT - 1 else 0)
            if dt == NHT - 1:
                ch.op_m(S["sst"], sstn, v.scalar_tensor_tensor,
                        xo[u % 4][:, dt * 128:(dt + 1) * 128],
                        ps_o[odt % 2][:, sq * 128:(sq + 1) * 128],
                        stat2[:, cp + 11:cp + 12],
                        xsl[u % 4][:, dt * 128:(dt + 1) * 128], ALU.mult, ALU.add)
            else:
                ch.op(v.scalar_tensor_tensor,
                      xo[u % 4][:, dt * 128:(dt + 1) * 128],
                      ps_o[odt % 2][:, sq * 128:(sq + 1) * 128],
                      stat2[:, cp + 11:cp + 12],
                      xsl[u % 4][:, dt * 128:(dt + 1) * 128], ALU.mult, ALU.add)
    return sstn


def _b_trans(t, S, up, pb_sb, ps_t, idn_sb, NKt):
    t.wait_ge(S["exq"], 4 * (up + 1))
    if up >= 1:
        t.wait_ge(S["tpc2"], up)
        t.wait_ge(S["tpc2a"], up)
    for j in range(NKt):
        ins = t.transpose(ps_t[j // 8][:, (j % 8) * 128:(j % 8 + 1) * 128],
                          pb_sb[up % 2][:, j * 128:(j + 1) * 128], idn_sb[:, :])
        if j == NKt - 1:
            ins.then_inc(S["tp2"], 1)


def _b_pv(t, S, su, ptsb, vbuf, osb, ps_o, idn32, NHT, NKt, HD, REP):
    hp = ((su * 4) // NT) // REP
    t.wait_ge(S["tpc2"], 4 * su + 4)
    t.wait_ge(S["tpc2a"], 4 * su + 4)
    for dt in range(NHT):
        odt = su * NHT + dt
        if odt >= 2:
            su2 = (odt - 2) // NHT
            t.wait_ge(S["sst"], 4 * (su2 + 1))
        for kt in range(NKt):
            ins = t.matmul(ps_o[odt % 2][:, :],
                           vbuf[hp % 2][:, kt * HD + dt * 128:kt * HD + (dt + 1) * 128],
                           ptsb[su % 2][:, kt * 512:(kt + 1) * 512],
                           start=(kt == 0), stop=(kt == NKt - 1))
            if kt == NKt - 1:
                ins.then_inc(S["pvs"], 1)

def _b_untr(t, S, su, osb, ps_o, idn32, NHT):
    for dt in range(NHT):
        odt = su * NHT + dt
        t.wait_ge(S["oev"], odt + 1)
        for sq in range(4):
            ins = t.transpose(ps_o[odt % 2][:, sq * 128:(sq + 1) * 128],
                              osb[odt % 2][:, sq * 128:(sq + 1) * 128], idn32[:, :])
            if sq == 3:
                ins.then_inc(S["untr"], 1)




def build_dequant():
    nc = bass.Bass("TRN2", target_bir_lowering=False)
    T = {}
    T["codes"] = nc.dram_tensor("codes", [512, D], I32, kind="ExternalInput")
    T["amx"] = nc.dram_tensor("amx", [8, D], F32, kind="ExternalInput")
    T["wsh"] = nc.dram_tensor("wsh", [512, D], BF16, kind="ExternalOutput")
    with ExitStack() as sst:
        emit_dequant(nc, _semf(nc, sst, "C"), T, fused=False)
    return nc


def emit_dequant(nc, sf, T, fused, pre=""):
    codes, amx = T["codes"], T["amx"]
    NTL = 16 if fused else 4
    if fused:
        wT_sh = T["wT_sh"]
        side_sb, flags_sb = T["side_sb"], T["flags_sb"]
        fl0, fl1, FLC = T["flW0"], T["flW1"], T["flcW"]
    else:
        wsh = T["wsh"]
    with ExitStack() as st:
        sb = lambda nm, sh, dt: st.enter_context(nc.sbuf_tensor(pre + nm, sh, dt))
        sem = sf
        cbuf = [sb(f"cbuf{i}", [128, D], I32) for i in range(2)]
        mbuf = [sb(f"mbuf{i}", [128, D], F32) for i in range(2)]
        cf = sb("cf", [128, D], BF16)
        selt = sb("selt", [128, D], BF16)
        acc = [sb(f"acc{i}", [128, D], BF16) for i in range(2)]
        blk = st.enter_context(nc.Block())
        S = {n: sem(n) for n in ["cl0", "cl1", "wd", "st0", "st1", "vch", "flst"]}

        @blk.sync
        def _(s):
            sd_ctx = s.register(pre + "sd") if fused else None
            sd = sd_ctx.__enter__() if fused else None
            if fused:
                s.reg_load(sd, side_sb[0:1, 0:1])

            def do_store(t):
                s.wait_ge(S["wd"], t + 1)
                if fused:
                    with s.If_eq(sd, 0):
                        s.dma_start(out=wT_sh[t * 128:(t + 1) * 128, :],
                                    in_=acc[t % 2][:, :]).then_inc(S[f"st{t % 2}"], 16)
                    with s.Else():
                        s.dma_start(out=wT_sh[2048 + t * 128:2048 + (t + 1) * 128, :],
                                    in_=acc[t % 2][:, :]).then_inc(S[f"st{t % 2}"], 16)
                else:
                    s.dma_start(out=wsh[t * 128:(t + 1) * 128, :],
                                in_=acc[t % 2][:, :]).then_inc(S[f"st{t % 2}"], 16)

            for t in range(NTL):
                if t >= 2:
                    s.wait_ge(S["wd"], t - 1)
                s.dma_start(out=cbuf[t % 2][:, :],
                            in_=codes[t * 128:(t + 1) * 128, :]).then_inc(S[f"cl{t % 2}"], 16)
                s.dma_start(out=mbuf[t % 2][:, :],
                            in_=bass.AP(amx, (t * 2) * D, [[D, 2], [0, 64], [1, D]])).then_inc(S[f"cl{t % 2}"], 16)
                if t >= 2:
                    do_store(t - 2)
            do_store(NTL - 2)
            do_store(NTL - 1)
            s.wait_ge(S["st0"], 16 * (NTL // 2))
            s.wait_ge(S["st1"], 16 * (NTL // 2))
            if fused:
                with s.If_eq(sd, 0):
                    s.dma_start(out=fl0[:, :], in_=flags_sb[0:1, FLC * 16:(FLC + 1) * 16]).then_inc(S["flst"], 16)
                with s.Else():
                    s.dma_start(out=fl1[:, :], in_=flags_sb[0:1, FLC * 16:(FLC + 1) * 16]).then_inc(S["flst"], 16)
                s.wait_ge(S["flst"], 16)
                sd_ctx.__exit__(None, None, None)

        @blk.vector
        def _(v):
            ch = Ch(v, S["vch"])
            for t in range(NTL):
                ch.wait(S[f"cl{t % 2}"], 32 * (t // 2 + 1))
                if t >= 2:
                    ch.wait(S[f"st{t % 2}"], 16 * (t // 2))
                ch.op(v.tensor_copy, cf[:, :], cbuf[t % 2][:, :])
                ch.op(v.tensor_scalar, acc[t % 2][:, :], cf[:, :], 0.0, NF4[0],
                      ALU.is_equal, ALU.mult)
                for tv in range(1, 16):
                    ch.op(v.tensor_scalar, selt[:, :], cf[:, :], float(tv), NF4[tv],
                          ALU.is_equal, ALU.mult)
                    ch.op(v.tensor_tensor, acc[t % 2][:, :], acc[t % 2][:, :],
                          selt[:, :], ALU.add)
                ch.op_m(S["wd"], t + 1, v.tensor_tensor, acc[t % 2][:, :],
                        acc[t % 2][:, :], mbuf[t % 2][:, :], ALU.mult)


def build_ffn(layer):
    nc = bass.Bass("TRN2", target_bir_lowering=False)
    T = {}
    T["x"] = nc.dram_tensor("x", [TOK, D], F32, kind="ExternalInput")
    if layer == 0:
        T["wT"] = nc.dram_tensor("wT", [D, D], BF16, kind="ExternalInput")
        T["waT"] = nc.dram_tensor("waT", [128, 512], F32, kind="ExternalInput")
        T["wbT"] = nc.dram_tensor("wbT", [16, D], F32, kind="ExternalInput")
    else:
        T["wT"] = nc.dram_tensor("wT", [D, D], F16, kind="ExternalInput")
        T["brow"] = nc.dram_tensor("brow", [1, D], F16, kind="ExternalInput")
    T["idn"] = nc.dram_tensor("idn", [128, 128], BF16, kind="ExternalInput")
    T["xo"] = nc.dram_tensor("xo", [TOK, D], F32, kind="ExternalOutput")
    with ExitStack() as sst:
        emit_ffn(nc, _semf(nc, sst, "D"), T, layer, fused=False)
    return nc


def emit_ffn(nc, sf, T, layer, fused, pre=""):
    x_i, wT, idn, xo_o = T["x"], T["wT"], T["idn"], T["xo"]
    if layer == 0:
        waT, wbT = T["waT"], T["wbT"]
    else:
        brow = T["brow"]
    if fused and layer == 0:
        side_sb, flags_sb, poll_sb = T["side_sb"], T["flags_sb"], T["poll_sb"]
        fl0, fl1, FLC = T["flW0"], T["flW1"], T["flcW"]
    NCH = 8
    CW = 512
    WDT = BF16 if layer == 0 else F16
    import os
    dbg_nolora = bool(os.environ.get("FUSE_DBG_NOLORA"))

    with ExitStack() as st:
        sb = lambda nm, sh, dt: st.enter_context(nc.sbuf_tensor(pre + nm, sh, dt))
        ps = lambda nm, sh, dt: st.enter_context(nc.psum_tensor(pre + nm, sh, dt))
        sem = sf
        xnT = sb("xnT", [128, ND * TOK], BF16)
        xnT_r = xnT.ap().rearrange("p (f t) -> p f t", t=TOK)
        x_t = [sb(f"x_t{i}", [128, D], F32) for i in range(2)]
        stat = sb("stat", [128, 8 * NT], F32)
        bnbuf = sb("bnbuf", [128, 48], F32)
        xn_tok = sb("xn_tok", [128, D], BF16)
        idn_sb = sb("idn_sb", [128, 128], BF16)
        wvbuf = [sb(f"wvbuf{i}", [128, ND * CW], WDT) for i in range(2)]
        xsl = [sb(f"xsl{i}", [128, CW], F32) for i in range(2)]
        xos = [sb(f"xos{i}", [128, CW], F32) for i in range(2)]
        if layer == 0:
            wa_sb = sb("wa_sb", [128, 512], BF16)
            wb_sb = sb("wb_sb", [16, D], BF16)
            zsb = sb("zsb", [16, TOK], BF16)
        else:
            biasb = [sb(f"biasb{i}", [128, CW], F32) for i in range(2)]
            tmpb = sb("tmpb", [128, CW], F32)
        ps_ln = [ps(f"ps_ln{i}", [128, 1024], BF16) for i in range(2)]
        ps_y = [ps(f"ps_y{i}", [128, 512], F32) for i in range(4)]
        if layer == 0:
            ps_z = ps("ps_z", [16, TOK], F32)
        blk = st.enter_context(nc.Block())
        S = {n: sem(n) for n in
             ["cload", "cload2", "vch", "ach", "xload0", "xload1", "var", "sqv", "xn",
              "tp", "tpc", "vwload0", "vwload1", "vmm", "vev",
              "zmm", "zev", "xld0", "xld1", "xst0", "xst1", "psem"]}

        @blk.sync
        def _(s):
            sd_ctx = s.register(pre + "sd") if fused else None
            sd = sd_ctx.__enter__() if fused else None
            if fused:
                s.reg_load(sd, side_sb[0:1, 0:1])

            def dq_store(j):
                r, hf = j // 2, j % 2
                cs = slice(hf * 2048, (hf + 1) * 2048)
                with s.If_eq(sd, 0):
                    s.dma_start(out=dq_out[r * 128:(r + 1) * 128, cs],
                                in_=dq_acc[j % 2][:, :]).then_inc(S[f"qdst{j % 2}"], 16)
                with s.Else():
                    s.dma_start(out=dq_out[2048 + r * 128:2048 + (r + 1) * 128, cs],
                                in_=dq_acc[j % 2][:, :]).then_inc(S[f"qdst{j % 2}"], 16)

            s.dma_start(out=idn_sb[:, :], in_=idn[:, :]).then_inc(S["cload"], 16)
            for i in range(NT):
                if i >= 2:
                    s.wait_ge(S["xn"], i - 1)
                s.dma_start(out=x_t[i % 2][:, :], in_=x_i[i * 128:(i + 1) * 128, :]).then_inc(S[f"xload{i % 2}"], 16)
            NE = NCH * 8
            for e in range(NE):
                chn, tt = e // 8, e % 8
                if e >= 2:
                    s.wait_ge(S["vev"], e - 1)
                s.dma_start(out=xsl[e % 2][:, :],
                            in_=bass.AP(x_i, tt * 128 * D + chn * CW,
                                        [[D, 128], [1, CW]])).then_inc(S[f"xld{e % 2}"], 16)
                if e >= 1:
                    ep = e - 1
                    s.wait_ge(S["vev"], ep + 1)
                    s.dma_start(out=bass.AP(xo_o, (ep % 8) * 128 * D + (ep // 8) * CW,
                                            [[D, 128], [1, CW]]),
                                in_=xos[ep % 2][:, :]).then_inc(S[f"xst{ep % 2}"], 16)
            ep = NE - 1
            s.wait_ge(S["vev"], ep + 1)
            s.dma_start(out=bass.AP(xo_o, (ep % 8) * 128 * D + (ep // 8) * CW,
                                    [[D, 128], [1, CW]]),
                        in_=xos[ep % 2][:, :]).then_inc(S[f"xst{ep % 2}"], 16)
            s.wait_ge(S["xst0"], 16 * (NE // 2))
            s.wait_ge(S["xst1"], 16 * (NE // 2))

        @blk.gpsimd
        def _(g):
            if layer == 0:
                g.dma_start(out=wa_sb[:, :], in_=waT[:, :]).then_inc(S["cload2"], 16)
                g.dma_start(out=wb_sb[:, :], in_=wbT[:, :]).then_inc(S["cload2"], 16)
            import os
            if fused and layer == 0 and not os.environ.get("FUSE_DBG_NOPOLL"):
                emit_poll(g, S, "psem", side_sb, flags_sb, poll_sb, FLC, fl0, fl1, pre=pre)
            for chn in range(NCH):
                if chn >= 2:
                    g.wait_ge(S["vev"], (chn - 1) * 8)
                if layer == 1:
                    g.dma_start(out=biasb[chn % 2][:, :],
                                in_=bass.AP(brow, chn * CW, [[0, 128], [1, CW]])).then_inc(S[f"vwload{chn % 2}"], 16)
                for part in range(4):
                    srcw = bass.AP(wT, (part * 8 * 128) * D + chn * CW,
                                   [[D, 128], [128 * D, 8], [1, CW]])
                    g.dma_start(out=wvbuf[chn % 2][:, part * 8 * CW:(part + 1) * 8 * CW],
                                in_=srcw).then_inc(S[f"vwload{chn % 2}"], 16)

        @blk.vector
        def _(v):
            ch = Ch(v, S["vch"])
            ch.wait(S["cload"], 16)
            xn_n = 0
            tpc_n = 0
            for i in range(NT):
                c = 8 * i
                ch.wait(S[f"xload{i % 2}"], 16 * (i // 2 + 1))
                for ck in range(8):
                    ch.op(v.bn_stats, bnbuf[:, ck * 6:(ck + 1) * 6],
                          x_t[i % 2][:, ck * 512:(ck + 1) * 512])
                ch.op(v.bn_aggr, stat[:, c:c + 2], bnbuf[:, :])
                ch.op_m(S["var"], i + 1, v.tensor_scalar, stat[:, c + 2:c + 3],
                        stat[:, c + 1:c + 2], LN_EPS, None, ALU.add)
                ch.wait(S["sqv"], i + 1)
                ch.op(v.reciprocal, stat[:, c + 7:c + 8], stat[:, c + 6:c + 7])
                if i >= 1:
                    ch.wait(S["tp"], 4 * i)
                xn_n += 1
                ch.op_m(S["xn"], xn_n, v.tensor_scalar, xn_tok[:, :], x_t[i % 2][:, :],
                        stat[:, c:c + 1], stat[:, c + 7:c + 8], ALU.subtract, ALU.mult)
                for gidx in range(4):
                    gi = i * 4 + gidx
                    ch.wait(S["tp"], gi + 1)
                    tpc_n += 1
                    ch.op_m(S["tpc"], tpc_n, v.tensor_copy,
                            xnT_r[:, gidx * 8:(gidx + 1) * 8, i * 128:(i + 1) * 128],
                            ps_ln[gi % 2][:, :])
            vev_n = 0
            for chn in range(NCH):
                for w in range(2):
                    wv_idx = chn * 2 + w
                    ch.wait(S["vmm"], wv_idx + 1)
                    for t4 in range(4):
                        tt = w * 4 + t4
                        e = chn * 8 + tt
                        ch.wait(S[f"xld{e % 2}"], 16 * (e // 2 + 1))
                        if e >= 2:
                            ch.wait(S[f"xst{e % 2}"], 16 * (e // 2))
                        vev_n += 1
                        if layer == 0:
                            ch.op_m(S["vev"], vev_n, v.tensor_tensor, xos[e % 2][:, :],
                                    ps_y[t4][:, 0:CW], xsl[e % 2][:, :], ALU.add)
                        else:
                            ch.op(v.tensor_tensor, tmpb[:, :], ps_y[t4][:, 0:CW],
                                  biasb[chn % 2][:, :], ALU.add)
                            ch.op_m(S["vev"], vev_n, v.tensor_tensor, xos[e % 2][:, :],
                                    tmpb[:, :], xsl[e % 2][:, :], ALU.add)

        @blk.scalar
        def _(a):
            ch = Ch(a, S["ach"])
            for i in range(NT):
                c = 8 * i
                ch.wait(S["var"], i + 1)
                ch.op_m(S["sqv"], i + 1, a.activation, stat[:, c + 6:c + 7],
                        stat[:, c + 2:c + 3], AF.Sqrt)
            if layer == 0 and not dbg_nolora:
                for tc in range(2):
                    ch.wait(S["zmm"], tc + 1)
                    ch.op_m(S["zev"], tc + 1, a.activation,
                            zsb[:, tc * 512:(tc + 1) * 512],
                            ps_z[:, tc * 512:(tc + 1) * 512], AF.Copy,
                            scale=float(FFN_SCALE))

        @blk.tensor
        def _(t):
            t.wait_ge(S["cload"], 16)
            if layer == 0:
                t.wait_ge(S["cload2"], 32)
            for i in range(NT):
                t.wait_ge(S["xn"], i + 1)
                for gidx in range(4):
                    gi = i * 4 + gidx
                    if gi >= 2:
                        t.wait_ge(S["tpc"], gi - 1)
                    for j in range(8):
                        ins = t.transpose(ps_ln[gi % 2][:, j * 128:(j + 1) * 128],
                                          xn_tok[:, (gidx * 8 + j) * 128:(gidx * 8 + j + 1) * 128],
                                          idn_sb[:, :])
                        if j == 7:
                            ins.then_inc(S["tp"], 1)
            t.wait_ge(S["tpc"], 32)
            if layer == 0 and not dbg_nolora:
                for tc in range(2):
                    for in_ in range(ND):
                        ins = t.matmul(ps_z[:, tc * 512:(tc + 1) * 512],
                                       wa_sb[:, in_ * 16:(in_ + 1) * 16],
                                       xnT[:, in_ * TOK + tc * 512:in_ * TOK + (tc + 1) * 512],
                                       start=(in_ == 0), stop=(in_ == ND - 1))
                        if in_ == ND - 1:
                            ins.then_inc(S["zmm"], 1)
            for chn in range(NCH):
                t.wait_ge(S[f"vwload{chn % 2}"], (80 if layer == 1 else 64) * (chn // 2 + 1))
                for w in range(2):
                    wv_idx = chn * 2 + w
                    if wv_idx >= 1:
                        t.wait_ge(S["vev"], wv_idx * 4)
                    last_in = (layer == 1) or dbg_nolora
                    for in_ in range(ND):
                        for t4 in range(4):
                            tt = w * 4 + t4
                            ins = t.matmul(ps_y[t4][:, 0:CW],
                                           xnT[:, in_ * TOK + tt * 128:in_ * TOK + (tt + 1) * 128],
                                           wvbuf[chn % 2][:, in_ * CW:(in_ + 1) * CW],
                                           start=(in_ == 0),
                                           stop=(last_in and in_ == ND - 1))
                            if layer == 1 and in_ == ND - 1 and t4 == 3:
                                ins.then_inc(S["vmm"], 1)
                    if layer == 0 and not dbg_nolora:
                        if chn == 0 and w == 0:
                            t.wait_ge(S["zev"], 2)
                        for t4 in range(4):
                            tt = w * 4 + t4
                            ins = t.matmul(ps_y[t4][:, 0:CW],
                                           zsb[:, tt * 128:(tt + 1) * 128],
                                           wb_sb[:, chn * CW:(chn + 1) * CW],
                                           start=False, stop=True)
                            if t4 == 3:
                                ins.then_inc(S["vmm"], 1)




def build_l1qkv():
    nc = bass.Bass("TRN2", target_bir_lowering=False)
    KVD = L1_KV * L1_HD
    NKT = KVD // 128
    T = {}
    T["x"] = nc.dram_tensor("x", [TOK, D], F32, kind="ExternalInput")
    T["wq1"] = nc.dram_tensor("wq1", [D, D], F32, kind="ExternalInput")
    T["wk1"] = nc.dram_tensor("wk1", [D, KVD], F32, kind="ExternalInput")
    T["wv1"] = nc.dram_tensor("wv1", [D, KVD], F32, kind="ExternalInput")
    T["bq1"] = nc.dram_tensor("bq1", [128, ND], F32, kind="ExternalInput")
    T["bk1"] = nc.dram_tensor("bk1", [128, NKT], F32, kind="ExternalInput")
    T["bv1"] = nc.dram_tensor("bv1", [1, KVD], F32, kind="ExternalInput")
    T["cosx"] = nc.dram_tensor("cosx", [64, TOK], F32, kind="ExternalInput")
    T["sinx"] = nc.dram_tensor("sinx", [64, TOK], F32, kind="ExternalInput")
    T["idn"] = nc.dram_tensor("idn", [128, 128], BF16, kind="ExternalInput")
    T["qT1"] = nc.dram_tensor("qT1", [D, TOK], BF16, kind="ExternalOutput")
    T["kT1"] = nc.dram_tensor("kT1", [KVD, TOK], BF16, kind="ExternalOutput")
    T["v1"] = nc.dram_tensor("v1", [TOK, KVD], BF16, kind="ExternalOutput")
    with ExitStack() as sst:
        emit_l1qkv(nc, _semf(nc, sst, "E"), T, fused=False)
    return nc


def emit_l1qkv(nc, sf, T, fused, pre=""):
    KVD = L1_KV * L1_HD
    NKT = KVD // 128
    x_i = T["x"]
    wq1, wk1, wv1 = T["wq1"], T["wk1"], T["wv1"]
    bq1, bk1, bv1 = T["bq1"], T["bk1"], T["bv1"]
    cosx, sinx, idn = T["cosx"], T["sinx"], T["idn"]
    qT_o = T["qT1"]
    if fused:
        k0, k1, v0, v1 = T["k10"], T["k11"], T["v10"], T["v11"]
        side_sb, flags_sb = T["side_sb"], T["flags_sb"]
        fl0, fl1, FLC = T["flE0"], T["flE1"], T["flcE"]
    else:
        kT_o, v_o = T["kT1"], T["v1"]
    NW = ND + NKT
    NCH = KVD // 512

    with ExitStack() as st:
        sb = lambda nm, sh, dt: st.enter_context(nc.sbuf_tensor(pre + nm, sh, dt))
        ps = lambda nm, sh, dt: st.enter_context(nc.psum_tensor(pre + nm, sh, dt))
        sem = sf
        xnT = sb("xnT", [128, ND * TOK], BF16)
        xnT_r = xnT.ap().rearrange("p (f t) -> p f t", t=TOK)
        x_t = [sb(f"x_t{i}", [128, D], F32) for i in range(2)]
        stat = sb("stat", [128, 8 * NT], F32)
        bnbuf = sb("bnbuf", [128, 48], F32)
        xn_tok = sb("xn_tok", [128, D], BF16)
        idn_sb = sb("idn_sb", [128, 128], BF16)
        bias_sb = sb("bias_sb", [128, NW], F32)
        cs_sb = sb("cs_sb", [128, 2 * TOK], F32)
        rtmp = sb("rtmp", [128, TOK], F32)
        swp = [sb(f"swp{i}", [128, TOK], BF16) for i in range(2)]
        wvbuf = [sb(f"wvbuf{i}", [128, ND * 512], BF16) for i in range(2)]
        wbuf = wvbuf
        stq = [sb(f"stq{i}", [128, TOK], BF16) for i in range(2)]
        stq2 = [sb(f"stq2{i}", [128, TOK], BF16) for i in range(2)]
        bvb = [sb(f"bvb{i}", [128, 512], BF16) for i in range(2)]
        vstage = [sb(f"vstage{i}", [128, 512], BF16) for i in range(8)]
        ps_ln = [ps(f"ps_ln{i}", [128, 1024], BF16) for i in range(2)]
        ps_a = ps("ps_a", [128, 512], F32)
        ps_b = ps("ps_b", [128, 512], F32)
        ps_v = [ps(f"ps_v{i}", [128, 512], F32) for i in range(4)]
        blk = st.enter_context(nc.Block())
        S = {n: sem(n) for n in
             ["cload", "vch", "ach", "xload0", "xload1", "var", "sqv", "xn",
              "tp", "tpc", "wload0", "wload1", "mm", "ev", "rope", "sw0", "sw1",
              "stq_s0", "stq_s1", "vwload0", "vwload1", "vmm", "vev", "flst"]
             + [f"vst{i}" for i in range(8)]}

        @blk.sync
        def _(s):
            s.dma_start(out=idn_sb[:, :], in_=idn[:, :]).then_inc(S["cload"], 16)
            s.dma_start(out=bias_sb[:, 0:ND], in_=bq1[:, :]).then_inc(S["cload"], 16)
            s.dma_start(out=bias_sb[:, ND:NW], in_=bk1[:, :]).then_inc(S["cload"], 16)
            s.dma_start(out=cs_sb[0:64, 0:TOK], in_=cosx[:, :]).then_inc(S["cload"], 16)
            s.dma_start(out=cs_sb[64:128, 0:TOK], in_=cosx[:, :]).then_inc(S["cload"], 16)
            s.dma_start(out=cs_sb[0:64, TOK:2 * TOK], in_=sinx[:, :]).then_inc(S["cload"], 16)
            s.dma_start(out=cs_sb[64:128, TOK:2 * TOK], in_=sinx[:, :]).then_inc(S["cload"], 16)
            for i in range(NT):
                if i >= 2:
                    s.wait_ge(S["xn"], i - 1)
                s.dma_start(out=x_t[i % 2][:, :], in_=x_i[i * 128:(i + 1) * 128, :]).then_inc(S[f"xload{i % 2}"], 16)
            sd_ctx = s.register(pre + "sd") if fused else None
            sd = sd_ctx.__enter__() if fused else None
            if fused:
                s.reg_load(sd, side_sb[0:1, 0:1])

            def qk_store(ot):
                s.wait_ge(S["rope"], ot + 1)
                if ot < ND:
                    s.dma_start(out=qT_o[ot * 128:(ot + 1) * 128, :],
                                in_=stq2[ot % 2][:, :]).then_inc(S[f"stq_s{ot % 2}"], 16)
                elif not fused:
                    s.dma_start(out=kT_o[(ot - ND) * 128:(ot - ND + 1) * 128, :],
                                in_=stq2[ot % 2][:, :]).then_inc(S[f"stq_s{ot % 2}"], 16)
                else:
                    kt = ot - ND
                    with s.If_eq(sd, 0):
                        s.dma_start(out=k0[kt * 128:(kt + 1) * 128, :],
                                    in_=stq2[ot % 2][:, :]).then_inc(S[f"stq_s{ot % 2}"], 16)
                    with s.Else():
                        s.dma_start(out=k1[kt * 128:(kt + 1) * 128, :],
                                    in_=stq2[ot % 2][:, :]).then_inc(S[f"stq_s{ot % 2}"], 16)
            for ot in range(NW):
                s.wait_ge(S["ev"], 2 * ot + 2)
                if ot >= 2:
                    s.wait_ge(S["rope"], ot - 1)
                s.dma_start(out=swp[ot % 2][0:64, :], in_=stq[ot % 2][64:128, :]).then_inc(S[f"sw{ot % 2}"], 16)
                s.dma_start(out=swp[ot % 2][64:128, :], in_=stq[ot % 2][0:64, :]).then_inc(S[f"sw{ot % 2}"], 16)
                if ot >= 1:
                    qk_store(ot - 1)
            qk_store(NW - 1)
            for chn in range(NCH):
                for tt in range(8):
                    s.wait_ge(S["vev"], chn * 8 + tt + 1)
                    if fused:
                        with s.If_eq(sd, 0):
                            s.dma_start(out=v0[tt * 128:(tt + 1) * 128, chn * 512:(chn + 1) * 512],
                                        in_=vstage[tt][:, :]).then_inc(S[f"vst{tt}"], 16)
                        with s.Else():
                            s.dma_start(out=v1[tt * 128:(tt + 1) * 128, chn * 512:(chn + 1) * 512],
                                        in_=vstage[tt][:, :]).then_inc(S[f"vst{tt}"], 16)
                    else:
                        s.dma_start(out=v_o[tt * 128:(tt + 1) * 128, chn * 512:(chn + 1) * 512],
                                    in_=vstage[tt][:, :]).then_inc(S[f"vst{tt}"], 16)
            for tt in range(8):
                s.wait_ge(S[f"vst{tt}"], 16 * NCH)
            for p in range(2):
                s.wait_ge(S[f"stq_s{p}"], 16 * (NW // 2))
            if fused:
                with s.If_eq(sd, 0):
                    s.dma_start(out=fl0[:, :], in_=flags_sb[0:1, FLC * 16:(FLC + 1) * 16]).then_inc(S["flst"], 16)
                with s.Else():
                    s.dma_start(out=fl1[:, :], in_=flags_sb[0:1, FLC * 16:(FLC + 1) * 16]).then_inc(S["flst"], 16)
                s.wait_ge(S["flst"], 16)
                sd_ctx.__exit__(None, None, None)

        @blk.gpsimd
        def _(g):
            for widx in range(NW):
                if widx >= 2:
                    g.wait_ge(S["ev"], 2 * widx - 2)
                wT = wq1 if widx < ND else wk1
                wout = D if widx < ND else KVD
                ot = widx if widx < ND else widx - ND
                for part in range(4):
                    srcw = bass.AP(wT, (part * 8 * 128) * wout + ot * 128,
                                   [[wout, 128], [128 * wout, 8], [1, 128]])
                    g.dma_start(out=wbuf[widx % 2][:, part * 1024:(part + 1) * 1024],
                                in_=srcw).then_inc(S[f"wload{widx % 2}"], 16)
            for chn in range(NCH):
                if chn < 2:
                    g.wait_ge(S["mm"], 2 * NW)
                if chn >= 2:
                    g.wait_ge(S["vev"], (chn - 1) * 8)
                g.dma_start(out=bvb[chn % 2][:, :],
                            in_=bass.AP(bv1, chn * 512, [[0, 128], [1, 512]])).then_inc(S[f"vwload{chn % 2}"], 16)
                for part in range(4):
                    srcw = bass.AP(wv1, (part * 8 * 128) * KVD + chn * 512,
                                   [[KVD, 128], [128 * KVD, 8], [1, 512]])
                    g.dma_start(out=wvbuf[chn % 2][:, part * 8 * 512:(part + 1) * 8 * 512],
                                in_=srcw).then_inc(S[f"vwload{chn % 2}"], 16)

        @blk.vector
        def _(v):
            ch = Ch(v, S["vch"])
            ch.wait(S["cload"], 16 * 7)
            xn_n = 0
            tpc_n = 0
            for i in range(NT):
                c = 8 * i
                ch.wait(S[f"xload{i % 2}"], 16 * (i // 2 + 1))
                for ck in range(8):
                    ch.op(v.bn_stats, bnbuf[:, ck * 6:(ck + 1) * 6],
                          x_t[i % 2][:, ck * 512:(ck + 1) * 512])
                ch.op(v.bn_aggr, stat[:, c:c + 2], bnbuf[:, :])
                ch.op_m(S["var"], i + 1, v.tensor_scalar, stat[:, c + 2:c + 3],
                        stat[:, c + 1:c + 2], LN_EPS, None, ALU.add)
                ch.wait(S["sqv"], i + 1)
                ch.op(v.reciprocal, stat[:, c + 7:c + 8], stat[:, c + 6:c + 7])
                if i >= 1:
                    ch.wait(S["tp"], 4 * i)
                xn_n += 1
                ch.op_m(S["xn"], xn_n, v.tensor_scalar, xn_tok[:, :], x_t[i % 2][:, :],
                        stat[:, c:c + 1], stat[:, c + 7:c + 8], ALU.subtract, ALU.mult)
                for gidx in range(4):
                    gi = i * 4 + gidx
                    ch.wait(S["tp"], gi + 1)
                    tpc_n += 1
                    ch.op_m(S["tpc"], tpc_n, v.tensor_copy,
                            xnT_r[:, gidx * 8:(gidx + 1) * 8, i * 128:(i + 1) * 128],
                            ps_ln[gi % 2][:, :])
            for ot in range(NW):
                ch.wait(S["ev"], 2 * ot + 2)
                ch.wait(S[f"sw{ot % 2}"], 32 * (ot // 2 + 1))
                if ot >= 2:
                    ch.wait(S[f"stq_s{ot % 2}"], 16 * (ot // 2))
                sg, sg2, sw = stq[ot % 2], stq2[ot % 2], swp[ot % 2]
                cosb, sinb = cs_sb[:, 0:TOK], cs_sb[:, TOK:2 * TOK]
                ch.op(v.tensor_tensor, rtmp[0:64, :], sg[0:64, :], cosb[0:64, :], ALU.mult)
                ch.op(v.tensor_tensor, sg2[0:64, :], sw[0:64, :], sinb[0:64, :], ALU.mult)
                ch.op(v.tensor_tensor, sg2[0:64, :], rtmp[0:64, :], sg2[0:64, :], ALU.subtract)
                ch.op(v.tensor_tensor, rtmp[64:128, :], sw[64:128, :], sinb[64:128, :], ALU.mult)
                ch.op(v.tensor_tensor, sg2[64:128, :], sg[64:128, :], cosb[64:128, :], ALU.mult)
                rp = ot + 1
                ch.op_m(S["rope"], rp, v.tensor_tensor, sg2[64:128, :],
                        rtmp[64:128, :], sg2[64:128, :], ALU.add)
            vev_n = 0
            for chn in range(NCH):
                for w in range(2):
                    wv_idx = chn * 2 + w
                    ch.wait(S["vmm"], wv_idx + 1)
                    for t4 in range(4):
                        tt = w * 4 + t4
                        if chn >= 1:
                            ch.wait(S[f"vst{tt}"], 16 * chn)
                        vev_n += 1
                        ch.op_m(S["vev"], vev_n, v.tensor_tensor, vstage[tt][:, :],
                                ps_v[t4][:, 0:512], bvb[chn % 2][:, :], ALU.add)

        @blk.scalar
        def _(a):
            ch = Ch(a, S["ach"])
            for i in range(NT):
                c = 8 * i
                ch.wait(S["var"], i + 1)
                ch.op_m(S["sqv"], i + 1, a.activation, stat[:, c + 6:c + 7],
                        stat[:, c + 2:c + 3], AF.Sqrt)
            ev_n = 0
            for ot in range(NW):
                for tc in range(2):
                    gidx = ot * 2 + tc
                    ch.wait(S["mm"], gidx + 1)
                    if ot >= 2 and tc == 0:
                        ch.wait(S["rope"], ot - 1)
                    ev_n += 1
                    ch.op_m(S["ev"], ev_n, a.activation,
                            stq[ot % 2][:, tc * 512:(tc + 1) * 512],
                            (ps_a if gidx % 2 == 0 else ps_b)[:, :],
                            AF.Identity, bias=bias_sb[:, ot:ot + 1])

        @blk.tensor
        def _(t):
            t.wait_ge(S["cload"], 16 * 7)
            for i in range(NT):
                t.wait_ge(S["xn"], i + 1)
                for gidx in range(4):
                    gi = i * 4 + gidx
                    if gi >= 2:
                        t.wait_ge(S["tpc"], gi - 1)
                    for j in range(8):
                        ins = t.transpose(ps_ln[gi % 2][:, j * 128:(j + 1) * 128],
                                          xn_tok[:, (gidx * 8 + j) * 128:(gidx * 8 + j + 1) * 128],
                                          idn_sb[:, :])
                        if j == 7:
                            ins.then_inc(S["tp"], 1)
            t.wait_ge(S["tpc"], 32)
            for widx in range(NW):
                t.wait_ge(S[f"wload{widx % 2}"], 64 * (widx // 2 + 1))
                for tc in range(2):
                    gidx = widx * 2 + tc
                    if gidx >= 2:
                        t.wait_ge(S["ev"], gidx - 1)
                    for in_ in range(ND):
                        ins = t.matmul((ps_a if gidx % 2 == 0 else ps_b)[:, :],
                                       wbuf[widx % 2][:, in_ * 128:(in_ + 1) * 128],
                                       xnT[:, in_ * TOK + tc * 512:in_ * TOK + (tc + 1) * 512],
                                       start=(in_ == 0), stop=(in_ == ND - 1))
                        if in_ == ND - 1:
                            ins.then_inc(S["mm"], 1)
            t.wait_ge(S["ev"], 2 * NW)
            for chn in range(NCH):
                t.wait_ge(S[f"vwload{chn % 2}"], 80 * (chn // 2 + 1))
                for w in range(2):
                    wv_idx = chn * 2 + w
                    if wv_idx >= 1:
                        t.wait_ge(S["vev"], wv_idx * 4)
                    for in_ in range(ND):
                        for t4 in range(4):
                            tt = w * 4 + t4
                            ins = t.matmul(ps_v[t4][:, 0:512],
                                           xnT[:, in_ * TOK + tt * 128:in_ * TOK + (tt + 1) * 128],
                                           wvbuf[chn % 2][:, in_ * 512:(in_ + 1) * 512],
                                           start=(in_ == 0), stop=(in_ == ND - 1))
                            if in_ == ND - 1 and t4 == 3:
                                ins.then_inc(S["vmm"], 1)


def _prep_host(inputs):
    d = {}
    x = np.asarray(inputs["L0_attn_x"], np.float32).reshape(NCORES * TOK, D)
    d["x_sh"] = [np.ascontiguousarray(x[c * TOK:(c + 1) * TOK]) for c in range(NCORES)]
    d["wqT"] = np.ascontiguousarray(np.asarray(inputs["L0_attn_Wq"], np.float32).T)
    d["wkT"] = np.ascontiguousarray(np.asarray(inputs["L0_attn_Wk"], np.float32).T)
    d["wvT"] = np.ascontiguousarray(np.asarray(inputs["L0_attn_Wv"], np.float32).T)
    d["bq"] = np.ascontiguousarray(np.asarray(inputs["L0_attn_bq"], np.float32).reshape(ND, 128).T)
    d["bk"] = np.ascontiguousarray(np.asarray(inputs["L0_attn_bk"], np.float32).reshape(ND, 128).T)
    d["bv"] = np.ascontiguousarray(np.asarray(inputs["L0_attn_bv"], np.float32).reshape(1, D))
    d["bwf"] = np.ascontiguousarray(np.asarray(inputs["L0_attn_bias_weight"], np.float32).reshape(1, 48))
    d["kdkg"] = _geo_consts()
    d["idn"] = np.eye(128, dtype=BF)
    d["idnf"] = np.eye(128, dtype=np.float32)
    codesT = np.ascontiguousarray(np.asarray(inputs["L0_ffn_w_codes"], np.int32).T)
    d["codes_sh"] = [np.ascontiguousarray(codesT[c * 512:(c + 1) * 512]) for c in range(NCORES)]
    amxT = np.ascontiguousarray(np.asarray(inputs["L0_ffn_absmax"], np.float32).T)
    d["amx_sh"] = [np.ascontiguousarray(amxT[c * 8:(c + 1) * 8]) for c in range(NCORES)]
    wa = np.asarray(inputs["L0_ffn_W_a"], np.float32)
    d["waT"] = np.ascontiguousarray(wa.reshape(ND, 128, 16).transpose(1, 0, 2).reshape(128, 512))
    d["wbT"] = np.ascontiguousarray(np.asarray(inputs["L0_ffn_W_b"], np.float32))
    d["wq1"] = np.ascontiguousarray(np.asarray(inputs["L1_attn_wq"], np.float32))
    d["wk1"] = np.ascontiguousarray(np.asarray(inputs["L1_attn_wk"], np.float32))
    d["wv1"] = np.ascontiguousarray(np.asarray(inputs["L1_attn_wv"], np.float32))
    d["bq1"] = np.ascontiguousarray(np.asarray(inputs["L1_attn_bq"], np.float32).reshape(ND, 128).T)
    d["bk1"] = np.ascontiguousarray(np.asarray(inputs["L1_attn_bk"], np.float32).reshape(8, 128).T)
    d["bv1"] = np.ascontiguousarray(np.asarray(inputs["L1_attn_bv"], np.float32).reshape(1, 1024))
    cosT, sinT = _rope_tables()
    d["cos_sh"] = [np.ascontiguousarray(cosT[:, (c % 2) * TOK:(c % 2 + 1) * TOK]) for c in range(NCORES)]
    d["sin_sh"] = [np.ascontiguousarray(sinT[:, (c % 2) * TOK:(c % 2 + 1) * TOK]) for c in range(NCORES)]
    d["w1T"] = np.ascontiguousarray(np.asarray(inputs["L1_ffn_weight"], np.float16).T)
    d["b1row"] = np.ascontiguousarray(np.asarray(inputs["L1_ffn_bias"], np.float16).reshape(1, D))
    return d


def _pair_concat(arrs, ncores, axis):
    out = []
    for c in range(ncores):
        b = c // 2
        out.append(np.concatenate([arrs[2 * b], arrs[2 * b + 1]], axis=axis))
    return out


def _run_phases(d, runner, ncores=NCORES):
    idn = d["idn"]
    resA = runner(build_phaseA(), [
        {"x": d["x_sh"][c], "wqT": d["wqT"], "wkT": d["wkT"], "wvT": d["wvT"],
         "bq": d["bq"], "bk": d["bk"], "bv": d["bv"], "bwf": d["bwf"],
         "kdkg": d["kdkg"], "idn": idn} for c in range(ncores)])
    kT_f = _pair_concat([resA[c]["kT"] for c in range(ncores)], ncores, 1)
    v_f = _pair_concat([resA[c]["v"] for c in range(ncores)], ncores, 0)
    resB = runner(build_attn(0), [
        {"qT": resA[c]["qT"], "kT": kT_f[c], "v": v_f[c],
         "x": d["x_sh"][c], "idn": idn} for c in range(ncores)])
    resC = runner(build_dequant(), [
        {"codes": d["codes_sh"][c], "amx": d["amx_sh"][c]} for c in range(ncores)])
    wTfull = np.concatenate([resC[c]["wsh"] for c in range(ncores)], axis=0)
    if ncores < NCORES:
        pass
    resD = runner(build_ffn(0), [
        {"x": resB[c]["x1"], "wT": wTfull, "waT": d["waT"], "wbT": d["wbT"],
         "idn": idn} for c in range(ncores)])
    resE = runner(build_l1qkv(), [
        {"x": resD[c]["xo"], "wq1": d["wq1"], "wk1": d["wk1"], "wv1": d["wv1"],
         "bq1": d["bq1"], "bk1": d["bk1"], "bv1": d["bv1"],
         "cosx": d["cos_sh"][c], "sinx": d["sin_sh"][c], "idn": idn}
        for c in range(ncores)])
    kT1_f = _pair_concat([resE[c]["kT1"] for c in range(ncores)], ncores, 1)
    v1_f = _pair_concat([resE[c]["v1"] for c in range(ncores)], ncores, 0)
    resF = runner(build_attn(1), [
        {"qT": resE[c]["qT1"], "kT": kT1_f[c], "v": v1_f[c],
         "x": resD[c]["xo"], "idn": idn} for c in range(ncores)])
    resG = runner(build_ffn(1), [
        {"x": resF[c]["x1"], "wT": d["w1T"], "brow": d["b1row"], "idn": idn}
        for c in range(ncores)])
    return [resG[c]["xo"] for c in range(ncores)]


def _device_runner(nc, in_maps):
    res = run_bass_kernel_spmd(nc, in_maps, core_ids=list(range(len(in_maps))))
    return res.results




def build_fused(upto=7, detect_races=True):
    import os
    _mask = os.environ.get("FUSE_PHASES")
    _sel = set(int(x) for x in _mask.split(",")) if _mask else None
    def _on(i):
        return (i in _sel) if _sel is not None else True
    nc = bass.Bass("TRN2", target_bir_lowering=False,
                   detect_race_conditions=detect_races)
    KVD = L1_KV * L1_HD
    t = {}
    EI = lambda n, sh, dt: nc.dram_tensor(n, sh, dt, kind="ExternalInput")
    t["x"] = EI("x", [TOK, D], F32)
    t["wqT"] = EI("wqT", [D, D], F32)
    t["wkT"] = EI("wkT", [D, D], F32)
    t["wvT"] = EI("wvT", [D, D], F32)
    t["bq"] = EI("bq", [128, ND], F32)
    t["bk"] = EI("bk", [128, ND], F32)
    t["bv"] = EI("bv", [1, D], F32)
    t["bwf"] = EI("bwf", [1, 48], F32)
    t["kdkg"] = EI("kdkg", [512, 128], F32)
    t["idn"] = EI("idn", [128, 128], BF16)
    t["idnf"] = EI("idnf", [128, 128], F32)
    t["codes"] = EI("codes", [2048, D], I32)
    t["amx"] = EI("amx", [32, D], F32)
    t["waT"] = EI("waT", [128, 512], F32)
    t["wbT"] = EI("wbT", [16, D], F32)
    t["wq1"] = EI("wq1", [D, D], F32)
    t["wk1"] = EI("wk1", [D, KVD], F32)
    t["wv1"] = EI("wv1", [D, KVD], F32)
    t["bq1"] = EI("bq1", [128, ND], F32)
    t["bk1"] = EI("bk1", [128, 8], F32)
    t["bv1"] = EI("bv1", [1, KVD], F32)
    t["cosx"] = EI("cosx", [64, TOK], F32)
    t["sinx"] = EI("sinx", [64, TOK], F32)
    t["w1T"] = EI("w1T", [D, D], F16)
    t["b1row"] = EI("b1row", [1, D], F16)
    t["side"] = EI("side", [1, 1], I32)
    t["flags"] = EI("flags", [1, 64], I32)
    t["out"] = nc.dram_tensor("out", [TOK, D], F32, kind="ExternalOutput")
    t["qT"] = nc.dram_tensor("qT", [D, TOK], BF16)
    t["x1"] = nc.dram_tensor("x1", [TOK, D], F32)
    t["x2"] = nc.dram_tensor("x2", [TOK, D], F32)
    t["x3"] = nc.dram_tensor("x3", [TOK, D], F32)
    t["qT1"] = nc.dram_tensor("qT1", [D, TOK], BF16)
    SH = lambda n, sh, dt: nc.dram_tensor(n, sh, dt, addr_space="Shared")
    t["k0"] = SH("k0", [D, TOK], BF16)
    t["k1"] = SH("k1", [D, TOK], BF16)
    t["v0"] = SH("v0", [TOK, D], BF16)
    t["v1"] = SH("v1", [TOK, D], BF16)
    t["wT_sh"] = SH("wT_sh", [D, D], BF16)
    t["k10"] = SH("k10", [KVD, TOK], BF16)
    t["k11"] = SH("k11", [KVD, TOK], BF16)
    t["v10"] = SH("v10", [TOK, KVD], BF16)
    t["v11"] = SH("v11", [TOK, KVD], BF16)
    for n in ["flA0", "flA1", "flW0", "flW1", "flE0", "flE1"]:
        t[n] = SH(n, [1, 16], I32)

    with ExitStack() as sst:
        side_sb = sst.enter_context(nc.sbuf_tensor("side_sb", [1, 1], I32))
        flags_sb = sst.enter_context(nc.sbuf_tensor("flags_sb", [1, 64], I32))
        poll_sb = sst.enter_context(nc.sbuf_tensor("poll_sb", [1, 16], I32))
        isem = sst.enter_context(nc.semaphore("init_sem"))
        pool = {}
        uniq = bool(os.environ.get("FUSE_DBG_UNIQSEM"))
        phase_tag = [""]
        SWDGE_NAMES = {"wload0", "wload1", "vwload0", "vwload1", "klo0", "klo1",
                       "psem", "cload2", "qlo0", "qlo1", "dcl0", "dcl1",
                       "dst0", "dst1", "dflst", "stq_s0", "stq_s1"}

        def sfp(name):
            key = (phase_tag[0] + name) if (uniq or name in SWDGE_NAMES) else name
            if key not in pool:
                pool[key] = sst.enter_context(nc.semaphore("P_" + key))
            return pool[key]

        def clear_sems():
            if uniq:
                return
            with nc.Block() as bc:
                @bc.sync
                def _(s):
                    for k, h in pool.items():
                        if not any(k.endswith(n) and len(k) == len(n) + 1 or k == n
                                   for n in SWDGE_NAMES):
                            s.sem_clear(h)
        with nc.Block() as b0:
            @b0.sync
            def _(s):
                s.dma_start(out=side_sb[:, :], in_=t["side"][:, :]).then_inc(isem, 16)
                s.dma_start(out=flags_sb[:, :], in_=t["flags"][:, :]).then_inc(isem, 16)
                s.wait_ge(isem, 32)
        com = {"side_sb": side_sb, "flags_sb": flags_sb, "poll_sb": poll_sb}
        TA = {**com, "x": t["x"], "wqT": t["wqT"], "wkT": t["wkT"], "wvT": t["wvT"],
              "bq": t["bq"], "bk": t["bk"], "bv": t["bv"], "bwf": t["bwf"],
              "kdkg": t["kdkg"], "idn": t["idn"], "qT": t["qT"],
              "k0": t["k0"], "k1": t["k1"], "v0": t["v0"], "v1": t["v1"],
              "flA0": t["flA0"], "flA1": t["flA1"], "flcA": 0}
        TCA = {"codes": t["codes"], "amx": t["amx"], "wT_sh": t["wT_sh"]}
        if upto >= 1 and _on(1):
            phase_tag[0] = "A"
            emit_phaseA(nc, sfp, TA, fused=True, pre="A_", deq=TCA)
        TB = {**com, "qT": t["qT"], "x": t["x"], "idn": t["idn"], "idnf": t["idnf"], "x1": t["x1"],
              "k0": t["k0"], "k1": t["k1"], "v0": t["v0"], "v1": t["v1"],
              "fl0": t["flA0"], "fl1": t["flA1"], "flc": 0}
        TC = {"codes": t["codes"], "amx": t["amx"], "wT_sh": t["wT_sh"],
              "flW0": t["flW0"], "flW1": t["flW1"], "flcW": 1}
        if upto >= 2 and _on(2):
            phase_tag[0] = "B"
            clear_sems()
            emit_attn(nc, sfp, TB, 0, fused=True, pre="B_", deq={**TC, "t0": 8})
        import os
        if os.environ.get("FUSE_DBG_LOCALW"):
            t["wtest"] = nc.dram_tensor("wtest", [D, D], BF16)
            _wD = t["wtest"]
        else:
            _wD = t["wT_sh"]
        TD = {**com, "x": t["x1"], "wT": _wD, "waT": t["waT"], "wbT": t["wbT"],
              "idn": t["idn"], "xo": t["x2"],
              "flW0": t["flW0"], "flW1": t["flW1"], "flcW": 1}
        if upto >= 4 and _on(4):
            phase_tag[0] = "D"
            clear_sems()
        if upto >= 4 and _on(4):
            emit_ffn(nc, sfp, TD, 0, fused=True, pre="D_")
        TE = {**com, "x": t["x2"], "wq1": t["wq1"], "wk1": t["wk1"], "wv1": t["wv1"],
              "bq1": t["bq1"], "bk1": t["bk1"], "bv1": t["bv1"],
              "cosx": t["cosx"], "sinx": t["sinx"], "idn": t["idn"],
              "qT1": t["qT1"], "k10": t["k10"], "k11": t["k11"],
              "v10": t["v10"], "v11": t["v11"],
              "flE0": t["flE0"], "flE1": t["flE1"], "flcE": 2}
        if upto >= 5 and _on(5):
            phase_tag[0] = "E"
            clear_sems()
        if upto >= 5 and _on(5):
            emit_l1qkv(nc, sfp, TE, fused=True, pre="E_")
        TF = {**com, "qT": t["qT1"], "x": t["x2"], "idn": t["idn"], "idnf": t["idnf"], "x1": t["x3"],
              "k0": t["k10"], "k1": t["k11"], "v0": t["v10"], "v1": t["v11"],
              "fl0": t["flE0"], "fl1": t["flE1"], "flc": 2}
        if upto >= 6 and _on(6):
            phase_tag[0] = "F"
            clear_sems()
        if upto >= 6 and _on(6):
            emit_attn(nc, sfp, TF, 1, fused=True, pre="F_")
        TG = {"x": t["x3"], "wT": t["w1T"], "brow": t["b1row"],
              "idn": t["idn"], "xo": t["out"]}
        if upto >= 7 and _on(7):
            phase_tag[0] = "G"
            clear_sems()
        if upto >= 7 and _on(7):
            emit_ffn(nc, sfp, TG, 1, fused=False, pre="G_")
    return nc


def _fused_in_maps(d):
    rng = np.random.default_rng()
    nonce = int(rng.integers(1, 2 ** 30))
    flags = np.zeros((1, 64), np.int32)
    for k in range(4):
        flags[0, k * 16:(k + 1) * 16] = nonce + k + 1
    maps = []
    for c in range(NCORES):
        s = c % 2
        maps.append({
            "x": d["x_sh"][c], "wqT": d["wqT"], "wkT": d["wkT"], "wvT": d["wvT"],
            "bq": d["bq"], "bk": d["bk"], "bv": d["bv"], "bwf": d["bwf"],
            "kdkg": d["kdkg"], "idn": d["idn"],
            "codes": d["codes_pair"][s], "amx": d["amx_pair"][s],
            "waT": d["waT"], "wbT": d["wbT"],
            "wq1": d["wq1"], "wk1": d["wk1"], "wv1": d["wv1"],
            "bq1": d["bq1"], "bk1": d["bk1"], "bv1": d["bv1"],
            "cosx": d["cos_sh"][c], "sinx": d["sin_sh"][c],
            "w1T": d["w1T"], "b1row": d["b1row"], "idnf": d["idnf"],
            "side": np.array([[s]], np.int32), "flags": flags,
        })
    return maps


def kernel(**inputs):
    d = _prep_host(inputs)
    codesT = np.concatenate(d["codes_sh"], axis=0)
    amxT = np.concatenate(d["amx_sh"], axis=0)
    d["codes_pair"] = [np.ascontiguousarray(codesT[s * 2048:(s + 1) * 2048]) for s in range(2)]
    d["amx_pair"] = [np.ascontiguousarray(amxT[s * 32:(s + 1) * 32]) for s in range(2)]
    nc = build_fused()
    res = run_bass_kernel_spmd(nc, _fused_in_maps(d), core_ids=list(range(NCORES)))
    out = np.concatenate([np.asarray(res.results[c]["out"], np.float32) for c in range(NCORES)], axis=0)
    return out.reshape(4, SEQ, D)


def kernel_unfused(**inputs):
    d = _prep_host(inputs)
    shards = _run_phases(d, _device_runner)
    out = np.concatenate([np.asarray(s, np.float32) for s in shards], axis=0)
    return out.reshape(4, SEQ, D)

